# version 06
# baseline (speedup 1.0000x reference)
import contextlib
import numpy as np
import ml_dtypes
import concourse.bass as bass
import concourse.mybir as mybir
from concourse.bass_utils import run_bass_kernel_spmd

F32 = mybir.dt.float32
BF16 = mybir.dt.bfloat16
AF = mybir.ActivationFunctionType
ALU = mybir.AluOpType

P = 2054
T = 2048
HALF = 3
D = 1024
DK = 8
NH = 8
FD = 64
DFF = 4096
MF = 32
MAGIC = 12582912.0
EPS_LN = 1e-5
EPS_Q = 1e-5
EPS_Z = 1e-8
N_ELEM_W = float(DFF * D)
MASKW = 134
N_XT = 17
N_TC = 16

SLABS5 = [(0, 512), (512, 512), (1024, 512), (1536, 512), (2048, 6)]
KCS = [(k * 128, min(128, P - k * 128)) for k in range(17)]


def _bwin(kc):
    j0, w = KCS[kc]
    return max(HALF, j0 - HALF), min(P - HALF, j0 + w + HALF)


SLAB_PARTS = [[] for _ in range(4)]
SLAB_LAST_KC = [0] * 4
for _kc in range(17):
    _tb, _te = _bwin(_kc)
    for _u in range(4):
        _su, _eu = HALF + 512 * _u, HALF + 512 * (_u + 1)
        _lo, _hi = max(_tb, _su), min(_te, _eu)
        if _lo < _hi:
            SLAB_PARTS[_u].append((_kc, _lo - _tb, _hi - _tb, _lo - _su))
            SLAB_LAST_KC[_u] = max(SLAB_LAST_KC[_u], _kc)


def build_nc(cut=None):
    nc = bass.Bass("TRN2", target_bir_lowering=False)

    x_tm_d = nc.dram_tensor("x_tm", [P, D], F32, kind="ExternalInput").ap()
    wqkv_d = nc.dram_tensor("wqkv", [D, 3 * D], F32, kind="ExternalInput").ap()
    proj_d = nc.dram_tensor("proj", [128, FD], F32, kind="ExternalInput").ap()
    wout_d = nc.dram_tensor("wout", [D, D], F32, kind="ExternalInput").ap()
    bout_d = nc.dram_tensor("bout", [1, D], F32, kind="ExternalInput").ap()
    w1t_d = nc.dram_tensor("w1t", [D, DFF], F32, kind="ExternalInput").ap()
    b1t_d = nc.dram_tensor("b1t", [128, MF], F32, kind="ExternalInput").ap()
    w2t_d = nc.dram_tensor("w2t", [DFF, D], F32, kind="ExternalInput").ap()
    b2t_d = nc.dram_tensor("b2t", [128, DK], F32, kind="ExternalInput").ap()
    ident_d = nc.dram_tensor("ident", [128, 128], BF16, kind="ExternalInput").ap()
    masks_d = nc.dram_tensor("masks", [128, 17 * MASKW], BF16, kind="ExternalInput").ap()
    emask_d = nc.dram_tensor("emask", [64, 6], F32, kind="ExternalInput").ap()
    out_d = nc.dram_tensor("out", [T, D], F32, kind="ExternalOutput").ap()

    wq1_sc = nc.dram_tensor("wq1_sc", [16, 128, 8, 256], BF16, kind="Internal").ap()
    wq2_sc = nc.dram_tensor("wq2_sc", [8, 128, 32, 128], BF16, kind="Internal").ap()
    zdram = nc.dram_tensor("zdram", [NH, T], BF16, kind="Internal").ap()
    zdram_f = nc.dram_tensor("zdram_f", [NH, T], F32, kind="Internal").ap()

    steps = []
    PH = {}
    C = {"TE": 0, "VE": 0, "SE": 0}

    def bump(name, n=1):
        C[name] = C.get(name, 0) + n
        return C[name]

    es = contextlib.ExitStack()
    with es:
        te_s = es.enter_context(nc.semaphore("te"))
        ve_s = es.enter_context(nc.semaphore("ve"))
        se_s = es.enter_context(nc.semaphore("se"))
        SEM = {"TE": te_s, "VE": ve_s, "SE": se_s}
        sem_es = es

        def dch(name):
            if name not in SEM:
                SEM[name] = sem_es.enter_context(nc.semaphore(name))
                C[name] = 0
            return SEM[name]

        def dma(ch, out, in_, waits=(), q="sync"):
            hnd = dch(ch)
            w = [(s, v) for (s, v) in waits if v]
            val = bump(ch, 16)

            def f(e, out=out, in_=in_, w=w, hnd=hnd):
                for (s, v) in w:
                    e.wait_ge(SEM[s], v)
                e.dma_start(out=out, in_=in_).then_inc(hnd, 16)
            import traceback
            steps.append((q, f, traceback.extract_stack()[-2].lineno))
            return val

        def op(eng, sem_name, emitf, waits=(), inc=1):
            w = [(s, v) for (s, v) in waits if v]
            val = bump(sem_name, inc) if inc else C[sem_name]

            def f(e, emitf=emitf, w=w, inc=inc, sem_name=sem_name):
                for (s, v) in w:
                    e.wait_ge(SEM[s], v)
                ins = emitf(e)
                if inc:
                    ins.then_inc(SEM[sem_name], inc)
            import traceback
            steps.append((eng, f, traceback.extract_stack()[-2].lineno))
            return val

        ident = es.enter_context(nc.sbuf_tensor("sb_ident", [128, 128], BF16))
        emask = es.enter_context(nc.sbuf_tensor("sb_emask", [64, 6], F32))
        proj_f = es.enter_context(nc.sbuf_tensor("proj_f", [128, FD], F32))
        proj_b = es.enter_context(nc.sbuf_tensor("proj_b", [128, FD], BF16))
        b1t = es.enter_context(nc.sbuf_tensor("sb_b1t", [128, MF], F32))
        b2t = es.enter_context(nc.sbuf_tensor("sb_b2t", [128, DK], F32))
        bout_f = es.enter_context(nc.sbuf_tensor("bout_f", [1, D], F32))
        bout_b = es.enter_context(nc.sbuf_tensor("bout_b", [1, D], BF16))
        ones_rb = es.enter_context(nc.sbuf_tensor("ones_rb", [1, 128], BF16))
        ones_mb = es.enter_context(nc.sbuf_tensor("ones_mb", [128, 128], BF16))
        ones_cf = es.enter_context(nc.sbuf_tensor("ones_cf", [128, 1], F32))
        magic_c = es.enter_context(nc.sbuf_tensor("magic_c", [128, 1], F32))
        ones_rf = es.enter_context(nc.sbuf_tensor("ones_rf", [1, 128], F32))
        gam = es.enter_context(nc.sbuf_tensor("gam", [128, 32], F32))
        gc2 = es.enter_context(nc.sbuf_tensor("gc2", [128, 2], F32))
        grow = es.enter_context(nc.sbuf_tensor("grow", [1, 2], F32))
        gtmp = es.enter_context(nc.sbuf_tensor("gtmp", [128, 2], F32))
        invg = es.enter_context(nc.sbuf_tensor("invg", [128, 2], F32))

        xn = es.enter_context(nc.sbuf_tensor("xn", [128, DK, P], BF16))
        attn_r = es.enter_context(nc.sbuf_tensor("attn_r", [128, NH, T], BF16))

        v_c0 = 0
        for (dst, src) in [(ident[:], ident_d), (emask[:], emask_d),
                           (proj_f[:], proj_d), (b1t[:], b1t_d), (b2t[:], b2t_d),
                           (bout_f[:], bout_d)]:
            v_c0 = dma("cA", dst, src)
        op("vector", "VE", lambda e: e.memset(ones_rb[:], 1.0), inc=0)
        op("vector", "VE", lambda e: e.memset(ones_mb[:], 1.0), inc=0)
        op("vector", "VE", lambda e: e.memset(ones_cf[:], 1.0), inc=0)
        op("vector", "VE", lambda e: e.memset(magic_c[:], MAGIC), inc=0)
        op("vector", "VE", lambda e: e.memset(ones_rf[:], 1.0), inc=0)
        op("vector", "VE", lambda e: e.tensor_copy(proj_b[:], proj_f[:]),
           waits=[("cA", v_c0)], inc=0)
        v_consts = op("vector", "VE", lambda e: e.tensor_copy(bout_b[:], bout_f[:]),
                      waits=[("cA", v_c0)])

        def qpiece_in(i):
            if i < 16:
                kc, hh = i // 2, i % 2
                return w1t_d[128 * kc:128 * kc + 128, 2048 * hh:2048 * hh + 2048]
            j = i - 16
            return bass.AP(w2t_d.tensor, 256 * j * D, [[D, 128], [128 * D, 2], [1, D]])

        def qpiece_out(i, qo_t):
            src = qo_t[:]
            if i < 16:
                kc, hh = i // 2, i % 2
                o = bass.AP(wq1_sc.tensor, (8 * hh) * (128 * 8 * 256) + kc * 256,
                            [[8 * 256, 128], [128 * 8 * 256, 8], [1, 256]])
                return [(o, src.rearrange("p (mp c) -> p mp c", mp=8))]
            j = i - 16
            pairs = []
            for rr in range(2):
                o = bass.AP(wq2_sc.tensor, (2 * j + rr) * 128,
                            [[32 * 128, 128], [128 * 32 * 128, 8], [1, 128]])
                s = bass.AP(qo_t, rr * 1024, [[2048, 128], [128, 8], [1, 128]])
                pairs.append((o, s))
            return pairs

        def emit_invg(col, gcols, psum_t, wait_v):
            vr = op("vector", "VE", lambda e: e.tensor_reduce(
                gc2[:, col:col + 1], gcols, mybir.AxisListType.X, ALU.add),
                waits=[("VE", wait_v)])
            t1 = op("tensor", "TE", lambda e: e.matmul(
                psum_t[0:1, 0:1], ones_cf[:], gc2[:, col:col + 1],
                start=True, stop=True, skip_group_check=True),
                waits=[("VE", vr)])
            v_gr = op("vector", "VE", lambda e: e.tensor_copy(
                grow[0:1, col:col + 1], psum_t[0:1, 0:1]), waits=[("TE", t1)])
            t2 = op("tensor", "TE", lambda e: e.matmul(
                psum_t[0:128, 2:3], ones_rf[:], grow[0:1, col:col + 1],
                start=True, stop=True, skip_group_check=True),
                waits=[("VE", v_gr)])
            vg1 = op("vector", "VE", lambda e: e.tensor_scalar(
                gtmp[:, col:col + 1], psum_t[0:128, 2:3], 1.0 / N_ELEM_W, EPS_Q,
                ALU.mult, ALU.add), waits=[("TE", t2)])
            return op("vector", "VE", lambda e: e.reciprocal(
                invg[:, col:col + 1], gtmp[:, col:col + 1]), waits=[("VE", vg1)])

        with contextlib.ExitStack() as es_vh:
            v_tm = es_vh.enter_context(nc.sbuf_tensor("v_tm", [128, N_XT, D], BF16))
            qbuf = [es_vh.enter_context(nc.sbuf_tensor(f"qb{i}", [128, 2048], F32))
                    for i in range(3)]
            qo = [es_vh.enter_context(nc.sbuf_tensor(f"qo{i}", [128, 2048], BF16))
                  for i in range(2)]

            q_dma = {}
            q_cons = {}
            q_wr = {}
            v_gabs = {}
            nqnt = [0]

            def emit_gamma(seq, i, extra=()):
                b = seq % 3
                xw = [q_cons.get(seq - 3)] + list(extra)
                q_dma[seq] = dma(f"cQB{b}", qbuf[b][:], qpiece_in(i),
                                 waits=[x for x in xw if x], q="gpsimd")
                v_gabs[i] = op("vector", "VE", lambda e, b=b, i=i: e.tensor_reduce(
                    gam[:, i:i + 1], qbuf[b][:], mybir.AxisListType.X, ALU.add,
                    apply_absolute_value=True),
                    waits=[(f"cQB{b}", q_dma[seq])])
                q_cons[seq] = ("VE", v_gabs[i])

            def emit_quant(seq, i, v_invg, extra=()):
                b = seq % 3
                col = 0 if i < 16 else 1
                bo = nqnt[0] % 2
                qn = nqnt[0]
                nqnt[0] += 1
                xw = [q_cons.get(seq - 3)] + list(extra)
                q_dma[seq] = dma(f"cQB{b}", qbuf[b][:], qpiece_in(i),
                                 waits=[x for x in xw if x], q="gpsimd")
                p1 = op("scalar", "SE", lambda e, b=b, col=col: e.activation(
                    qbuf[b][:], qbuf[b][:], AF.Identity,
                    scale=invg[:, col:col + 1], bias=magic_c[:]),
                    waits=[(f"cQB{b}", q_dma[seq]), ("VE", v_invg)])
                p2 = op("vector", "VE", lambda e, b=b: e.tensor_scalar(
                    qbuf[b][:], qbuf[b][:], MAGIC, 1.0, ALU.subtract, ALU.min),
                    waits=[("SE", p1)])
                p3 = op("vector", "VE", lambda e, b=b, bo=bo: e.tensor_scalar(
                    qo[bo][:], qbuf[b][:], -1.0, None, ALU.max),
                    waits=[(f"cQW{bo}", q_wr.get(qn - 2)), ("VE", p2)])
                for (o, s) in qpiece_out(i, qo[bo]):
                    q_wr[qn] = dma(f"cQW{bo}", o, s, waits=[("VE", p3)], q="gpsimd")
                q_cons[seq] = ("VE", p3)

            tr_done = {}
            v_xn_evac = {}
            v_vevac = {}
            with contextlib.ExitStack() as es_lnv:
                ptr = [es_lnv.enter_context(nc.psum_tensor(f"ptr{i}", [128, D], BF16))
                       for i in range(2)]
                pv = [es_lnv.enter_context(nc.psum_tensor(f"pv{i}", [128, D], F32))
                      for i in range(2)]

                with contextlib.ExitStack() as es_ln:
                    xt = [es_ln.enter_context(nc.sbuf_tensor(f"xt{i}", [128, D], F32))
                          for i in range(4)]
                    junk = es_ln.enter_context(nc.sbuf_tensor("junk", [128, D], F32))
                    ssum = es_ln.enter_context(nc.sbuf_tensor("ssum", [128, N_XT], F32))
                    ssq = es_ln.enter_context(nc.sbuf_tensor("ssq", [128, N_XT], F32))
                    m2c = es_ln.enter_context(nc.sbuf_tensor("m2c", [128, N_XT], F32))
                    rstd = es_ln.enter_context(nc.sbuf_tensor("rstd", [128, N_XT], F32))
                    nmr = es_ln.enter_context(nc.sbuf_tensor("nmr", [128, N_XT], F32))
                    xnt = es_ln.enter_context(nc.sbuf_tensor("xnt", [128, 2, D], BF16))

                    op("vector", "VE", lambda e: e.memset(ssum[:], 0.0), inc=0)
                    v_ms = op("vector", "VE", lambda e: e.memset(ssq[:], 0.0))

                    d_x1 = {}
                    s_sq = {}
                    v_sm = {}
                    for tc in range(N_XT):
                        rows = 128 if tc < 16 else 6
                        b = tc % 4
                        d_x1[tc] = dma(f"cX{b}", xt[b][0:rows, :],
                                       x_tm_d[128 * tc:128 * tc + rows, :],
                                       waits=[("SE", s_sq.get(tc - 4)),
                                              ("VE", v_sm.get(tc - 4))])
                        s_sq[tc] = op("scalar", "SE",
                            lambda e, tc=tc, rows=rows, b=b: e.activation(
                                junk[0:rows, :], xt[b][0:rows, :], AF.Square,
                                accum_out=ssq[0:rows, tc:tc + 1]),
                            waits=[(f"cX{b}", d_x1[tc]),
                                   ("SE", s_sq.get(tc - 1)),
                                   ("VE", v_ms if tc == 0 else None)])
                        v_sm[tc] = op("vector", "VE",
                            lambda e, tc=tc, rows=rows, b=b: e.tensor_reduce(
                                ssum[0:rows, tc:tc + 1], xt[b][0:rows, :],
                                mybir.AxisListType.X, ALU.add),
                            waits=[(f"cX{b}", d_x1[tc]),
                                   ("VE", v_ms if tc == 0 else None)])
                    w1_ = op("vector", "VE", lambda e: e.tensor_scalar(
                        ssum[:], ssum[:], 1.0 / D, None, ALU.mult),
                        waits=[("SE", s_sq[N_XT - 1]), ("VE", v_sm[N_XT - 1])])
                    w2_ = op("vector", "VE", lambda e: e.tensor_tensor(
                        m2c[:], ssum[:], ssum[:], ALU.mult), waits=[("VE", w1_)])
                    w3_ = op("vector", "VE", lambda e: e.scalar_tensor_tensor(
                        ssq[:], ssq[:], 1.0 / D, m2c[:], ALU.mult, ALU.subtract),
                        waits=[("VE", w2_)])
                    w4_ = op("vector", "VE", lambda e: e.tensor_scalar(
                        ssq[:], ssq[:], EPS_LN, None, ALU.add), waits=[("VE", w3_)])
                    v_stats = op("vector", "VE", lambda e: e.reciprocal(
                        m2c[:], ssq[:]), waits=[("VE", w4_)])
                    s_rstd = op("scalar", "SE", lambda e: e.activation(
                        rstd[:], m2c[:], AF.Sqrt), waits=[("VE", v_stats)])
                    v_nmr = op("vector", "VE", lambda e: e.scalar_tensor_tensor(
                        nmr[:], ssum[:], -1.0, rstd[:], ALU.mult, ALU.mult),
                        waits=[("SE", s_rstd), ("VE", v_stats)])

                    d_x2 = {}
                    s_xn = {}
                    for tc in range(N_XT):
                        rows = 128 if tc < 16 else 6
                        b = tc % 4
                        buf = tc % 2
                        d_x2[tc] = dma(f"cX{b}", xt[b][0:rows, :],
                                       x_tm_d[128 * tc:128 * tc + rows, :],
                                       waits=[("SE", s_xn.get(tc - 4)),
                                              ("SE", s_sq[N_XT - 1]),
                                              ("VE", v_sm[N_XT - 1])])
                        s_w = [(f"cX{b}", d_x2[tc]), ("VE", v_nmr),
                               ("SE", s_xn.get(tc - 1))]
                        if tc >= 2:
                            s_w.append(("TE", tr_done[tc - 2]))
                        s_xn[tc] = op("scalar", "SE",
                            lambda e, tc=tc, rows=rows, b=b, buf=buf: e.activation(
                                xnt[0:rows, buf, :], xt[b][0:rows, :], AF.Identity,
                                scale=rstd[0:rows, tc:tc + 1],
                                bias=nmr[0:rows, tc:tc + 1]),
                            waits=s_w)
                        for j in range(DK):
                            w_list = []
                            if j == 0:
                                w_list = [("SE", s_xn[tc])]
                                if tc == 0:
                                    w_list.append(("cA", v_c0))
                                if tc >= 2:
                                    w_list.append(("VE", v_xn_evac[tc - 2]))
                            tr_done[tc] = op("tensor", "TE",
                                lambda e, rows=rows, buf=buf, j=j: e.transpose(
                                    ptr[buf][:, 128 * j:128 * j + rows],
                                    xnt[0:rows, buf, 128 * j:128 * (j + 1)],
                                    ident[0:rows, 0:rows]),
                                waits=w_list)
                        v_xn_evac[tc] = op("vector", "VE",
                            lambda e, tc=tc, rows=rows, buf=buf: e.tensor_copy(
                                xn[:, :, 128 * tc:128 * tc + rows],
                                ptr[buf][:].rearrange("p (j r) -> p j r", j=DK)[:, :, 0:rows]),
                            waits=[("TE", tr_done[tc])])
                        if tc < 16:
                            emit_gamma(tc, tc)
                PH["ln"] = len(steps)

                ln_end = dict(C)
                with contextlib.ExitStack() as es_v:
                    wvf = [es_v.enter_context(nc.sbuf_tensor(f"wvf{i}", [128, D], F32))
                           for i in range(2)]
                    wv_b = es_v.enter_context(nc.sbuf_tensor("wv_b", [128, DK, D], BF16))

                    v_wvc = {}
                    for kc in range(DK):
                        ch = f"cWV{kc % 2}"
                        xw = [("VE", v_wvc.get(kc - 2))]
                        if kc < 2:
                            xw += [("SE", ln_end["SE"]), ("TE", ln_end["TE"]),
                                   ("VE", ln_end["VE"])]
                        d_wv = dma(ch, wvf[kc % 2][:],
                                   wqkv_d[128 * kc:128 * kc + 128, 2048:3072],
                                   waits=xw)
                        v_wvc[kc] = op("vector", "VE", lambda e, kc=kc: e.tensor_copy(
                            wv_b[:, kc, :], wvf[kc % 2][:]), waits=[(ch, d_wv)])

                    for tc in range(N_XT):
                        rows = 128 if tc < 16 else 6
                        buf = tc % 2
                        t_last = None
                        for kc in range(DK):
                            for hh in range(2):
                                wl = []
                                if kc == 0 and hh == 0:
                                    wl = [("VE", v_xn_evac[tc]), ("VE", v_wvc[DK - 1])]
                                    if tc >= 2:
                                        wl.append(("VE", v_vevac[tc - 2]))
                                t_last = op("tensor", "TE",
                                    lambda e, tc=tc, rows=rows, buf=buf, kc=kc, hh=hh:
                                    e.matmul(
                                        pv[buf][0:rows, 512 * hh:512 * hh + 512],
                                        xn[:, kc, 128 * tc:128 * tc + rows],
                                        wv_b[:, kc, 512 * hh:512 * hh + 512],
                                        start=(kc == 0), stop=(kc == DK - 1),
                                        skip_group_check=True),
                                    waits=wl)
                        v_vevac[tc] = op("vector", "VE",
                            lambda e, tc=tc, rows=rows, buf=buf: e.tensor_copy(
                                v_tm[0:rows, tc, :], pv[buf][0:rows, :]),
                            waits=[("TE", t_last)])
            PH["v"] = len(steps)

            v_end = dict(C)
            v_aevac = {}
            v_zevac = {}
            with contextlib.ExitStack() as es_h:
                masks = es_h.enter_context(
                    nc.sbuf_tensor("sb_masks", [128, 17 * MASKW], BF16))
                whf = [es_h.enter_context(nc.sbuf_tensor(f"whf{i}", [128, D], F32))
                       for i in range(2)]
                whb = [es_h.enter_context(nc.sbuf_tensor(f"whb{i}", [128, D], BF16))
                       for i in range(2)]
                wT_sb = es_h.enter_context(nc.sbuf_tensor("wT_sb", [128, D], BF16))
                wqp_sb = [es_h.enter_context(
                    nc.sbuf_tensor(f"wqp{i}", [128, DK * FD], BF16)) for i in range(2)]
                e_scr = [es_h.enter_context(nc.sbuf_tensor(f"e_scr{i}", [64, 512], F32))
                         for i in range(2)]
                r_scr = [es_h.enter_context(nc.sbuf_tensor(f"r_scr{i}", [64, 512], F32))
                         for i in range(2)]
                q_phi = es_h.enter_context(nc.sbuf_tensor("q_phi", [64, P], BF16))
                k_phi = es_h.enter_context(nc.sbuf_tensor("k_phi", [64, P], BF16))
                s_bf = [es_h.enter_context(nc.sbuf_tensor(f"s_bf{i}", [128, MASKW], BF16))
                        for i in range(3)]
                z_all = es_h.enter_context(nc.sbuf_tensor("z_all", [NH, T], F32))
                z2 = es_h.enter_context(nc.sbuf_tensor("z2", [NH, T], F32))
                zinv_b = es_h.enter_context(nc.sbuf_tensor("zinv_b", [NH, T], BF16))
                z_stage = [es_h.enter_context(nc.sbuf_tensor(f"z_st{i}", [1, 512], F32))
                           for i in range(2)]
                pphi = [es_h.enter_context(nc.psum_tensor(f"pphi{i}", [128, 512], F32))
                        for i in range(2)]
                ptr_b = es_h.enter_context(nc.psum_tensor("ptr_b", [128, D], BF16))
                pwq = es_h.enter_context(nc.psum_tensor("pwq", [128, 512], F32))
                pA = [es_h.enter_context(nc.psum_tensor(f"pA{i}", [128, 512], F32))
                      for i in range(2)]
                pZ = [es_h.enter_context(nc.psum_tensor(f"pZ{i}", [128, 512], F32))
                      for i in range(2)]

                d_masks = dma("cM", masks[:], masks_d,
                              waits=[("TE", v_end["TE"]), ("VE", v_end["VE"])])

                v_invg1 = emit_invg(0, gam[:, 0:16], pA[0], v_gabs[15])

                d_zs = {}
                v_wcast = {}
                v_wT = {}
                v_wqp = {}
                v_phislab = {}
                s_exp = {}
                s_relu = {}
                v_sbf = {}
                t_sbf_free = {}
                hseq = [16]

                for h in range(NH):
                    pend = [(1, 2 * h), (1, 2 * h + 1), (0, 16 + 2 * h), (0, 17 + 2 * h)]

                    def emit_piece(h=h, pend=pend):
                        if not pend:
                            return
                        kind, i = pend.pop(0)
                        s = hseq[0]
                        hseq[0] += 1
                        extra = []
                        if s < 19:
                            extra = [("SE", ln_end["SE"]), ("TE", v_end["TE"]),
                                     ("VE", v_end["VE"])]
                        if kind == 1:
                            emit_quant(s, i, v_invg1, extra)
                        else:
                            emit_gamma(s, i, extra)

                    for side in range(2):
                        coff = side * D + 128 * h
                        ch = f"cWH{side}"
                        xw = [("VE", v_wcast.get((h - 1, side)))]
                        if h == 0:
                            xw += [("TE", v_end["TE"]), ("VE", v_end["VE"])]
                        d_w = dma(ch, whf[side][:].rearrange("p (k c) -> p k c", k=8),
                                  bass.AP(wqkv_d.tensor, coff,
                                          [[3 * D, 128], [128 * 3 * D, 8], [1, 128]]),
                                  waits=xw)
                        v_wcast[(h, side)] = op("vector", "VE",
                            lambda e, side=side: e.tensor_copy(
                                whb[side][:], whf[side][:]),
                            waits=[(ch, d_w)])

                    for side in range(2):
                        t_tr = None
                        for kcb in range(DK):
                            wl = []
                            if kcb == 0:
                                prev_wt = v_wT.get((h, side - 1)) if side == 1 else \
                                    v_wT.get((h - 1, 1))
                                wl = [("VE", v_wcast[(h, side)]), ("VE", prev_wt)]
                                if h == 0 and side == 0:
                                    wl.append(("VE", v_invg1))
                            t_tr = op("tensor", "TE",
                                lambda e, side=side, kcb=kcb: e.transpose(
                                    ptr_b[:, 128 * kcb:128 * (kcb + 1)],
                                    whb[side][:, 128 * kcb:128 * (kcb + 1)],
                                    ident[:]),
                                waits=wl)
                        v_wT[(h, side)] = op("vector", "VE",
                            lambda e: e.tensor_copy(wT_sb[:], ptr_b[:]),
                            waits=[("TE", t_tr)])
                        t_wq = None
                        for kcb in range(DK):
                            wl = []
                            if kcb == 0:
                                prev_wq = v_wqp.get((h, side - 1)) if side == 1 else \
                                    v_wqp.get((h - 1, 1))
                                wl = [("VE", v_wT[(h, side)]), ("VE", prev_wq),
                                      ("VE", v_sbf.get((h - 1, 14)))]
                            t_wq = op("tensor", "TE",
                                lambda e, kcb=kcb: e.matmul(
                                    pwq[:, FD * kcb:FD * (kcb + 1)],
                                    wT_sb[:, 128 * kcb:128 * (kcb + 1)], proj_b[:],
                                    start=True, stop=True, skip_group_check=True),
                                waits=wl)
                        v_wqp[(h, side)] = op("vector", "VE",
                            lambda e, side=side: e.tensor_copy(
                                wqp_sb[side][:], pwq[:]),
                            waits=[("TE", t_wq)])

                    for side, phi in enumerate([q_phi, k_phi]):
                        for si, (sb0, sw) in enumerate(SLABS5):
                            buf = si % 2
                            t_last = None
                            for kc in range(DK):
                                wl = []
                                if kc == 0:
                                    if si >= 2:
                                        pk = (h, side, si - 2)
                                    elif side == 1:
                                        pk = (h, 0, 4 - si)
                                    else:
                                        pk = (h - 1, 1, 4 - si)
                                    wl = [("VE", v_wqp[(h, side)]),
                                          ("SE", s_relu.get(pk)),
                                          ("VE", v_phislab.get(pk))]
                                    if side == 0 and si < 2:
                                        wl.append(("VE", v_sbf.get((h - 1, 15 + si))))
                                t_last = op("tensor", "TE",
                                    lambda e, buf=buf, kc=kc, sb0=sb0, sw=sw,
                                           side=side: e.matmul(
                                        pphi[buf][0:64, 0:sw],
                                        wqp_sb[side][:, FD * kc:FD * (kc + 1)],
                                        xn[:, kc, sb0:sb0 + sw],
                                        start=(kc == 0), stop=(kc == DK - 1),
                                        skip_group_check=True),
                                    waits=wl)
                            s_exp[(h, side, si)] = op("scalar", "SE",
                                lambda e, buf=buf, sw=sw: e.activation(
                                    e_scr[buf][:, 0:sw], pphi[buf][0:64, 0:sw], AF.Exp),
                                waits=[("TE", t_last)])
                            s_relu[(h, side, si)] = op("scalar", "SE",
                                lambda e, buf=buf, sw=sw: e.activation(
                                    r_scr[buf][:, 0:sw], pphi[buf][0:64, 0:sw], AF.Relu),
                                waits=[("SE", s_exp[(h, side, si)])])
                            v_phislab[(h, side, si)] = op("vector", "VE",
                                lambda e, buf=buf, sb0=sb0, sw=sw, phi=phi:
                                e.scalar_tensor_tensor(
                                    phi[:, sb0:sb0 + sw], e_scr[buf][:, 0:sw], 1.0,
                                    r_scr[buf][:, 0:sw], ALU.min, ALU.add),
                                waits=[("SE", s_relu[(h, side, si)]),
                                       ("VE", v_phislab.get((h, side, si - 2)))])
                            emit_piece()

                    op("vector", "VE", lambda e: e.tensor_tensor(
                        k_phi[:, 0:3], k_phi[:, 0:3], emask[:, 0:3], ALU.mult),
                        waits=[("VE", v_phislab[(h, 1, 4)])], inc=0)
                    v_kedge = op("vector", "VE", lambda e: e.tensor_tensor(
                        k_phi[:, 2051:2054], k_phi[:, 2051:2054], emask[:, 3:6],
                        ALU.mult))

                    pSp = [pphi[0], pphi[1], pwq]
                    first_a = [True] * 4
                    first_z = [True] * 4
                    watermark = [0] * 4
                    t_band = {}
                    for kc in range(17):
                        j0, w = KCS[kc]
                        tb, te = _bwin(kc)
                        tw = te - tb
                        buf = kc % 3
                        wl = [("VE", v_sbf.get((h, kc - 3)))]
                        if kc == 0:
                            wl = [("VE", v_kedge), ("cM", d_masks),
                                  ("SE", s_relu[(h, 1, 4)])]
                        t_s = op("tensor", "TE",
                            lambda e, buf=buf, j0=j0, w=w, tb=tb, tw=tw: e.matmul(
                                pSp[buf][0:w, 0:tw], k_phi[:, j0:j0 + w],
                                q_phi[:, tb:tb + tw],
                                start=True, stop=True, skip_group_check=True),
                            waits=wl)
                        sbf_prev = t_sbf_free.get((h, kc - 3)) if kc >= 3 else \
                            t_sbf_free.get((h - 1, 14 + kc))
                        v_sbf[(h, kc)] = op("vector", "VE",
                            lambda e, buf=buf, w=w, tw=tw, kc=kc: e.tensor_tensor(
                                s_bf[buf][0:w, 0:tw], pSp[buf][0:w, 0:tw],
                                masks[0:w, MASKW * kc:MASKW * kc + tw], ALU.mult),
                            waits=[("TE", t_s), ("TE", sbf_prev)])
                        for u in range(4):
                            for (k2, a, b_, pa) in SLAB_PARTS[u]:
                                if k2 != kc:
                                    continue
                                wl_a = [("VE", v_sbf[(h, kc)])]
                                if h == 0 and first_a[u]:
                                    wl_a.append(("VE", v_vevac[N_XT - 1]))
                                if first_a[u]:
                                    pkey = (h, u - 2) if u >= 2 else (h - 1, u + 2)
                                    wl_a.append(("VE", v_aevac.get(pkey)))
                                    wl_a.append(("VE", v_zevac.get(pkey)))
                                    if h == 0 and u == 0:
                                        wl_a.append(("VE", v_invg1))
                                wm = watermark[u]
                                pieces = []
                                if pa < wm:
                                    pieces.append((a, a + min(wm - pa, b_ - a), pa, False))
                                if pa + (b_ - a) > wm:
                                    a2 = a + max(0, wm - pa)
                                    pieces.append((a2, b_, pa + (a2 - a), first_a[u]))
                                for (aa, bb, pp, st_flag) in pieces:
                                    op("tensor", "TE",
                                        lambda e, u=u, aa=aa, bb=bb, pp=pp, buf=buf,
                                               w=w, kc=kc, h=h, fa=st_flag: e.matmul(
                                            pA[u % 2][:, pp:pp + (bb - aa)],
                                            v_tm[0:w, kc, 128 * h:128 * h + 128],
                                            s_bf[buf][0:w, aa:bb],
                                            start=fa, stop=True,
                                            skip_group_check=True),
                                        waits=wl_a)
                                    wl_a = []
                                    t_z = op("tensor", "TE",
                                        lambda e, u=u, aa=aa, bb=bb, pp=pp, buf=buf,
                                               w=w, fz=st_flag: e.matmul(
                                            pZ[u % 2][:, pp:pp + (bb - aa)],
                                            ones_mb[0:w, :], s_bf[buf][0:w, aa:bb],
                                            start=fz, stop=True,
                                            skip_group_check=True))
                                watermark[u] = max(wm, pa + (b_ - a))
                                first_a[u] = False
                                first_z[u] = False
                                t_band[u] = t_z
                                t_sbf_free[(h, kc)] = t_z
                        for u in range(4):
                            if SLAB_LAST_KC[u] != kc:
                                continue
                            idx = h * 4 + u
                            v_aevac[(h, u)] = op("vector", "VE",
                                lambda e, u=u, h=h: e.tensor_copy(
                                    attn_r[:, h, 512 * u:512 * u + 512],
                                    pA[u % 2][:, 0:512]),
                                waits=[("TE", t_band[u])])
                            v_zevac[(h, u)] = op("vector", "VE",
                                lambda e, u=u: e.tensor_copy(
                                    z_stage[u % 2][0:1, :], pZ[u % 2][0:1, 0:512]),
                                waits=[(f"cZS{u % 2}", d_zs.get(idx - 2))])
                            d_zs[idx] = dma(f"cZS{u % 2}",
                                            zdram_f[h:h + 1, 512 * u:512 * u + 512],
                                            z_stage[u % 2][0:1, :],
                                            waits=[("VE", v_zevac[(h, u)])])

                d_zall = dma("cZA", z_all[:], zdram_f,
                             waits=[("cZS0", d_zs[NH * 4 - 2]),
                                    ("cZS1", d_zs[NH * 4 - 1])])
                vz1 = op("vector", "VE", lambda e: e.tensor_scalar(
                    z_all[:], z_all[:], EPS_Z, None, ALU.add),
                    waits=[("cZA", d_zall)])
                vz2 = op("vector", "VE", lambda e: e.reciprocal(z2[:], z_all[:]),
                         waits=[("VE", vz1)])
                v_zi = op("vector", "VE", lambda e: e.tensor_copy(zinv_b[:], z2[:]),
                          waits=[("VE", vz2)])
                d_z1 = dma("cZI", zdram, zinv_b[:], waits=[("VE", v_zi)])
                v_gabs2_last = v_gabs[31]
                sq_scr0 = C.get("cQW0", 0)
                sq_scr1 = C.get("cQW1", 0)
                d_z1_val = C["cZI"]
            h_end = dict(C)
            PH["heads"] = len(steps)

        with contextlib.ExitStack() as es_f:
            h_sb = es_f.enter_context(nc.sbuf_tensor("h_sb", [128, MF, 512], BF16))
            g_fm = es_f.enter_context(nc.sbuf_tensor("g_fm", [128, DK, 512], BF16))
            g_tm = es_f.enter_context(nc.sbuf_tensor("g_tm", [128, N_TC, D], BF16))
            wq1b = [es_f.enter_context(nc.sbuf_tensor(f"wq1b{i}", [128, 2048], BF16))
                    for i in range(2)]
            w2tb = [es_f.enter_context(nc.sbuf_tensor(f"w2tb{i}", [128, 2048], BF16))
                    for i in range(2)]
            wout_b = es_f.enter_context(nc.sbuf_tensor("wout_b", [128, DK, D], BF16))
            qb2 = [es_f.enter_context(nc.sbuf_tensor(f"qb2_{i}", [128, 2048], F32))
                   for i in range(2)]
            qo2 = [es_f.enter_context(nc.sbuf_tensor(f"qo2_{i}", [128, 2048], BF16))
                   for i in range(2)]
            zr2 = es_f.enter_context(nc.sbuf_tensor("zr2", [1, T], BF16))
            xtf = qb2

            with contextlib.ExitStack() as es_p1:
                ph = [es_p1.enter_context(nc.psum_tensor(f"ph{i}", [128, 512], F32))
                      for i in range(2)]
                pg = [es_p1.enter_context(nc.psum_tensor(f"pg{i}", [128, 512], F32))
                      for i in range(2)]
                pgt = [es_p1.enter_context(nc.psum_tensor(f"pgt{i}", [128, D], BF16))
                       for i in range(2)]
                psf = [es_p1.enter_context(nc.psum_tensor(f"psf{i}", [128, 512], F32))
                       for i in range(2)]

                hw = [("TE", h_end["TE"]), ("VE", h_end["VE"]), ("SE", h_end["SE"])]

                v_woc = {}
                for dc in range(DK):
                    ch = f"cWO{dc % 2}"
                    xw = [("VE", v_woc.get(dc - 2))]
                    if dc < 2:
                        xw += hw
                    d_wo = dma(ch, qb2[dc % 2][:, 0:1024],
                               wout_d[128 * dc:128 * dc + 128, :], waits=xw)
                    v_woc[dc] = op("vector", "VE", lambda e, dc=dc: e.tensor_copy(
                        wout_b[:, dc, :], qb2[dc % 2][:, 0:1024]),
                        waits=[(ch, d_wo)])
                v_wo_last = v_woc[DK - 1]

                v_invg2 = emit_invg(1, gam[:, 16:32], ph[0], v_gabs2_last)

                q2_dma = {}
                q2_cons = {}
                q2_wr = {}
                n2 = [0]

                def emit_quant2():
                    i2 = n2[0]
                    if i2 >= 16:
                        return
                    n2[0] += 1
                    i = 16 + i2
                    b = i2 % 2
                    xw = [q2_cons.get(i2 - 2)]
                    if i2 < 2:
                        xw += [("VE", v_woc[6 + b])]
                        xw += hw
                    q2_dma[i2] = dma(f"cQ2{b}", qb2[b][:], qpiece_in(i),
                                     waits=[x for x in xw if x], q="gpsimd")
                    p1 = op("scalar", "SE", lambda e, b=b: e.activation(
                        qb2[b][:], qb2[b][:], AF.Identity,
                        scale=invg[:, 1:2], bias=magic_c[:]),
                        waits=[(f"cQ2{b}", q2_dma[i2]), ("VE", v_invg2)])
                    p2 = op("vector", "VE", lambda e, b=b: e.tensor_scalar(
                        qb2[b][:], qb2[b][:], MAGIC, 1.0, ALU.subtract, ALU.min),
                        waits=[("SE", p1)])
                    p3 = op("vector", "VE", lambda e, b=b: e.tensor_scalar(
                        qo2[b][:], qb2[b][:], -1.0, None, ALU.max),
                        waits=[(f"cQ2W{b}", q2_wr.get(i2 - 2)), ("VE", p2)])
                    for (o, s) in qpiece_out(i, qo2[b]):
                        q2_wr[i2] = dma(f"cQ2W{b}", o, s, waits=[("VE", p3)],
                                        q="gpsimd")
                    q2_cons[i2] = ("VE", p3)

                v_div = {}
                d_z2d = {}
                for h2 in range(NH):
                    xw = [("cZI", d_z1_val), ("VE", v_div.get(4 * h2 - 1))]
                    if h2 == 0:
                        xw += hw
                    d_z2d[h2] = dma("cZR", zr2[0:1, :],
                                    bass.AP(zdram.tensor, h2 * T, [[T, 1], [1, T]]),
                                    waits=xw)
                    for u in range(4):
                        idx = h2 * 4 + u
                        buf = idx % 2
                        wl = [("cZR", d_z2d[h2])]
                        if idx >= 2:
                            wl.append(("VE", v_div[idx - 2]))
                        t_r = op("tensor", "TE", lambda e, buf=buf, u=u: e.matmul(
                            psf[buf][:, 0:512], ones_rb[:],
                            zr2[0:1, 512 * u:512 * u + 512],
                            start=True, stop=True, skip_group_check=True),
                            waits=wl)
                        v_div[idx] = op("vector", "VE",
                            lambda e, buf=buf, h2=h2, u=u: e.tensor_tensor(
                                attn_r[:, h2, 512 * u:512 * u + 512],
                                attn_r[:, h2, 512 * u:512 * u + 512],
                                psf[buf][:, 0:512], ALU.mult),
                            waits=[("TE", t_r)])
                v_div_last = v_div[NH * 4 - 1]

                h_ev = {}
                h_eng = {}
                t_wq1 = {}
                t_w2_half = {}
                t_ffn2_last = {}
                s_gev = {}
                v_gt = {}
                v_stt = {}
                v_ob = {}
                d_out = {}

                def emit_final(tc):
                    buf = tc % 2
                    xw = [("VE", v_ob.get(tc - 2))]
                    if tc < 2:
                        c2 = q2_cons.get(14 + buf)
                        if c2:
                            xw.append(c2)
                        xw.append((f"cQ2{buf}", q2_dma.get(14 + buf)))
                    d_x = dma(f"cXF{buf}", xtf[buf][:, 0:1024],
                              x_tm_d[HALF + 128 * tc:HALF + 128 * tc + 128, :],
                              waits=xw)
                    for hh in range(2):
                        t_last = None
                        for dc in range(DK):
                            wl = []
                            if dc == 0:
                                wl = [("VE", v_gt[tc]), ("VE", v_consts)]
                                if tc == 0 and hh == 0:
                                    wl += [("VE", v_div_last), ("VE", v_wo_last)]
                                pv_ = v_stt.get((tc - 1, hh))
                                if tc == 0:
                                    pv_ = v_div_last
                                wl.append(("VE", pv_))
                            t_last = op("tensor", "TE",
                                lambda e, hh=hh, dc=dc, tc=tc: e.matmul(
                                    psf[hh][:, 0:512],
                                    attn_r[:, dc, 128 * tc:128 * (tc + 1)],
                                    wout_b[:, dc, 512 * hh:512 * hh + 512],
                                    start=(dc == 0), stop=False,
                                    skip_group_check=True),
                                waits=wl)
                        t_last = op("tensor", "TE",
                            lambda e, hh=hh: e.matmul(
                                psf[hh][:, 0:512], ones_rb[:],
                                bout_b[0:1, 512 * hh:512 * hh + 512],
                                start=False, stop=True, skip_group_check=True))
                        v_stt[(tc, hh)] = op("vector", "VE",
                            lambda e, buf=buf, tc=tc, hh=hh: e.scalar_tensor_tensor(
                                qb2_out[buf][:, 512 * hh:512 * hh + 512],
                                psf[hh][:, 0:512], 0.0,
                                g_tm[:, tc, 512 * hh:512 * hh + 512],
                                ALU.add, ALU.mult),
                            waits=[("TE", t_last), (f"cXF{buf}", d_x),
                                   (f"cOB{buf}", d_out.get(tc - 2))])
                    v_ob[tc] = op("vector", "VE", lambda e, buf=buf: e.tensor_tensor(
                        qb2_out[buf][:], qb2_out[buf][:], xtf[buf][:, 0:1024],
                        ALU.add),
                        waits=[("VE", v_stt[(tc, 0)]), ("VE", v_stt[(tc, 1)])])
                    d_out[tc] = dma(f"cOB{buf}", out_d[128 * tc:128 * (tc + 1), :],
                                    qb2_out[buf][:], waits=[("VE", v_ob[tc])])

                qb2_out = [bass.AP(qb2[0], 1024, [[2048, 128], [1, 1024]]),
                           bass.AP(qb2[1], 1024, [[2048, 128], [1, 1024]])]

                for u in range(4):
                    xs = HALF + 512 * u
                    for mp in range(16):
                        wv = []
                        if (u, mp - 2) in t_wq1:
                            wv.append(("TE", t_wq1[(u, mp - 2)]))
                        elif u > 0 and mp < 2:
                            wv.append(("TE", t_wq1[(u - 1, 14 + mp)]))
                        if u == 0 and mp < 2:
                            wv.append(("cQW0", sq_scr0))
                            wv.append(("cQW1", sq_scr1))
                            wv += hw
                        d_wq1 = dma(
                            f"cW1{mp % 2}", wq1b[mp % 2][:],
                            bass.AP(wq1_sc.tensor, mp * (128 * 8 * 256),
                                    [[8 * 256, 128], [1, 8 * 256]]),
                            waits=wv, q="gpsimd")
                        for mo in range(2):
                            m = 2 * mp + mo
                            t_last = None
                            for kc in range(DK):
                                wl = []
                                if kc == 0:
                                    wl = [(f"cW1{mp % 2}", d_wq1)]
                                    if m >= 2:
                                        wl.append((h_eng[(u, m - 2)], h_ev[(u, m - 2)]))
                                    elif u > 0:
                                        wl.append((h_eng[(u - 1, 30 + m)],
                                                   h_ev[(u - 1, 30 + m)]))
                                t_last = op("tensor", "TE",
                                    lambda e, m=m, kc=kc, xs=xs, mp=mp: e.matmul(
                                        ph[m % 2][:, 0:512],
                                        wq1b[mp % 2][:, 256 * kc + 128 * (m - 2 * mp):
                                                      256 * kc + 128 * (m - 2 * mp) + 128],
                                        xn[:, kc, xs:xs + 512],
                                        start=(kc == 0), stop=(kc == DK - 1),
                                        skip_group_check=True),
                                    waits=wl)
                            t_wq1[(u, mp)] = t_last
                            if m % 2 == 0:
                                h_eng[(u, m)] = "VE"
                                h_ev[(u, m)] = op("vector", "VE",
                                    lambda e, m=m: e.tensor_scalar(
                                        h_sb[:, m, :], ph[m % 2][:, 0:512],
                                        b1t[:, m:m + 1], 0.0, ALU.add, ALU.max),
                                    waits=[("TE", t_last)])
                            else:
                                h_eng[(u, m)] = "SE"
                                h_ev[(u, m)] = op("scalar", "SE",
                                    lambda e, m=m: e.activation(
                                        h_sb[:, m, :], ph[m % 2][:, 0:512], AF.Relu,
                                        bias=b1t[:, m:m + 1]),
                                    waits=[("TE", t_last)])
                        if u == 0:
                            emit_quant2()
                    for dc in range(DK):
                        t_last = None
                        for half in range(2):
                            wv = [("TE", t_w2_half.get(half))]
                            if u == 0 and dc == 0:
                                wv.append((f"cQ2W0", q2_wr.get(14)))
                                wv.append((f"cQ2W1", q2_wr.get(15)))
                                wv += hw
                            d_w2t = dma(
                                f"cW2{half}", w2tb[half][:],
                                bass.AP(wq2_sc.tensor,
                                        dc * (128 * 32 * 128) + half * 2048,
                                        [[32 * 128, 128], [1, 2048]]),
                                waits=wv, q="gpsimd")
                            for kk in range(16):
                                kc2 = 16 * half + kk
                                wl = []
                                if kk == 0:
                                    wl = [(f"cW2{half}", d_w2t)]
                                    if half == 0:
                                        wl += [("VE", h_ev[(u, 30)]),
                                               ("SE", h_ev[(u, 31)]),
                                               ("SE", s_gev.get((u, dc - 2)))]
                                t_last = op("tensor", "TE",
                                    lambda e, dc=dc, kc2=kc2, half=half, kk=kk: e.matmul(
                                        pg[dc % 2][:, 0:512],
                                        w2tb[half][:, 128 * kk:128 * kk + 128],
                                        h_sb[:, kc2, :],
                                        start=(kc2 == 0), stop=(kc2 == MF - 1),
                                        skip_group_check=True),
                                    waits=wl)
                            t_w2_half[half] = t_last
                        t_ffn2_last[(u, dc)] = t_last
                        s_gev[(u, dc)] = op("scalar", "SE",
                            lambda e, dc=dc: e.activation(
                                g_fm[:, dc, :], pg[dc % 2][:, 0:512], AF.Silu,
                                bias=b2t[:, dc:dc + 1]),
                            waits=[("TE", t_last)])
                    for tt in range(4):
                        tc = 4 * u + tt
                        buf = tt % 2
                        t_tr = None
                        for dc2 in range(DK):
                            wl = []
                            if dc2 == 0:
                                wl = [("SE", s_gev[(u, DK - 1)])]
                                if tc >= 2:
                                    wl.append(("VE", v_gt.get(tc - 2)))
                            t_tr = op("tensor", "TE",
                                lambda e, buf=buf, dc2=dc2, tt=tt: e.transpose(
                                    pgt[buf][:, 128 * dc2:128 * (dc2 + 1)],
                                    g_fm[0:128, dc2, 128 * tt:128 * (tt + 1)],
                                    ident[:]),
                                waits=wl)
                        v_gt[tc] = op("vector", "VE",
                            lambda e, buf=buf, tc=tc: e.tensor_copy(
                                g_tm[:, tc, :], pgt[buf][:]),
                            waits=[("TE", t_tr)])
                        emit_final(tc)

                fin0 = C.get("cOB0", 0)
                fin1 = C.get("cOB1", 0)

                def f_fin(e, fin0=fin0, fin1=fin1):
                    e.wait_ge(SEM["cOB0"], fin0)
                    e.wait_ge(SEM["cOB1"], fin1)
                steps.append(("sync", f_fin, 0))

        PH["all"] = len(steps)
        nc._phase_marks = PH
        nc._steps_meta = [(s[0], s[2]) for s in steps]
        block = es.enter_context(nc.Block())
        use_steps = steps if cut is None else steps[:cut]

        def mk(section_name):
            def _(eng):
                for st in use_steps:
                    if st[0] == section_name:
                        st[1](eng)
            return _

        block.sync(mk("sync"))
        block.tensor(mk("tensor"))
        block.vector(mk("vector"))
        block.scalar(mk("scalar"))
        block.gpsimd(mk("gpsimd"))

    return nc


_NC_CACHE = None


def _get_nc():
    global _NC_CACHE
    if _NC_CACHE is None:
        _NC_CACHE = build_nc()
    return _NC_CACHE


def _host_masks():
    m = np.zeros((128, 17 * MASKW), dtype=np.float32)
    for kc in range(17):
        j0, w = KCS[kc]
        tb, te = _bwin(kc)
        jj = np.arange(w)[:, None]
        tt = np.arange(te - tb)[None, :]
        m[0:w, MASKW * kc:MASKW * kc + (te - tb)] = (
            np.abs((tb + tt) - (j0 + jj)) <= HALF).astype(np.float32)
    return m.astype(ml_dtypes.bfloat16)


def kernel(L0_attn_x, L0_attn_w_qkv, L0_attn_proj, L0_attn_w_out, L0_attn_b_out,
           L0_ffn_W1, L0_ffn_b1, L0_ffn_W2, L0_ffn_b2):
    x = np.asarray(L0_attn_x, dtype=np.float32)
    wqkv = np.ascontiguousarray(np.asarray(L0_attn_w_qkv, dtype=np.float32))
    proj = np.ascontiguousarray(np.asarray(L0_attn_proj, dtype=np.float32))
    wout = np.ascontiguousarray(np.asarray(L0_attn_w_out, dtype=np.float32))
    bout = np.ascontiguousarray(np.asarray(L0_attn_b_out, dtype=np.float32).reshape(1, D))
    w1 = np.asarray(L0_ffn_W1, dtype=np.float32)
    b1 = np.asarray(L0_ffn_b1, dtype=np.float32)
    w2 = np.asarray(L0_ffn_W2, dtype=np.float32)
    b2 = np.asarray(L0_ffn_b2, dtype=np.float32)

    w1t = np.ascontiguousarray(w1.T)
    w2t = np.ascontiguousarray(w2.T)
    b1t = np.ascontiguousarray(b1.reshape(MF, 128).T)
    b2t = np.ascontiguousarray(b2.reshape(DK, 128).T)
    ident = np.eye(128, dtype=ml_dtypes.bfloat16)
    masks = _host_masks()

    in_maps = []
    for c in range(8):
        seq, t0 = c // 4, (c % 4) * T
        sl = np.zeros((P, D), dtype=np.float32)
        lo, hi = max(0, t0 - HALF), min(8192, t0 + T + HALF)
        sl[lo - (t0 - HALF):lo - (t0 - HALF) + (hi - lo)] = x[seq, lo:hi]
        em = np.ones((64, 6), dtype=np.float32)
        if t0 == 0:
            em[:, 0:3] = 0.0
        if t0 + T == 8192:
            em[:, 3:6] = 0.0
        in_maps.append({
            "x_tm": sl, "wqkv": wqkv, "proj": proj, "wout": wout, "bout": bout,
            "w1t": w1t, "b1t": b1t, "w2t": w2t, "b2t": b2t,
            "ident": ident, "masks": masks, "emask": em,
        })

    nc = _get_nc()
    res = run_bass_kernel_spmd(nc, in_maps, core_ids=list(range(8)))
    out = np.empty((2, 8192, D), dtype=np.float32)
    for c in range(8):
        seq, t0 = c // 4, (c % 4) * T
        out[seq, t0:t0 + T] = res.results[c]["out"]
    return out


# version 07
# speedup vs baseline: 1.1089x; 1.1089x over previous
import contextlib
import numpy as np
import ml_dtypes
import concourse.bass as bass
import concourse.mybir as mybir
from concourse.bass_utils import run_bass_kernel_spmd

F32 = mybir.dt.float32
BF16 = mybir.dt.bfloat16
AF = mybir.ActivationFunctionType
ALU = mybir.AluOpType

P = 2054
T = 2048
HALF = 3
D = 1024
DK = 8
NH = 8
FD = 64
DFF = 4096
MF = 32
MAGIC = 12582912.0
EPS_LN = 1e-5
EPS_Q = 1e-5
EPS_Z = 1e-8
N_ELEM_W = float(DFF * D)
MASKW = 134
N_XT = 17
N_TC = 16

SLABS5 = [(0, 512), (512, 512), (1024, 512), (1536, 512), (2048, 6)]
KCS = [(k * 128, min(128, P - k * 128)) for k in range(17)]


def _bwin(kc):
    j0, w = KCS[kc]
    return max(HALF, j0 - HALF), min(P - HALF, j0 + w + HALF)


SLAB_PARTS = [[] for _ in range(4)]
SLAB_LAST_KC = [0] * 4
for _kc in range(17):
    _tb, _te = _bwin(_kc)
    for _u in range(4):
        _su, _eu = HALF + 512 * _u, HALF + 512 * (_u + 1)
        _lo, _hi = max(_tb, _su), min(_te, _eu)
        if _lo < _hi:
            SLAB_PARTS[_u].append((_kc, _lo - _tb, _hi - _tb, _lo - _su))
            SLAB_LAST_KC[_u] = max(SLAB_LAST_KC[_u], _kc)


def build_nc(cut=None):
    nc = bass.Bass("TRN2", target_bir_lowering=False)

    x_tm_d = nc.dram_tensor("x_tm", [P, D], F32, kind="ExternalInput").ap()
    wqkv_d = nc.dram_tensor("wqkv", [D, 3 * D], F32, kind="ExternalInput").ap()
    proj_d = nc.dram_tensor("proj", [128, FD], F32, kind="ExternalInput").ap()
    wout_d = nc.dram_tensor("wout", [D, D], F32, kind="ExternalInput").ap()
    bout_d = nc.dram_tensor("bout", [1, D], F32, kind="ExternalInput").ap()
    w1t_d = nc.dram_tensor("w1t", [D, DFF], F32, kind="ExternalInput").ap()
    b1t_d = nc.dram_tensor("b1t", [128, MF], F32, kind="ExternalInput").ap()
    w2t_d = nc.dram_tensor("w2t", [DFF, D], F32, kind="ExternalInput").ap()
    b2t_d = nc.dram_tensor("b2t", [128, DK], F32, kind="ExternalInput").ap()
    ident_d = nc.dram_tensor("ident", [128, 128], BF16, kind="ExternalInput").ap()
    masks_d = nc.dram_tensor("masks", [128, 17 * MASKW], BF16, kind="ExternalInput").ap()
    emask_d = nc.dram_tensor("emask", [64, 6], F32, kind="ExternalInput").ap()
    out_d = nc.dram_tensor("out", [T, D], F32, kind="ExternalOutput").ap()

    wq1_sc = nc.dram_tensor("wq1_sc", [16, 128, 8, 256], BF16, kind="Internal").ap()
    wq2_sc = nc.dram_tensor("wq2_sc", [8, 128, 32, 128], BF16, kind="Internal").ap()
    zdram = nc.dram_tensor("zdram", [NH, T], BF16, kind="Internal").ap()
    zdram_f = nc.dram_tensor("zdram_f", [NH, T], F32, kind="Internal").ap()

    steps = []
    PH = {}
    C = {"TE": 0, "VE": 0, "SE": 0}

    def bump(name, n=1):
        C[name] = C.get(name, 0) + n
        return C[name]

    es = contextlib.ExitStack()
    with es:
        te_s = es.enter_context(nc.semaphore("te"))
        ve_s = es.enter_context(nc.semaphore("ve"))
        se_s = es.enter_context(nc.semaphore("se"))
        SEM = {"TE": te_s, "VE": ve_s, "SE": se_s}
        sem_es = es

        def dch(name):
            if name not in SEM:
                SEM[name] = sem_es.enter_context(nc.semaphore(name))
                C[name] = 0
            return SEM[name]

        def dma(ch, out, in_, waits=(), q="sync"):
            hnd = dch(ch)
            w = [(s, v) for (s, v) in waits if v]
            val = bump(ch, 16)

            def f(e, out=out, in_=in_, w=w, hnd=hnd):
                for (s, v) in w:
                    e.wait_ge(SEM[s], v)
                e.dma_start(out=out, in_=in_).then_inc(hnd, 16)
            import traceback
            steps.append((q, f, traceback.extract_stack()[-2].lineno))
            return val

        def op(eng, sem_name, emitf, waits=(), inc=1):
            w = [(s, v) for (s, v) in waits if v]
            val = bump(sem_name, inc) if inc else C[sem_name]

            def f(e, emitf=emitf, w=w, inc=inc, sem_name=sem_name):
                for (s, v) in w:
                    e.wait_ge(SEM[s], v)
                ins = emitf(e)
                if inc:
                    ins.then_inc(SEM[sem_name], inc)
            import traceback
            steps.append((eng, f, traceback.extract_stack()[-2].lineno))
            return val

        ident = es.enter_context(nc.sbuf_tensor("sb_ident", [128, 128], BF16))
        emask = es.enter_context(nc.sbuf_tensor("sb_emask", [64, 6], F32))
        proj_f = es.enter_context(nc.sbuf_tensor("proj_f", [128, FD], F32))
        proj_b = es.enter_context(nc.sbuf_tensor("proj_b", [128, FD], BF16))
        b1t = es.enter_context(nc.sbuf_tensor("sb_b1t", [128, MF], F32))
        b2t = es.enter_context(nc.sbuf_tensor("sb_b2t", [128, DK], F32))
        bout_f = es.enter_context(nc.sbuf_tensor("bout_f", [1, D], F32))
        bout_b = es.enter_context(nc.sbuf_tensor("bout_b", [1, D], BF16))
        ones_rb = es.enter_context(nc.sbuf_tensor("ones_rb", [1, 128], BF16))
        ones_mb = es.enter_context(nc.sbuf_tensor("ones_mb", [128, 128], BF16))
        ones_cf = es.enter_context(nc.sbuf_tensor("ones_cf", [128, 1], F32))
        magic_c = es.enter_context(nc.sbuf_tensor("magic_c", [128, 1], F32))
        ones_rf = es.enter_context(nc.sbuf_tensor("ones_rf", [1, 128], F32))
        gam = es.enter_context(nc.sbuf_tensor("gam", [128, 32], F32))
        gc2 = es.enter_context(nc.sbuf_tensor("gc2", [128, 2], F32))
        grow = es.enter_context(nc.sbuf_tensor("grow", [1, 2], F32))
        gtmp = es.enter_context(nc.sbuf_tensor("gtmp", [128, 2], F32))
        invg = es.enter_context(nc.sbuf_tensor("invg", [128, 2], F32))

        xn = es.enter_context(nc.sbuf_tensor("xn", [128, DK, P], BF16))
        attn_r = es.enter_context(nc.sbuf_tensor("attn_r", [128, NH, T], BF16))

        v_c0 = 0
        for (dst, src) in [(ident[:], ident_d), (emask[:], emask_d),
                           (proj_f[:], proj_d), (b1t[:], b1t_d), (b2t[:], b2t_d),
                           (bout_f[:], bout_d)]:
            v_c0 = dma("cA", dst, src)
        op("vector", "VE", lambda e: e.memset(ones_rb[:], 1.0), inc=0)
        op("vector", "VE", lambda e: e.memset(ones_mb[:], 1.0), inc=0)
        op("vector", "VE", lambda e: e.memset(ones_cf[:], 1.0), inc=0)
        op("vector", "VE", lambda e: e.memset(magic_c[:], MAGIC), inc=0)
        op("vector", "VE", lambda e: e.memset(ones_rf[:], 1.0), inc=0)
        op("vector", "VE", lambda e: e.tensor_copy(proj_b[:], proj_f[:]),
           waits=[("cA", v_c0)], inc=0)
        v_consts = op("vector", "VE", lambda e: e.tensor_copy(bout_b[:], bout_f[:]),
                      waits=[("cA", v_c0)])

        def qpiece_in(i):
            if i < 16:
                kc, hh = i // 2, i % 2
                return w1t_d[128 * kc:128 * kc + 128, 2048 * hh:2048 * hh + 2048]
            j = i - 16
            return bass.AP(w2t_d.tensor, 256 * j * D, [[D, 128], [128 * D, 2], [1, D]])

        def qpiece_out(i, qo_t):
            src = qo_t[:]
            if i < 16:
                kc, hh = i // 2, i % 2
                o = bass.AP(wq1_sc.tensor, (8 * hh) * (128 * 8 * 256) + kc * 256,
                            [[8 * 256, 128], [128 * 8 * 256, 8], [1, 256]])
                return [(o, src.rearrange("p (mp c) -> p mp c", mp=8))]
            j = i - 16
            pairs = []
            for rr in range(2):
                o = bass.AP(wq2_sc.tensor, (2 * j + rr) * 128,
                            [[32 * 128, 128], [128 * 32 * 128, 8], [1, 128]])
                s = bass.AP(qo_t, rr * 1024, [[2048, 128], [128, 8], [1, 128]])
                pairs.append((o, s))
            return pairs

        def emit_invg(col, gcols, psum_t, wait_v):
            vr = op("vector", "VE", lambda e: e.tensor_reduce(
                gc2[:, col:col + 1], gcols, mybir.AxisListType.X, ALU.add),
                waits=[("VE", wait_v)])
            t1 = op("tensor", "TE", lambda e: e.matmul(
                psum_t[0:1, 0:1], ones_cf[:], gc2[:, col:col + 1],
                start=True, stop=True, skip_group_check=True),
                waits=[("VE", vr)])
            v_gr = op("vector", "VE", lambda e: e.tensor_copy(
                grow[0:1, col:col + 1], psum_t[0:1, 0:1]), waits=[("TE", t1)])
            t2 = op("tensor", "TE", lambda e: e.matmul(
                psum_t[0:128, 2:3], ones_rf[:], grow[0:1, col:col + 1],
                start=True, stop=True, skip_group_check=True),
                waits=[("VE", v_gr)])
            vg1 = op("vector", "VE", lambda e: e.tensor_scalar(
                gtmp[:, col:col + 1], psum_t[0:128, 2:3], 1.0 / N_ELEM_W, EPS_Q,
                ALU.mult, ALU.add), waits=[("TE", t2)])
            return op("vector", "VE", lambda e: e.reciprocal(
                invg[:, col:col + 1], gtmp[:, col:col + 1]), waits=[("VE", vg1)])

        with contextlib.ExitStack() as es_vh:
            v_tm = es_vh.enter_context(nc.sbuf_tensor("v_tm", [128, N_XT, D], BF16))
            qbuf = [es_vh.enter_context(nc.sbuf_tensor(f"qb{i}", [128, 2048], F32))
                    for i in range(3)]
            qo = [es_vh.enter_context(nc.sbuf_tensor(f"qo{i}", [128, 2048], BF16))
                  for i in range(2)]

            q_dma = {}
            q_cons = {}
            q_wr = {}
            v_gabs = {}
            nqnt = [0]

            def emit_gamma(seq, i, extra=()):
                b = seq % 3
                xw = [q_cons.get(seq - 3)] + list(extra)
                q_dma[seq] = dma(f"cQB{b}", qbuf[b][:], qpiece_in(i),
                                 waits=[x for x in xw if x], q="gpsimd")
                v_gabs[i] = op("vector", "VE", lambda e, b=b, i=i: e.tensor_reduce(
                    gam[:, i:i + 1], qbuf[b][:], mybir.AxisListType.X, ALU.add,
                    apply_absolute_value=True),
                    waits=[(f"cQB{b}", q_dma[seq])])
                q_cons[seq] = ("VE", v_gabs[i])

            def emit_quant(seq, i, v_invg, extra=()):
                b = seq % 3
                col = 0 if i < 16 else 1
                bo = nqnt[0] % 2
                qn = nqnt[0]
                nqnt[0] += 1
                xw = [q_cons.get(seq - 3)] + list(extra)
                q_dma[seq] = dma(f"cQB{b}", qbuf[b][:], qpiece_in(i),
                                 waits=[x for x in xw if x], q="gpsimd")
                p1 = op("scalar", "SE", lambda e, b=b, col=col: e.activation(
                    qbuf[b][:], qbuf[b][:], AF.Identity,
                    scale=invg[:, col:col + 1], bias=magic_c[:]),
                    waits=[(f"cQB{b}", q_dma[seq]), ("VE", v_invg)])
                p2 = op("vector", "VE", lambda e, b=b: e.tensor_scalar(
                    qbuf[b][:], qbuf[b][:], MAGIC, 1.0, ALU.subtract, ALU.min),
                    waits=[("SE", p1)])
                p3 = op("vector", "VE", lambda e, b=b, bo=bo: e.tensor_scalar(
                    qo[bo][:], qbuf[b][:], -1.0, None, ALU.max),
                    waits=[(f"cQW{bo}", q_wr.get(qn - 2)), ("VE", p2)])
                for (o, s) in qpiece_out(i, qo[bo]):
                    q_wr[qn] = dma(f"cQW{bo}", o, s, waits=[("VE", p3)], q="gpsimd")
                q_cons[seq] = ("VE", p3)

            tr_done = {}
            v_xn_evac = {}
            v_vevac = {}
            with contextlib.ExitStack() as es_lnv:
                ptr = [es_lnv.enter_context(nc.psum_tensor(f"ptr{i}", [128, D], BF16))
                       for i in range(2)]
                pv = [es_lnv.enter_context(nc.psum_tensor(f"pv{i}", [128, D], F32))
                      for i in range(2)]

                with contextlib.ExitStack() as es_ln:
                    xt = [es_ln.enter_context(nc.sbuf_tensor(f"xt{i}", [128, D], F32))
                          for i in range(4)]
                    junk = es_ln.enter_context(nc.sbuf_tensor("junk", [128, D], F32))
                    ssum = es_ln.enter_context(nc.sbuf_tensor("ssum", [128, N_XT], F32))
                    ssq = es_ln.enter_context(nc.sbuf_tensor("ssq", [128, N_XT], F32))
                    m2c = es_ln.enter_context(nc.sbuf_tensor("m2c", [128, N_XT], F32))
                    rstd = es_ln.enter_context(nc.sbuf_tensor("rstd", [128, N_XT], F32))
                    nmr = es_ln.enter_context(nc.sbuf_tensor("nmr", [128, N_XT], F32))
                    xnt = es_ln.enter_context(nc.sbuf_tensor("xnt", [128, 2, D], BF16))

                    op("vector", "VE", lambda e: e.memset(ssum[:], 0.0), inc=0)
                    v_ms = op("vector", "VE", lambda e: e.memset(ssq[:], 0.0))

                    d_x1 = {}
                    s_sq = {}
                    v_sm = {}
                    for tc in range(N_XT):
                        rows = 128 if tc < 16 else 6
                        b = tc % 4
                        d_x1[tc] = dma(f"cX{b}", xt[b][0:rows, :],
                                       x_tm_d[128 * tc:128 * tc + rows, :],
                                       waits=[("SE", s_sq.get(tc - 4)),
                                              ("VE", v_sm.get(tc - 4))])
                        s_sq[tc] = op("scalar", "SE",
                            lambda e, tc=tc, rows=rows, b=b: e.activation(
                                junk[0:rows, :], xt[b][0:rows, :], AF.Square,
                                accum_out=ssq[0:rows, tc:tc + 1]),
                            waits=[(f"cX{b}", d_x1[tc]),
                                   ("SE", s_sq.get(tc - 1)),
                                   ("VE", v_ms if tc == 0 else None)])
                        v_sm[tc] = op("vector", "VE",
                            lambda e, tc=tc, rows=rows, b=b: e.tensor_reduce(
                                ssum[0:rows, tc:tc + 1], xt[b][0:rows, :],
                                mybir.AxisListType.X, ALU.add),
                            waits=[(f"cX{b}", d_x1[tc]),
                                   ("VE", v_ms if tc == 0 else None)])
                    w1_ = op("vector", "VE", lambda e: e.tensor_scalar(
                        ssum[:], ssum[:], 1.0 / D, None, ALU.mult),
                        waits=[("SE", s_sq[N_XT - 1]), ("VE", v_sm[N_XT - 1])])
                    w2_ = op("vector", "VE", lambda e: e.tensor_tensor(
                        m2c[:], ssum[:], ssum[:], ALU.mult), waits=[("VE", w1_)])
                    w3_ = op("vector", "VE", lambda e: e.scalar_tensor_tensor(
                        ssq[:], ssq[:], 1.0 / D, m2c[:], ALU.mult, ALU.subtract),
                        waits=[("VE", w2_)])
                    w4_ = op("vector", "VE", lambda e: e.tensor_scalar(
                        ssq[:], ssq[:], EPS_LN, None, ALU.add), waits=[("VE", w3_)])
                    v_stats = op("vector", "VE", lambda e: e.reciprocal(
                        m2c[:], ssq[:]), waits=[("VE", w4_)])
                    s_rstd = op("scalar", "SE", lambda e: e.activation(
                        rstd[:], m2c[:], AF.Sqrt), waits=[("VE", v_stats)])
                    v_nmr = op("vector", "VE", lambda e: e.scalar_tensor_tensor(
                        nmr[:], ssum[:], -1.0, rstd[:], ALU.mult, ALU.mult),
                        waits=[("SE", s_rstd), ("VE", v_stats)])

                    d_x2 = {}
                    s_xn = {}
                    for tc in range(N_XT):
                        rows = 128 if tc < 16 else 6
                        b = tc % 4
                        buf = tc % 2
                        d_x2[tc] = dma(f"cX{b}", xt[b][0:rows, :],
                                       x_tm_d[128 * tc:128 * tc + rows, :],
                                       waits=[("SE", s_xn.get(tc - 4)),
                                              ("SE", s_sq[N_XT - 1]),
                                              ("VE", v_sm[N_XT - 1])])
                        s_w = [(f"cX{b}", d_x2[tc]), ("VE", v_nmr),
                               ("SE", s_xn.get(tc - 1))]
                        if tc >= 2:
                            s_w.append(("TE", tr_done[tc - 2]))
                        s_xn[tc] = op("scalar", "SE",
                            lambda e, tc=tc, rows=rows, b=b, buf=buf: e.activation(
                                xnt[0:rows, buf, :], xt[b][0:rows, :], AF.Identity,
                                scale=rstd[0:rows, tc:tc + 1],
                                bias=nmr[0:rows, tc:tc + 1]),
                            waits=s_w)
                        for j in range(DK):
                            w_list = []
                            if j == 0:
                                w_list = [("SE", s_xn[tc])]
                                if tc == 0:
                                    w_list.append(("cA", v_c0))
                                if tc >= 2:
                                    w_list.append(("VE", v_xn_evac[tc - 2]))
                            tr_done[tc] = op("tensor", "TE",
                                lambda e, rows=rows, buf=buf, j=j: e.transpose(
                                    ptr[buf][:, 128 * j:128 * j + rows],
                                    xnt[0:rows, buf, 128 * j:128 * (j + 1)],
                                    ident[0:rows, 0:rows]),
                                waits=w_list)
                        v_xn_evac[tc] = op("vector", "VE",
                            lambda e, tc=tc, rows=rows, buf=buf: e.tensor_copy(
                                xn[:, :, 128 * tc:128 * tc + rows],
                                ptr[buf][:].rearrange("p (j r) -> p j r", j=DK)[:, :, 0:rows]),
                            waits=[("TE", tr_done[tc])])
                        if tc < 16:
                            emit_gamma(tc, tc)
                PH["ln"] = len(steps)

                ln_end = dict(C)
                with contextlib.ExitStack() as es_v:
                    wvf = [es_v.enter_context(nc.sbuf_tensor(f"wvf{i}", [128, D], F32))
                           for i in range(2)]
                    wv_b = es_v.enter_context(nc.sbuf_tensor("wv_b", [128, DK, D], BF16))

                    v_wvc = {}
                    for kc in range(DK):
                        ch = f"cWV{kc % 2}"
                        xw = [("VE", v_wvc.get(kc - 2))]
                        if kc < 2:
                            xw += [("SE", ln_end["SE"]), ("TE", ln_end["TE"]),
                                   ("VE", ln_end["VE"])]
                        d_wv = dma(ch, wvf[kc % 2][:],
                                   wqkv_d[128 * kc:128 * kc + 128, 2048:3072],
                                   waits=xw)
                        v_wvc[kc] = op("vector", "VE", lambda e, kc=kc: e.tensor_copy(
                            wv_b[:, kc, :], wvf[kc % 2][:]), waits=[(ch, d_wv)])

                    for tc in range(N_XT):
                        rows = 128 if tc < 16 else 6
                        buf = tc % 2
                        t_last = None
                        for kc in range(DK):
                            for hh in range(2):
                                wl = []
                                if kc == 0 and hh == 0:
                                    wl = [("VE", v_xn_evac[tc]), ("VE", v_wvc[DK - 1])]
                                    if tc >= 2:
                                        wl.append(("VE", v_vevac[tc - 2]))
                                t_last = op("tensor", "TE",
                                    lambda e, tc=tc, rows=rows, buf=buf, kc=kc, hh=hh:
                                    e.matmul(
                                        pv[buf][0:rows, 512 * hh:512 * hh + 512],
                                        xn[:, kc, 128 * tc:128 * tc + rows],
                                        wv_b[:, kc, 512 * hh:512 * hh + 512],
                                        start=(kc == 0), stop=(kc == DK - 1),
                                        skip_group_check=True),
                                    waits=wl)
                        v_vevac[tc] = op("vector", "VE",
                            lambda e, tc=tc, rows=rows, buf=buf: e.tensor_copy(
                                v_tm[0:rows, tc, :], pv[buf][0:rows, :]),
                            waits=[("TE", t_last)])
            PH["v"] = len(steps)

            v_end = dict(C)
            v_aevac = {}
            v_zevac = {}
            with contextlib.ExitStack() as es_h:
                masks = es_h.enter_context(
                    nc.sbuf_tensor("sb_masks", [128, 17 * MASKW], BF16))
                whf = [es_h.enter_context(nc.sbuf_tensor(f"whf{i}", [128, D], F32))
                       for i in range(2)]
                whb = [es_h.enter_context(nc.sbuf_tensor(f"whb{i}", [128, D], BF16))
                       for i in range(2)]
                wT_sb = es_h.enter_context(nc.sbuf_tensor("wT_sb", [128, D], BF16))
                wqp_sb = [es_h.enter_context(
                    nc.sbuf_tensor(f"wqp{i}", [128, DK * FD], BF16)) for i in range(2)]
                e_scr = [es_h.enter_context(nc.sbuf_tensor(f"e_scr{i}", [64, 512], F32))
                         for i in range(2)]
                r_scr = [es_h.enter_context(nc.sbuf_tensor(f"r_scr{i}", [64, 512], F32))
                         for i in range(2)]
                q_phi = es_h.enter_context(nc.sbuf_tensor("q_phi", [64, P], BF16))
                k_phi = es_h.enter_context(nc.sbuf_tensor("k_phi", [64, P], BF16))
                s_bf = [es_h.enter_context(nc.sbuf_tensor(f"s_bf{i}", [128, MASKW], BF16))
                        for i in range(17)]
                z_all = es_h.enter_context(nc.sbuf_tensor("z_all", [NH, T], F32))
                z2 = es_h.enter_context(nc.sbuf_tensor("z2", [NH, T], F32))
                zinv_b = es_h.enter_context(nc.sbuf_tensor("zinv_b", [NH, T], BF16))
                z_stage = [es_h.enter_context(nc.sbuf_tensor(f"z_st{i}", [1, 512], F32))
                           for i in range(2)]
                pphi = [es_h.enter_context(nc.psum_tensor(f"pphi{i}", [128, 512], F32))
                        for i in range(2)]
                ptr_b = es_h.enter_context(nc.psum_tensor("ptr_b", [128, D], BF16))
                pwq = es_h.enter_context(nc.psum_tensor("pwq", [128, 512], F32))
                pA = [es_h.enter_context(nc.psum_tensor(f"pA{i}", [128, 512], F32))
                      for i in range(2)]
                pZ = [es_h.enter_context(nc.psum_tensor(f"pZ{i}", [128, 512], F32))
                      for i in range(2)]

                d_masks = dma("cM", masks[:], masks_d,
                              waits=[("TE", v_end["TE"]), ("VE", v_end["VE"])])

                v_invg1 = emit_invg(0, gam[:, 0:16], pA[0], v_gabs[15])

                d_zs = {}
                v_wcast = {}
                v_wT = {}
                v_wqp = {}
                v_phislab = {}
                s_exp = {}
                s_relu = {}
                v_sbf = {}
                t_sbf_free = {}
                t_trlast = {}
                hseq = [16]

                for h in range(NH):
                    pend = [(1, 2 * h), (1, 2 * h + 1), (0, 16 + 2 * h), (0, 17 + 2 * h)]

                    def emit_piece(h=h, pend=pend):
                        if not pend:
                            return
                        kind, i = pend.pop(0)
                        s = hseq[0]
                        hseq[0] += 1
                        extra = []
                        if s < 19:
                            extra = [("SE", ln_end["SE"]), ("TE", v_end["TE"]),
                                     ("VE", v_end["VE"])]
                        if kind == 1:
                            emit_quant(s, i, v_invg1, extra)
                        else:
                            emit_gamma(s, i, extra)

                    def emit_wload(hh2):
                        for side in range(2):
                            coff = side * D + 128 * hh2
                            ch = f"cWH{side}"
                            xw = [("VE", v_wcast.get((hh2 - 1, side)))]
                            if hh2 == 0:
                                xw += [("TE", v_end["TE"]), ("VE", v_end["VE"])]
                            d_w = dma(ch,
                                      whf[side][:].rearrange("p (k c) -> p k c", k=8),
                                      bass.AP(wqkv_d.tensor, coff,
                                              [[3 * D, 128], [128 * 3 * D, 8],
                                               [1, 128]]),
                                      waits=xw)
                            v_wcast[(hh2, side)] = op("vector", "VE",
                                lambda e, side=side: e.tensor_copy(
                                    whb[side][:], whf[side][:]),
                                waits=[(ch, d_w),
                                       ("TE", t_trlast.get((hh2 - 1, side)))])

                    if h == 0:
                        emit_wload(0)

                    for side in range(2):
                        t_tr = None
                        for kcb in range(DK):
                            wl = []
                            if kcb == 0:
                                prev_wt = v_wT.get((h, side - 1)) if side == 1 else \
                                    v_wT.get((h - 1, 1))
                                wl = [("VE", v_wcast[(h, side)]), ("VE", prev_wt)]
                                if h == 0 and side == 0:
                                    wl.append(("VE", v_invg1))
                            t_tr = op("tensor", "TE",
                                lambda e, side=side, kcb=kcb: e.transpose(
                                    ptr_b[:, 128 * kcb:128 * (kcb + 1)],
                                    whb[side][:, 128 * kcb:128 * (kcb + 1)],
                                    ident[:]),
                                waits=wl)
                        v_wT[(h, side)] = op("vector", "VE",
                            lambda e: e.tensor_copy(wT_sb[:], ptr_b[:]),
                            waits=[("TE", t_tr)])
                        t_wq = None
                        for kcb in range(DK):
                            wl = []
                            if kcb == 0:
                                prev_wq = v_wqp.get((h, side - 1)) if side == 1 else \
                                    v_wqp.get((h - 1, 1))
                                wl = [("VE", v_wT[(h, side)]), ("VE", prev_wq),
                                      ("VE", v_sbf.get((h - 1, 14)))]
                            t_wq = op("tensor", "TE",
                                lambda e, kcb=kcb: e.matmul(
                                    pwq[:, FD * kcb:FD * (kcb + 1)],
                                    wT_sb[:, 128 * kcb:128 * (kcb + 1)], proj_b[:],
                                    start=True, stop=True, skip_group_check=True),
                                waits=wl)
                        t_trlast[(h, side)] = t_tr
                        v_wqp[(h, side)] = op("vector", "VE",
                            lambda e, side=side: e.tensor_copy(
                                wqp_sb[side][:], pwq[:]),
                            waits=[("TE", t_wq)])
                    if h + 1 < NH:
                        emit_wload(h + 1)

                    for side, phi in enumerate([q_phi, k_phi]):
                        for si, (sb0, sw) in enumerate(SLABS5):
                            buf = si % 2
                            t_last = None
                            for kc in range(DK):
                                wl = []
                                if kc == 0:
                                    if si >= 2:
                                        pk = (h, side, si - 2)
                                    elif side == 1:
                                        pk = (h, 0, 4 - si)
                                    else:
                                        pk = (h - 1, 1, 4 - si)
                                    wl = [("VE", v_wqp[(h, side)]),
                                          ("SE", s_relu.get(pk)),
                                          ("VE", v_phislab.get(pk))]
                                    if side == 0 and si < 2:
                                        wl.append(("VE", v_sbf.get((h - 1, 15 + si))))
                                t_last = op("tensor", "TE",
                                    lambda e, buf=buf, kc=kc, sb0=sb0, sw=sw,
                                           side=side: e.matmul(
                                        pphi[buf][0:64, 0:sw],
                                        wqp_sb[side][:, FD * kc:FD * (kc + 1)],
                                        xn[:, kc, sb0:sb0 + sw],
                                        start=(kc == 0), stop=(kc == DK - 1),
                                        skip_group_check=True),
                                    waits=wl)
                            s_exp[(h, side, si)] = op("scalar", "SE",
                                lambda e, buf=buf, sw=sw: e.activation(
                                    e_scr[buf][:, 0:sw], pphi[buf][0:64, 0:sw], AF.Exp),
                                waits=[("TE", t_last)])
                            s_relu[(h, side, si)] = op("scalar", "SE",
                                lambda e, buf=buf, sw=sw: e.activation(
                                    r_scr[buf][:, 0:sw], pphi[buf][0:64, 0:sw], AF.Relu),
                                waits=[("SE", s_exp[(h, side, si)])])
                            v_phislab[(h, side, si)] = op("vector", "VE",
                                lambda e, buf=buf, sb0=sb0, sw=sw, phi=phi:
                                e.scalar_tensor_tensor(
                                    phi[:, sb0:sb0 + sw], e_scr[buf][:, 0:sw], 1.0,
                                    r_scr[buf][:, 0:sw], ALU.min, ALU.add),
                                waits=[("SE", s_relu[(h, side, si)]),
                                       ("VE", v_phislab.get((h, side, si - 2)))])
                            emit_piece()

                    op("vector", "VE", lambda e: e.tensor_tensor(
                        k_phi[:, 0:3], k_phi[:, 0:3], emask[:, 0:3], ALU.mult),
                        waits=[("VE", v_phislab[(h, 1, 4)])], inc=0)
                    v_kedge = op("vector", "VE", lambda e: e.tensor_tensor(
                        k_phi[:, 2051:2054], k_phi[:, 2051:2054], emask[:, 3:6],
                        ALU.mult))

                    pSp = [pphi[0], pphi[1], pwq]
                    first_a = [True] * 4
                    first_z = [True] * 4
                    watermark = [0] * 4
                    t_band = {}
                    for kc in range(17):
                        j0, w = KCS[kc]
                        tb, te = _bwin(kc)
                        tw = te - tb
                        buf = kc % 3
                        wl = [("VE", v_sbf.get((h, kc - 3)))]
                        if kc == 0:
                            wl = [("VE", v_kedge), ("cM", d_masks),
                                  ("SE", s_relu[(h, 1, 4)])]
                        t_s = op("tensor", "TE",
                            lambda e, buf=buf, j0=j0, w=w, tb=tb, tw=tw: e.matmul(
                                pSp[buf][0:w, 0:tw], k_phi[:, j0:j0 + w],
                                q_phi[:, tb:tb + tw],
                                start=True, stop=True, skip_group_check=True),
                            waits=wl)
                        sbf_prev = t_sbf_free.get((h - 1, kc))
                        v_sbf[(h, kc)] = op("vector", "VE",
                            lambda e, buf=buf, w=w, tw=tw, kc=kc: e.tensor_tensor(
                                s_bf[kc][0:w, 0:tw], pSp[buf][0:w, 0:tw],
                                masks[0:w, MASKW * kc:MASKW * kc + tw], ALU.mult),
                            waits=[("TE", t_s), ("TE", sbf_prev)])
                    for kc in range(17):
                        j0, w = KCS[kc]
                        tb, te = _bwin(kc)
                        tw = te - tb
                        buf = kc
                        for u in range(4):
                            for (k2, a, b_, pa) in SLAB_PARTS[u]:
                                if k2 != kc:
                                    continue
                                wl_a = [("VE", v_sbf[(h, kc)])]
                                if h == 0 and first_a[u]:
                                    wl_a.append(("VE", v_vevac[N_XT - 1]))
                                if first_a[u]:
                                    pkey = (h, u - 2) if u >= 2 else (h - 1, u + 2)
                                    wl_a.append(("VE", v_aevac.get(pkey)))
                                    wl_a.append(("VE", v_zevac.get(pkey)))
                                    if h == 0 and u == 0:
                                        wl_a.append(("VE", v_invg1))
                                wm = watermark[u]
                                pieces = []
                                if pa < wm:
                                    pieces.append((a, a + min(wm - pa, b_ - a), pa, False))
                                if pa + (b_ - a) > wm:
                                    a2 = a + max(0, wm - pa)
                                    pieces.append((a2, b_, pa + (a2 - a), first_a[u]))
                                for (aa, bb, pp, st_flag) in pieces:
                                    op("tensor", "TE",
                                        lambda e, u=u, aa=aa, bb=bb, pp=pp, buf=buf,
                                               w=w, kc=kc, h=h, fa=st_flag: e.matmul(
                                            pA[u % 2][:, pp:pp + (bb - aa)],
                                            v_tm[0:w, kc, 128 * h:128 * h + 128],
                                            s_bf[kc][0:w, aa:bb],
                                            start=fa, stop=True,
                                            skip_group_check=True),
                                        waits=wl_a)
                                    wl_a = []
                                    t_z = op("tensor", "TE",
                                        lambda e, u=u, aa=aa, bb=bb, pp=pp, buf=buf,
                                               w=w, kc=kc, fz=st_flag: e.matmul(
                                            pZ[u % 2][:, pp:pp + (bb - aa)],
                                            ones_mb[0:w, :], s_bf[kc][0:w, aa:bb],
                                            start=fz, stop=True,
                                            skip_group_check=True))
                                watermark[u] = max(wm, pa + (b_ - a))
                                first_a[u] = False
                                first_z[u] = False
                                t_band[u] = t_z
                                t_sbf_free[(h, kc)] = t_z
                        for u in range(4):
                            if SLAB_LAST_KC[u] != kc:
                                continue
                            idx = h * 4 + u
                            v_aevac[(h, u)] = op("vector", "VE",
                                lambda e, u=u, h=h: e.tensor_copy(
                                    attn_r[:, h, 512 * u:512 * u + 512],
                                    pA[u % 2][:, 0:512]),
                                waits=[("TE", t_band[u])])
                            v_zevac[(h, u)] = op("vector", "VE",
                                lambda e, u=u: e.tensor_copy(
                                    z_stage[u % 2][0:1, :], pZ[u % 2][0:1, 0:512]),
                                waits=[(f"cZS{u % 2}", d_zs.get(idx - 2))])
                            d_zs[idx] = dma(f"cZS{u % 2}",
                                            zdram_f[h:h + 1, 512 * u:512 * u + 512],
                                            z_stage[u % 2][0:1, :],
                                            waits=[("VE", v_zevac[(h, u)])])

                d_zall = dma("cZA", z_all[:], zdram_f,
                             waits=[("cZS0", d_zs[NH * 4 - 2]),
                                    ("cZS1", d_zs[NH * 4 - 1])])
                vz1 = op("vector", "VE", lambda e: e.tensor_scalar(
                    z_all[:], z_all[:], EPS_Z, None, ALU.add),
                    waits=[("cZA", d_zall)])
                vz2 = op("vector", "VE", lambda e: e.reciprocal(z2[:], z_all[:]),
                         waits=[("VE", vz1)])
                v_zi = op("vector", "VE", lambda e: e.tensor_copy(zinv_b[:], z2[:]),
                          waits=[("VE", vz2)])
                d_z1 = dma("cZI", zdram, zinv_b[:], waits=[("VE", v_zi)])
                v_gabs2_last = v_gabs[31]
                sq_scr0 = C.get("cQW0", 0)
                sq_scr1 = C.get("cQW1", 0)
                d_z1_val = C["cZI"]
            h_end = dict(C)
            PH["heads"] = len(steps)

        with contextlib.ExitStack() as es_f:
            h_sb = es_f.enter_context(nc.sbuf_tensor("h_sb", [128, MF, 512], BF16))
            g_fm = es_f.enter_context(nc.sbuf_tensor("g_fm", [128, DK, 512], BF16))
            g_tm = es_f.enter_context(nc.sbuf_tensor("g_tm", [128, N_TC, D], BF16))
            wq1b = [es_f.enter_context(nc.sbuf_tensor(f"wq1b{i}", [128, 2048], BF16))
                    for i in range(2)]
            w2tb = [es_f.enter_context(nc.sbuf_tensor(f"w2tb{i}", [128, 2048], BF16))
                    for i in range(2)]
            wout_b = es_f.enter_context(nc.sbuf_tensor("wout_b", [128, DK, D], BF16))
            qb2 = [es_f.enter_context(nc.sbuf_tensor(f"qb2_{i}", [128, 2048], F32))
                   for i in range(2)]
            qo2 = [es_f.enter_context(nc.sbuf_tensor(f"qo2_{i}", [128, 2048], BF16))
                   for i in range(2)]
            zr2 = es_f.enter_context(nc.sbuf_tensor("zr2", [1, T], BF16))
            xtf = qb2

            with contextlib.ExitStack() as es_p1:
                ph = [es_p1.enter_context(nc.psum_tensor(f"ph{i}", [128, 512], F32))
                      for i in range(2)]
                pg = [es_p1.enter_context(nc.psum_tensor(f"pg{i}", [128, 512], F32))
                      for i in range(2)]
                pgt = [es_p1.enter_context(nc.psum_tensor(f"pgt{i}", [128, D], BF16))
                       for i in range(2)]
                psf = [es_p1.enter_context(nc.psum_tensor(f"psf{i}", [128, 512], F32))
                       for i in range(2)]

                hw = [("TE", h_end["TE"]), ("VE", h_end["VE"]), ("SE", h_end["SE"])]

                v_woc = {}
                for dc in range(DK):
                    ch = f"cWO{dc % 2}"
                    xw = [("VE", v_woc.get(dc - 2))]
                    if dc < 2:
                        xw += hw
                    d_wo = dma(ch, qb2[dc % 2][:, 0:1024],
                               wout_d[128 * dc:128 * dc + 128, :], waits=xw)
                    v_woc[dc] = op("vector", "VE", lambda e, dc=dc: e.tensor_copy(
                        wout_b[:, dc, :], qb2[dc % 2][:, 0:1024]),
                        waits=[(ch, d_wo)])
                v_wo_last = v_woc[DK - 1]

                v_invg2 = emit_invg(1, gam[:, 16:32], ph[0], v_gabs2_last)

                q2_dma = {}
                q2_cons = {}
                q2_wr = {}
                n2 = [0]

                def emit_quant2():
                    i2 = n2[0]
                    if i2 >= 16:
                        return
                    n2[0] += 1
                    i = 16 + i2
                    b = i2 % 2
                    xw = [q2_cons.get(i2 - 2)]
                    if i2 < 2:
                        xw += [("VE", v_woc[6 + b])]
                        xw += hw
                    q2_dma[i2] = dma(f"cQ2{b}", qb2[b][:], qpiece_in(i),
                                     waits=[x for x in xw if x], q="gpsimd")
                    p1 = op("scalar", "SE", lambda e, b=b: e.activation(
                        qb2[b][:], qb2[b][:], AF.Identity,
                        scale=invg[:, 1:2], bias=magic_c[:]),
                        waits=[(f"cQ2{b}", q2_dma[i2]), ("VE", v_invg2)])
                    p2 = op("vector", "VE", lambda e, b=b: e.tensor_scalar(
                        qb2[b][:], qb2[b][:], MAGIC, 1.0, ALU.subtract, ALU.min),
                        waits=[("SE", p1)])
                    p3 = op("vector", "VE", lambda e, b=b: e.tensor_scalar(
                        qo2[b][:], qb2[b][:], -1.0, None, ALU.max),
                        waits=[(f"cQ2W{b}", q2_wr.get(i2 - 2)), ("VE", p2)])
                    for (o, s) in qpiece_out(i, qo2[b]):
                        q2_wr[i2] = dma(f"cQ2W{b}", o, s, waits=[("VE", p3)],
                                        q="gpsimd")
                    q2_cons[i2] = ("VE", p3)

                v_div = {}
                d_z2d = {}
                for h2 in range(NH):
                    xw = [("cZI", d_z1_val), ("VE", v_div.get(4 * h2 - 1))]
                    if h2 == 0:
                        xw += hw
                    d_z2d[h2] = dma("cZR", zr2[0:1, :],
                                    bass.AP(zdram.tensor, h2 * T, [[T, 1], [1, T]]),
                                    waits=xw)
                    for u in range(4):
                        idx = h2 * 4 + u
                        buf = idx % 2
                        wl = [("cZR", d_z2d[h2])]
                        if idx >= 2:
                            wl.append(("VE", v_div[idx - 2]))
                        t_r = op("tensor", "TE", lambda e, buf=buf, u=u: e.matmul(
                            psf[buf][:, 0:512], ones_rb[:],
                            zr2[0:1, 512 * u:512 * u + 512],
                            start=True, stop=True, skip_group_check=True),
                            waits=wl)
                        v_div[idx] = op("vector", "VE",
                            lambda e, buf=buf, h2=h2, u=u: e.tensor_tensor(
                                attn_r[:, h2, 512 * u:512 * u + 512],
                                attn_r[:, h2, 512 * u:512 * u + 512],
                                psf[buf][:, 0:512], ALU.mult),
                            waits=[("TE", t_r)])
                v_div_last = v_div[NH * 4 - 1]

                h_ev = {}
                h_eng = {}
                t_wq1 = {}
                t_w2_half = {}
                t_ffn2_last = {}
                s_gev = {}
                v_gt = {}
                v_stt = {}
                v_ob = {}
                d_out = {}

                def emit_final(tc):
                    buf = tc % 2
                    xw = [("VE", v_ob.get(tc - 2))]
                    if tc < 2:
                        c2 = q2_cons.get(14 + buf)
                        if c2:
                            xw.append(c2)
                        xw.append((f"cQ2{buf}", q2_dma.get(14 + buf)))
                    d_x = dma(f"cXF{buf}", xtf[buf][:, 0:1024],
                              x_tm_d[HALF + 128 * tc:HALF + 128 * tc + 128, :],
                              waits=xw)
                    for hh in range(2):
                        t_last = None
                        for dc in range(DK):
                            wl = []
                            if dc == 0:
                                wl = [("VE", v_gt[tc]), ("VE", v_consts)]
                                if tc == 0 and hh == 0:
                                    wl += [("VE", v_div_last), ("VE", v_wo_last)]
                                pv_ = v_stt.get((tc - 1, hh))
                                if tc == 0:
                                    pv_ = v_div_last
                                wl.append(("VE", pv_))
                            t_last = op("tensor", "TE",
                                lambda e, hh=hh, dc=dc, tc=tc: e.matmul(
                                    psf[hh][:, 0:512],
                                    attn_r[:, dc, 128 * tc:128 * (tc + 1)],
                                    wout_b[:, dc, 512 * hh:512 * hh + 512],
                                    start=(dc == 0), stop=False,
                                    skip_group_check=True),
                                waits=wl)
                        t_last = op("tensor", "TE",
                            lambda e, hh=hh: e.matmul(
                                psf[hh][:, 0:512], ones_rb[:],
                                bout_b[0:1, 512 * hh:512 * hh + 512],
                                start=False, stop=True, skip_group_check=True))
                        v_stt[(tc, hh)] = op("vector", "VE",
                            lambda e, buf=buf, tc=tc, hh=hh: e.scalar_tensor_tensor(
                                qb2_out[buf][:, 512 * hh:512 * hh + 512],
                                psf[hh][:, 0:512], 0.0,
                                g_tm[:, tc, 512 * hh:512 * hh + 512],
                                ALU.add, ALU.mult),
                            waits=[("TE", t_last), (f"cXF{buf}", d_x),
                                   (f"cOB{buf}", d_out.get(tc - 2))])
                    v_ob[tc] = op("vector", "VE", lambda e, buf=buf: e.tensor_tensor(
                        qb2_out[buf][:], qb2_out[buf][:], xtf[buf][:, 0:1024],
                        ALU.add),
                        waits=[("VE", v_stt[(tc, 0)]), ("VE", v_stt[(tc, 1)])])
                    d_out[tc] = dma(f"cOB{buf}", out_d[128 * tc:128 * (tc + 1), :],
                                    qb2_out[buf][:], waits=[("VE", v_ob[tc])])

                qb2_out = [bass.AP(qb2[0], 1024, [[2048, 128], [1, 1024]]),
                           bass.AP(qb2[1], 1024, [[2048, 128], [1, 1024]])]

                for u in range(4):
                    xs = HALF + 512 * u
                    for mp in range(16):
                        wv = []
                        if (u, mp - 2) in t_wq1:
                            wv.append(("TE", t_wq1[(u, mp - 2)]))
                        elif u > 0 and mp < 2:
                            wv.append(("TE", t_wq1[(u - 1, 14 + mp)]))
                        if u == 0 and mp < 2:
                            wv.append(("cQW0", sq_scr0))
                            wv.append(("cQW1", sq_scr1))
                            wv += hw
                        d_wq1 = dma(
                            f"cW1{mp % 2}", wq1b[mp % 2][:],
                            bass.AP(wq1_sc.tensor, mp * (128 * 8 * 256),
                                    [[8 * 256, 128], [1, 8 * 256]]),
                            waits=wv, q="sync")
                        for mo in range(2):
                            m = 2 * mp + mo
                            t_last = None
                            for kc in range(DK):
                                wl = []
                                if kc == 0:
                                    wl = [(f"cW1{mp % 2}", d_wq1)]
                                    if m >= 2:
                                        wl.append((h_eng[(u, m - 2)], h_ev[(u, m - 2)]))
                                    elif u > 0:
                                        wl.append((h_eng[(u - 1, 30 + m)],
                                                   h_ev[(u - 1, 30 + m)]))
                                t_last = op("tensor", "TE",
                                    lambda e, m=m, kc=kc, xs=xs, mp=mp: e.matmul(
                                        ph[m % 2][:, 0:512],
                                        wq1b[mp % 2][:, 256 * kc + 128 * (m - 2 * mp):
                                                      256 * kc + 128 * (m - 2 * mp) + 128],
                                        xn[:, kc, xs:xs + 512],
                                        start=(kc == 0), stop=(kc == DK - 1),
                                        skip_group_check=True),
                                    waits=wl)
                            t_wq1[(u, mp)] = t_last
                            if m % 2 == 0:
                                h_eng[(u, m)] = "VE"
                                h_ev[(u, m)] = op("vector", "VE",
                                    lambda e, m=m: e.tensor_scalar(
                                        h_sb[:, m, :], ph[m % 2][:, 0:512],
                                        b1t[:, m:m + 1], 0.0, ALU.add, ALU.max),
                                    waits=[("TE", t_last)])
                            else:
                                h_eng[(u, m)] = "SE"
                                h_ev[(u, m)] = op("scalar", "SE",
                                    lambda e, m=m: e.activation(
                                        h_sb[:, m, :], ph[m % 2][:, 0:512], AF.Relu,
                                        bias=b1t[:, m:m + 1]),
                                    waits=[("TE", t_last)])
                        if u == 0:
                            emit_quant2()
                    for dc in range(DK):
                        t_last = None
                        for half in range(2):
                            wv = [("TE", t_w2_half.get(half))]
                            if u == 0 and dc == 0:
                                wv.append((f"cQ2W0", q2_wr.get(14)))
                                wv.append((f"cQ2W1", q2_wr.get(15)))
                                wv += hw
                            d_w2t = dma(
                                f"cW2{half}", w2tb[half][:],
                                bass.AP(wq2_sc.tensor,
                                        dc * (128 * 32 * 128) + half * 2048,
                                        [[32 * 128, 128], [1, 2048]]),
                                waits=wv, q="sync")
                            for kk in range(16):
                                kc2 = 16 * half + kk
                                wl = []
                                if kk == 0:
                                    wl = [(f"cW2{half}", d_w2t)]
                                    if half == 0:
                                        wl += [("VE", h_ev[(u, 30)]),
                                               ("SE", h_ev[(u, 31)]),
                                               ("SE", s_gev.get((u, dc - 2)))]
                                t_last = op("tensor", "TE",
                                    lambda e, dc=dc, kc2=kc2, half=half, kk=kk: e.matmul(
                                        pg[dc % 2][:, 0:512],
                                        w2tb[half][:, 128 * kk:128 * kk + 128],
                                        h_sb[:, kc2, :],
                                        start=(kc2 == 0), stop=(kc2 == MF - 1),
                                        skip_group_check=True),
                                    waits=wl)
                            t_w2_half[half] = t_last
                        t_ffn2_last[(u, dc)] = t_last
                        s_gev[(u, dc)] = op("scalar", "SE",
                            lambda e, dc=dc: e.activation(
                                g_fm[:, dc, :], pg[dc % 2][:, 0:512], AF.Silu,
                                bias=b2t[:, dc:dc + 1]),
                            waits=[("TE", t_last)])
                    for tt in range(4):
                        tc = 4 * u + tt
                        buf = tt % 2
                        t_tr = None
                        for dc2 in range(DK):
                            wl = []
                            if dc2 == 0:
                                wl = [("SE", s_gev[(u, DK - 1)])]
                                if tc >= 2:
                                    wl.append(("VE", v_gt.get(tc - 2)))
                            t_tr = op("tensor", "TE",
                                lambda e, buf=buf, dc2=dc2, tt=tt: e.transpose(
                                    pgt[buf][:, 128 * dc2:128 * (dc2 + 1)],
                                    g_fm[0:128, dc2, 128 * tt:128 * (tt + 1)],
                                    ident[:]),
                                waits=wl)
                        v_gt[tc] = op("vector", "VE",
                            lambda e, buf=buf, tc=tc: e.tensor_copy(
                                g_tm[:, tc, :], pgt[buf][:]),
                            waits=[("TE", t_tr)])
                        emit_final(tc)

                fin0 = C.get("cOB0", 0)
                fin1 = C.get("cOB1", 0)

                def f_fin(e, fin0=fin0, fin1=fin1):
                    e.wait_ge(SEM["cOB0"], fin0)
                    e.wait_ge(SEM["cOB1"], fin1)
                steps.append(("sync", f_fin, 0))

        PH["all"] = len(steps)
        nc._phase_marks = PH
        nc._steps_meta = [(s[0], s[2]) for s in steps]
        block = es.enter_context(nc.Block())
        use_steps = steps if cut is None else steps[:cut]

        def mk(section_name):
            def _(eng):
                for st in use_steps:
                    if st[0] == section_name:
                        st[1](eng)
            return _

        block.sync(mk("sync"))
        block.tensor(mk("tensor"))
        block.vector(mk("vector"))
        block.scalar(mk("scalar"))
        block.gpsimd(mk("gpsimd"))

    return nc


_NC_CACHE = None


def _get_nc():
    global _NC_CACHE
    if _NC_CACHE is None:
        _NC_CACHE = build_nc()
    return _NC_CACHE


def _host_masks():
    m = np.zeros((128, 17 * MASKW), dtype=np.float32)
    for kc in range(17):
        j0, w = KCS[kc]
        tb, te = _bwin(kc)
        jj = np.arange(w)[:, None]
        tt = np.arange(te - tb)[None, :]
        m[0:w, MASKW * kc:MASKW * kc + (te - tb)] = (
            np.abs((tb + tt) - (j0 + jj)) <= HALF).astype(np.float32)
    return m.astype(ml_dtypes.bfloat16)


def kernel(L0_attn_x, L0_attn_w_qkv, L0_attn_proj, L0_attn_w_out, L0_attn_b_out,
           L0_ffn_W1, L0_ffn_b1, L0_ffn_W2, L0_ffn_b2):
    x = np.asarray(L0_attn_x, dtype=np.float32)
    wqkv = np.ascontiguousarray(np.asarray(L0_attn_w_qkv, dtype=np.float32))
    proj = np.ascontiguousarray(np.asarray(L0_attn_proj, dtype=np.float32))
    wout = np.ascontiguousarray(np.asarray(L0_attn_w_out, dtype=np.float32))
    bout = np.ascontiguousarray(np.asarray(L0_attn_b_out, dtype=np.float32).reshape(1, D))
    w1 = np.asarray(L0_ffn_W1, dtype=np.float32)
    b1 = np.asarray(L0_ffn_b1, dtype=np.float32)
    w2 = np.asarray(L0_ffn_W2, dtype=np.float32)
    b2 = np.asarray(L0_ffn_b2, dtype=np.float32)

    w1t = np.ascontiguousarray(w1.T)
    w2t = np.ascontiguousarray(w2.T)
    b1t = np.ascontiguousarray(b1.reshape(MF, 128).T)
    b2t = np.ascontiguousarray(b2.reshape(DK, 128).T)
    ident = np.eye(128, dtype=ml_dtypes.bfloat16)
    masks = _host_masks()

    in_maps = []
    for c in range(8):
        seq, t0 = c // 4, (c % 4) * T
        sl = np.zeros((P, D), dtype=np.float32)
        lo, hi = max(0, t0 - HALF), min(8192, t0 + T + HALF)
        sl[lo - (t0 - HALF):lo - (t0 - HALF) + (hi - lo)] = x[seq, lo:hi]
        em = np.ones((64, 6), dtype=np.float32)
        if t0 == 0:
            em[:, 0:3] = 0.0
        if t0 + T == 8192:
            em[:, 3:6] = 0.0
        in_maps.append({
            "x_tm": sl, "wqkv": wqkv, "proj": proj, "wout": wout, "bout": bout,
            "w1t": w1t, "b1t": b1t, "w2t": w2t, "b2t": b2t,
            "ident": ident, "masks": masks, "emask": em,
        })

    nc = _get_nc()
    res = run_bass_kernel_spmd(nc, in_maps, core_ids=list(range(8)))
    out = np.empty((2, 8192, D), dtype=np.float32)
    for c in range(8):
        seq, t0 = c // 4, (c % 4) * T
        out[seq, t0:t0 + T] = res.results[c]["out"]
    return out


# version 08
# speedup vs baseline: 1.2475x; 1.1250x over previous
import contextlib
import numpy as np
import ml_dtypes
import concourse.bass as bass
import concourse.mybir as mybir
from concourse.bass_utils import run_bass_kernel_spmd

F32 = mybir.dt.float32
BF16 = mybir.dt.bfloat16
AF = mybir.ActivationFunctionType
ALU = mybir.AluOpType

P = 2054
T = 2048
HALF = 3
D = 1024
DK = 8
NH = 8
FD = 64
DFF = 4096
MF = 32
MAGIC = 12582912.0
EPS_LN = 1e-5
EPS_Q = 1e-5
EPS_Z = 1e-8
N_ELEM_W = float(DFF * D)
MASKW = 134
N_XT = 17
N_TC = 16

SLABS5 = [(0, 512), (512, 512), (1024, 512), (1536, 512), (2048, 6)]
KCS = [(k * 128, min(128, P - k * 128)) for k in range(17)]


def _bwin(kc):
    j0, w = KCS[kc]
    return max(HALF, j0 - HALF), min(P - HALF, j0 + w + HALF)


SLAB_PARTS = [[] for _ in range(4)]
SLAB_LAST_KC = [0] * 4
for _kc in range(17):
    _tb, _te = _bwin(_kc)
    for _u in range(4):
        _su, _eu = HALF + 512 * _u, HALF + 512 * (_u + 1)
        _lo, _hi = max(_tb, _su), min(_te, _eu)
        if _lo < _hi:
            SLAB_PARTS[_u].append((_kc, _lo - _tb, _hi - _tb, _lo - _su))
            SLAB_LAST_KC[_u] = max(SLAB_LAST_KC[_u], _kc)


def build_nc(cut=None):
    nc = bass.Bass("TRN2", target_bir_lowering=False)

    x_tm_d = nc.dram_tensor("x_tm", [P, D], F32, kind="ExternalInput").ap()
    wqkv_d = nc.dram_tensor("wqkv", [D, 3 * D], F32, kind="ExternalInput").ap()
    proj_d = nc.dram_tensor("proj", [128, FD], F32, kind="ExternalInput").ap()
    wout_d = nc.dram_tensor("wout", [D, D], F32, kind="ExternalInput").ap()
    bout_d = nc.dram_tensor("bout", [1, D], F32, kind="ExternalInput").ap()
    w1t_d = nc.dram_tensor("w1t", [D, DFF], F32, kind="ExternalInput").ap()
    b1t_d = nc.dram_tensor("b1t", [128, MF], F32, kind="ExternalInput").ap()
    w2t_d = nc.dram_tensor("w2t", [DFF, D], F32, kind="ExternalInput").ap()
    b2t_d = nc.dram_tensor("b2t", [128, DK], F32, kind="ExternalInput").ap()
    ident_d = nc.dram_tensor("ident", [128, 128], BF16, kind="ExternalInput").ap()
    masks_d = nc.dram_tensor("masks", [128, 17 * MASKW], BF16, kind="ExternalInput").ap()
    emask_d = nc.dram_tensor("emask", [64, 6], F32, kind="ExternalInput").ap()
    out_d = nc.dram_tensor("out", [T, D], F32, kind="ExternalOutput").ap()

    wq1_sc = nc.dram_tensor("wq1_sc", [16, 128, 8, 256], BF16, kind="Internal").ap()
    wq2_sc = nc.dram_tensor("wq2_sc", [8, 128, 32, 128], BF16, kind="Internal").ap()
    zdram = nc.dram_tensor("zdram", [NH, T], BF16, kind="Internal").ap()
    zdram_f = nc.dram_tensor("zdram_f", [NH, T], F32, kind="Internal").ap()

    steps = []
    PH = {}
    C = {"TE": 0, "VE": 0, "SE": 0}

    def bump(name, n=1):
        C[name] = C.get(name, 0) + n
        return C[name]

    es = contextlib.ExitStack()
    with es:
        te_s = es.enter_context(nc.semaphore("te"))
        ve_s = es.enter_context(nc.semaphore("ve"))
        se_s = es.enter_context(nc.semaphore("se"))
        SEM = {"TE": te_s, "VE": ve_s, "SE": se_s}
        sem_es = es

        def dch(name):
            if name not in SEM:
                SEM[name] = sem_es.enter_context(nc.semaphore(name))
                C[name] = 0
            return SEM[name]

        def dma(ch, out, in_, waits=(), q="sync"):
            hnd = dch(ch)
            w = [(s, v) for (s, v) in waits if v]
            val = bump(ch, 16)

            def f(e, out=out, in_=in_, w=w, hnd=hnd):
                for (s, v) in w:
                    e.wait_ge(SEM[s], v)
                e.dma_start(out=out, in_=in_).then_inc(hnd, 16)
            import traceback
            steps.append((q, f, traceback.extract_stack()[-2].lineno))
            return val

        def op(eng, sem_name, emitf, waits=(), inc=1):
            w = [(s, v) for (s, v) in waits if v]
            val = bump(sem_name, inc) if inc else C[sem_name]

            def f(e, emitf=emitf, w=w, inc=inc, sem_name=sem_name):
                for (s, v) in w:
                    e.wait_ge(SEM[s], v)
                ins = emitf(e)
                if inc:
                    ins.then_inc(SEM[sem_name], inc)
            import traceback
            steps.append((eng, f, traceback.extract_stack()[-2].lineno))
            return val

        ident = es.enter_context(nc.sbuf_tensor("sb_ident", [128, 128], BF16))
        emask = es.enter_context(nc.sbuf_tensor("sb_emask", [64, 6], F32))
        proj_f = es.enter_context(nc.sbuf_tensor("proj_f", [128, FD], F32))
        proj_b = es.enter_context(nc.sbuf_tensor("proj_b", [128, FD], BF16))
        b1t = es.enter_context(nc.sbuf_tensor("sb_b1t", [128, MF], F32))
        b2t = es.enter_context(nc.sbuf_tensor("sb_b2t", [128, DK], F32))
        bout_f = es.enter_context(nc.sbuf_tensor("bout_f", [1, D], F32))
        bout_b = es.enter_context(nc.sbuf_tensor("bout_b", [1, D], BF16))
        ones_rb = es.enter_context(nc.sbuf_tensor("ones_rb", [1, 128], BF16))
        ones_mb = es.enter_context(nc.sbuf_tensor("ones_mb", [128, 128], BF16))
        ones_cf = es.enter_context(nc.sbuf_tensor("ones_cf", [128, 1], F32))
        magic_c = es.enter_context(nc.sbuf_tensor("magic_c", [128, 1], F32))
        ones_rf = es.enter_context(nc.sbuf_tensor("ones_rf", [1, 128], F32))
        gam = es.enter_context(nc.sbuf_tensor("gam", [128, 32], F32))
        gc2 = es.enter_context(nc.sbuf_tensor("gc2", [128, 2], F32))
        grow = es.enter_context(nc.sbuf_tensor("grow", [1, 2], F32))
        gtmp = es.enter_context(nc.sbuf_tensor("gtmp", [128, 2], F32))
        invg = es.enter_context(nc.sbuf_tensor("invg", [128, 2], F32))

        xn = es.enter_context(nc.sbuf_tensor("xn", [128, DK, P], BF16))
        attn_r = es.enter_context(nc.sbuf_tensor("attn_r", [128, NH, T], BF16))

        v_c0 = 0
        for (dst, src) in [(ident[:], ident_d), (emask[:], emask_d),
                           (proj_f[:], proj_d), (b1t[:], b1t_d), (b2t[:], b2t_d),
                           (bout_f[:], bout_d)]:
            v_c0 = dma("cA", dst, src)
        op("vector", "VE", lambda e: e.memset(ones_rb[:], 1.0), inc=0)
        op("vector", "VE", lambda e: e.memset(ones_mb[:], 1.0), inc=0)
        op("vector", "VE", lambda e: e.memset(ones_cf[:], 1.0), inc=0)
        op("vector", "VE", lambda e: e.memset(magic_c[:], MAGIC), inc=0)
        op("vector", "VE", lambda e: e.memset(ones_rf[:], 1.0), inc=0)
        op("vector", "VE", lambda e: e.tensor_copy(proj_b[:], proj_f[:]),
           waits=[("cA", v_c0)], inc=0)
        v_consts = op("vector", "VE", lambda e: e.tensor_copy(bout_b[:], bout_f[:]),
                      waits=[("cA", v_c0)])

        def qpiece_in(i):
            if i < 16:
                kc, hh = i // 2, i % 2
                return w1t_d[128 * kc:128 * kc + 128, 2048 * hh:2048 * hh + 2048]
            j = i - 16
            return bass.AP(w2t_d.tensor, 256 * j * D, [[D, 128], [128 * D, 2], [1, D]])

        def qpiece_out(i, qo_t):
            src = qo_t[:]
            if i < 16:
                kc, hh = i // 2, i % 2
                o = bass.AP(wq1_sc.tensor, (8 * hh) * (128 * 8 * 256) + kc * 256,
                            [[8 * 256, 128], [128 * 8 * 256, 8], [1, 256]])
                return [(o, src.rearrange("p (mp c) -> p mp c", mp=8))]
            j = i - 16
            pairs = []
            for rr in range(2):
                o = bass.AP(wq2_sc.tensor, (2 * j + rr) * 128,
                            [[32 * 128, 128], [128 * 32 * 128, 8], [1, 128]])
                s = bass.AP(qo_t, rr * 1024, [[2048, 128], [128, 8], [1, 128]])
                pairs.append((o, s))
            return pairs

        def emit_invg(col, gcols, psum_t, wait_v):
            vr = op("vector", "VE", lambda e: e.tensor_reduce(
                gc2[:, col:col + 1], gcols, mybir.AxisListType.X, ALU.add),
                waits=[("VE", wait_v)])
            t1 = op("tensor", "TE", lambda e: e.matmul(
                psum_t[0:1, 0:1], ones_cf[:], gc2[:, col:col + 1],
                start=True, stop=True, skip_group_check=True),
                waits=[("VE", vr)])
            v_gr = op("vector", "VE", lambda e: e.tensor_copy(
                grow[0:1, col:col + 1], psum_t[0:1, 0:1]), waits=[("TE", t1)])
            t2 = op("tensor", "TE", lambda e: e.matmul(
                psum_t[0:128, 2:3], ones_rf[:], grow[0:1, col:col + 1],
                start=True, stop=True, skip_group_check=True),
                waits=[("VE", v_gr)])
            vg1 = op("vector", "VE", lambda e: e.tensor_scalar(
                gtmp[:, col:col + 1], psum_t[0:128, 2:3], 1.0 / N_ELEM_W, EPS_Q,
                ALU.mult, ALU.add), waits=[("TE", t2)])
            return op("vector", "VE", lambda e: e.reciprocal(
                invg[:, col:col + 1], gtmp[:, col:col + 1]), waits=[("VE", vg1)])

        with contextlib.ExitStack() as es_vh:
            v_tm = es_vh.enter_context(nc.sbuf_tensor("v_tm", [128, N_XT, D], BF16))
            qbuf = [es_vh.enter_context(nc.sbuf_tensor(f"qb{i}", [128, 2048], F32))
                    for i in range(3)]
            qo = [es_vh.enter_context(nc.sbuf_tensor(f"qo{i}", [128, 2048], BF16))
                  for i in range(2)]

            q_dma = {}
            q_cons = {}
            q_wr = {}
            v_gabs = {}
            nqnt = [0]

            def emit_gamma(seq, i, extra=()):
                b = seq % 3
                xw = [q_cons.get(seq - 3)] + list(extra)
                q_dma[seq] = dma(f"cQB{b}", qbuf[b][:], qpiece_in(i),
                                 waits=[x for x in xw if x], q="gpsimd")
                v_gabs[i] = op("vector", "VE", lambda e, b=b, i=i: e.tensor_reduce(
                    gam[:, i:i + 1], qbuf[b][:], mybir.AxisListType.X, ALU.add,
                    apply_absolute_value=True),
                    waits=[(f"cQB{b}", q_dma[seq])])
                q_cons[seq] = ("VE", v_gabs[i])

            def emit_piece_dma(seq, i, extra=()):
                b = seq % 3
                xw = [q_cons.get(seq - 3)] + list(extra)
                q_dma[seq] = dma(f"cQB{b}", qbuf[b][:], qpiece_in(i),
                                 waits=[x for x in xw if x], q="gpsimd")

            def emit_quant_comp(seq, i, v_invg):
                b = seq % 3
                col = 0 if i < 16 else 1
                bo = nqnt[0] % 2
                qn = nqnt[0]
                nqnt[0] += 1
                p1 = op("scalar", "SE", lambda e, b=b, col=col: e.activation(
                    qbuf[b][:], qbuf[b][:], AF.Identity,
                    scale=invg[:, col:col + 1], bias=magic_c[:]),
                    waits=[(f"cQB{b}", q_dma[seq]), ("VE", v_invg)])
                p2 = op("vector", "VE", lambda e, b=b: e.tensor_scalar(
                    qbuf[b][:], qbuf[b][:], MAGIC, 1.0, ALU.subtract, ALU.min),
                    waits=[("SE", p1)])
                p3 = op("vector", "VE", lambda e, b=b, bo=bo: e.tensor_scalar(
                    qo[bo][:], qbuf[b][:], -1.0, None, ALU.max),
                    waits=[(f"cQW{bo}", q_wr.get(qn - 2)), ("VE", p2)])
                for (o, s) in qpiece_out(i, qo[bo]):
                    q_wr[qn] = dma(f"cQW{bo}", o, s, waits=[("VE", p3)], q="gpsimd")
                q_cons[seq] = ("VE", p3)

            def emit_gamma_comp(seq, i):
                b = seq % 3
                v_gabs[i] = op("vector", "VE", lambda e, b=b, i=i: e.tensor_reduce(
                    gam[:, i:i + 1], qbuf[b][:], mybir.AxisListType.X, ALU.add,
                    apply_absolute_value=True),
                    waits=[(f"cQB{b}", q_dma[seq])])
                q_cons[seq] = ("VE", v_gabs[i])

            tr_done = {}
            v_xn_evac = {}
            v_vevac = {}
            with contextlib.ExitStack() as es_lnv:
                ptr = [es_lnv.enter_context(nc.psum_tensor(f"ptr{i}", [128, D], BF16))
                       for i in range(2)]
                pv = [es_lnv.enter_context(nc.psum_tensor(f"pv{i}", [128, D], F32))
                      for i in range(2)]

                with contextlib.ExitStack() as es_ln:
                    xt = [es_ln.enter_context(nc.sbuf_tensor(f"xt{i}", [128, D], F32))
                          for i in range(4)]
                    junk = es_ln.enter_context(nc.sbuf_tensor("junk", [128, D], F32))
                    ssum = es_ln.enter_context(nc.sbuf_tensor("ssum", [128, N_XT], F32))
                    ssq = es_ln.enter_context(nc.sbuf_tensor("ssq", [128, N_XT], F32))
                    m2c = es_ln.enter_context(nc.sbuf_tensor("m2c", [128, N_XT], F32))
                    rstd = es_ln.enter_context(nc.sbuf_tensor("rstd", [128, N_XT], F32))
                    nmr = es_ln.enter_context(nc.sbuf_tensor("nmr", [128, N_XT], F32))
                    xnt = es_ln.enter_context(nc.sbuf_tensor("xnt", [128, 2, D], BF16))

                    op("vector", "VE", lambda e: e.memset(ssum[:], 0.0), inc=0)
                    v_ms = op("vector", "VE", lambda e: e.memset(ssq[:], 0.0))

                    d_x1 = {}
                    s_sq = {}
                    v_sm = {}
                    for tc in range(N_XT):
                        rows = 128 if tc < 16 else 6
                        b = tc % 4
                        d_x1[tc] = dma(f"cX{b}", xt[b][0:rows, :],
                                       x_tm_d[128 * tc:128 * tc + rows, :],
                                       waits=[("SE", s_sq.get(tc - 4)),
                                              ("VE", v_sm.get(tc - 4))])
                        s_sq[tc] = op("scalar", "SE",
                            lambda e, tc=tc, rows=rows, b=b: e.activation(
                                junk[0:rows, :], xt[b][0:rows, :], AF.Square,
                                accum_out=ssq[0:rows, tc:tc + 1]),
                            waits=[(f"cX{b}", d_x1[tc]),
                                   ("SE", s_sq.get(tc - 1)),
                                   ("VE", v_ms if tc == 0 else None)])
                        v_sm[tc] = op("vector", "VE",
                            lambda e, tc=tc, rows=rows, b=b: e.tensor_reduce(
                                ssum[0:rows, tc:tc + 1], xt[b][0:rows, :],
                                mybir.AxisListType.X, ALU.add),
                            waits=[(f"cX{b}", d_x1[tc]),
                                   ("VE", v_ms if tc == 0 else None)])
                    w1_ = op("vector", "VE", lambda e: e.tensor_scalar(
                        ssum[:], ssum[:], 1.0 / D, None, ALU.mult),
                        waits=[("SE", s_sq[N_XT - 1]), ("VE", v_sm[N_XT - 1])])
                    w2_ = op("vector", "VE", lambda e: e.tensor_tensor(
                        m2c[:], ssum[:], ssum[:], ALU.mult), waits=[("VE", w1_)])
                    w3_ = op("vector", "VE", lambda e: e.scalar_tensor_tensor(
                        ssq[:], ssq[:], 1.0 / D, m2c[:], ALU.mult, ALU.subtract),
                        waits=[("VE", w2_)])
                    w4_ = op("vector", "VE", lambda e: e.tensor_scalar(
                        ssq[:], ssq[:], EPS_LN, None, ALU.add), waits=[("VE", w3_)])
                    v_stats = op("vector", "VE", lambda e: e.reciprocal(
                        m2c[:], ssq[:]), waits=[("VE", w4_)])
                    s_rstd = op("scalar", "SE", lambda e: e.activation(
                        rstd[:], m2c[:], AF.Sqrt), waits=[("VE", v_stats)])
                    v_nmr = op("vector", "VE", lambda e: e.scalar_tensor_tensor(
                        nmr[:], ssum[:], -1.0, rstd[:], ALU.mult, ALU.mult),
                        waits=[("SE", s_rstd), ("VE", v_stats)])

                    d_x2 = {}
                    s_xn = {}
                    for tc in range(N_XT):
                        rows = 128 if tc < 16 else 6
                        b = tc % 4
                        buf = tc % 2
                        d_x2[tc] = dma(f"cX{b}", xt[b][0:rows, :],
                                       x_tm_d[128 * tc:128 * tc + rows, :],
                                       waits=[("SE", s_xn.get(tc - 4)),
                                              ("SE", s_sq[N_XT - 1]),
                                              ("VE", v_sm[N_XT - 1])])
                        s_w = [(f"cX{b}", d_x2[tc]), ("VE", v_nmr),
                               ("SE", s_xn.get(tc - 1))]
                        if tc >= 2:
                            s_w.append(("TE", tr_done[tc - 2]))
                        s_xn[tc] = op("scalar", "SE",
                            lambda e, tc=tc, rows=rows, b=b, buf=buf: e.activation(
                                xnt[0:rows, buf, :], xt[b][0:rows, :], AF.Identity,
                                scale=rstd[0:rows, tc:tc + 1],
                                bias=nmr[0:rows, tc:tc + 1]),
                            waits=s_w)
                        for j in range(DK):
                            w_list = []
                            if j == 0:
                                w_list = [("SE", s_xn[tc])]
                                if tc == 0:
                                    w_list.append(("cA", v_c0))
                                if tc >= 2:
                                    w_list.append(("VE", v_xn_evac[tc - 2]))
                            tr_done[tc] = op("tensor", "TE",
                                lambda e, rows=rows, buf=buf, j=j: e.transpose(
                                    ptr[buf][:, 128 * j:128 * j + rows],
                                    xnt[0:rows, buf, 128 * j:128 * (j + 1)],
                                    ident[0:rows, 0:rows]),
                                waits=w_list)
                        v_xn_evac[tc] = op("vector", "VE",
                            lambda e, tc=tc, rows=rows, buf=buf: e.tensor_copy(
                                xn[:, :, 128 * tc:128 * tc + rows],
                                ptr[buf][:].rearrange("p (j r) -> p j r", j=DK)[:, :, 0:rows]),
                            waits=[("TE", tr_done[tc])])
                        if tc < 16:
                            emit_gamma(tc, tc)
                PH["ln"] = len(steps)

                ln_end = dict(C)
                with contextlib.ExitStack() as es_v:
                    wvf = [es_v.enter_context(nc.sbuf_tensor(f"wvf{i}", [128, D], F32))
                           for i in range(2)]
                    wv_b = es_v.enter_context(nc.sbuf_tensor("wv_b", [128, DK, D], BF16))

                    v_wvc = {}
                    for kc in range(DK):
                        ch = f"cWV{kc % 2}"
                        xw = [("VE", v_wvc.get(kc - 2))]
                        if kc < 2:
                            xw += [("SE", ln_end["SE"]), ("TE", ln_end["TE"]),
                                   ("VE", ln_end["VE"])]
                        d_wv = dma(ch, wvf[kc % 2][:],
                                   wqkv_d[128 * kc:128 * kc + 128, 2048:3072],
                                   waits=xw)
                        v_wvc[kc] = op("vector", "VE", lambda e, kc=kc: e.tensor_copy(
                            wv_b[:, kc, :], wvf[kc % 2][:]), waits=[(ch, d_wv)])

                    for tc in range(N_XT):
                        rows = 128 if tc < 16 else 6
                        buf = tc % 2
                        t_last = None
                        for kc in range(DK):
                            for hh in range(2):
                                wl = []
                                if kc == 0 and hh == 0:
                                    wl = [("VE", v_xn_evac[tc]), ("VE", v_wvc[DK - 1])]
                                    if tc >= 2:
                                        wl.append(("VE", v_vevac[tc - 2]))
                                t_last = op("tensor", "TE",
                                    lambda e, tc=tc, rows=rows, buf=buf, kc=kc, hh=hh:
                                    e.matmul(
                                        pv[buf][0:rows, 512 * hh:512 * hh + 512],
                                        xn[:, kc, 128 * tc:128 * tc + rows],
                                        wv_b[:, kc, 512 * hh:512 * hh + 512],
                                        start=(kc == 0), stop=(kc == DK - 1),
                                        skip_group_check=True),
                                    waits=wl)
                        v_vevac[tc] = op("vector", "VE",
                            lambda e, tc=tc, rows=rows, buf=buf: e.tensor_copy(
                                v_tm[0:rows, tc, :], pv[buf][0:rows, :]),
                            waits=[("TE", t_last)])
            PH["v"] = len(steps)

            v_end = dict(C)
            v_aevac = {}
            v_zevac = {}
            with contextlib.ExitStack() as es_h:
                masks = es_h.enter_context(
                    nc.sbuf_tensor("sb_masks", [128, 17 * MASKW], BF16))
                whf = [es_h.enter_context(nc.sbuf_tensor(f"whf{i}", [128, D], F32))
                       for i in range(2)]
                whb = [es_h.enter_context(nc.sbuf_tensor(f"whb{i}", [128, D], BF16))
                       for i in range(2)]
                wT_sb = es_h.enter_context(nc.sbuf_tensor("wT_sb", [128, D], BF16))
                wqp_sb = [es_h.enter_context(
                    nc.sbuf_tensor(f"wqp{i}", [128, DK * FD], BF16)) for i in range(2)]
                e_scr = [es_h.enter_context(nc.sbuf_tensor(f"e_scr{i}", [64, 512], F32))
                         for i in range(2)]
                r_scr = [es_h.enter_context(nc.sbuf_tensor(f"r_scr{i}", [64, 512], F32))
                         for i in range(2)]
                q_phi = es_h.enter_context(nc.sbuf_tensor("q_phi", [64, P], BF16))
                k_phi = es_h.enter_context(nc.sbuf_tensor("k_phi", [64, P], BF16))
                s_bf = [es_h.enter_context(nc.sbuf_tensor(f"s_bf{i}", [128, MASKW], BF16))
                        for i in range(17)]
                z_all = es_h.enter_context(nc.sbuf_tensor("z_all", [NH, T], F32))
                z2 = es_h.enter_context(nc.sbuf_tensor("z2", [NH, T], F32))
                zinv_b = es_h.enter_context(nc.sbuf_tensor("zinv_b", [NH, T], BF16))
                z_stage = [es_h.enter_context(nc.sbuf_tensor(f"z_st{i}", [1, 512], F32))
                           for i in range(2)]
                pphi = [es_h.enter_context(nc.psum_tensor(f"pphi{i}", [128, 512], F32))
                        for i in range(2)]
                ptr_b = es_h.enter_context(nc.psum_tensor("ptr_b", [128, D], BF16))
                pwq = es_h.enter_context(nc.psum_tensor("pwq", [128, 512], F32))
                pA = [es_h.enter_context(nc.psum_tensor(f"pA{i}", [128, 512], F32))
                      for i in range(2)]
                pZ = [es_h.enter_context(nc.psum_tensor(f"pZ{i}", [128, 512], F32))
                      for i in range(2)]

                d_masks = dma("cM", masks[:], masks_d,
                              waits=[("TE", v_end["TE"]), ("VE", v_end["VE"])])

                v_invg1 = emit_invg(0, gam[:, 0:16], pA[0], v_gabs[15])

                d_zs = {}
                v_wcast = {}
                v_wT = {}
                v_wqp = {}
                v_phislab = {}
                s_exp = {}
                s_relu = {}
                v_sbf = {}
                t_sbf_free = {}
                t_trlast = {}
                hseq = [16]

                for h in range(NH):
                    plist = [(1, 2 * h), (1, 2 * h + 1), (0, 16 + 2 * h), (0, 17 + 2 * h)]
                    gidx = [0]

                    def emit_piece(h=h, plist=plist, gidx=gidx):
                        g = gidx[0]
                        gidx[0] += 1
                        if g < len(plist):
                            kind, i = plist[g]
                            s = hseq[0] + g
                            extra = []
                            if s < 19:
                                extra = [("SE", ln_end["SE"]), ("TE", v_end["TE"]),
                                         ("VE", v_end["VE"])]
                            emit_piece_dma(s, i, extra)
                        gc = g - 2
                        if 0 <= gc < len(plist):
                            kind, i = plist[gc]
                            s = hseq[0] + gc
                            if kind == 1:
                                emit_quant_comp(s, i, v_invg1)
                            else:
                                emit_gamma_comp(s, i)
                        if gc == len(plist) - 1:
                            hseq[0] += len(plist)

                    def emit_wload(hh2):
                        for side in range(2):
                            coff = side * D + 128 * hh2
                            ch = f"cWH{side}"
                            xw = [("VE", v_wcast.get((hh2 - 1, side)))]
                            if hh2 == 0:
                                xw += [("TE", v_end["TE"]), ("VE", v_end["VE"])]
                            d_w = dma(ch,
                                      whf[side][:].rearrange("p (k c) -> p k c", k=8),
                                      bass.AP(wqkv_d.tensor, coff,
                                              [[3 * D, 128], [128 * 3 * D, 8],
                                               [1, 128]]),
                                      waits=xw)
                            v_wcast[(hh2, side)] = op("vector", "VE",
                                lambda e, side=side: e.tensor_copy(
                                    whb[side][:], whf[side][:]),
                                waits=[(ch, d_w),
                                       ("TE", t_trlast.get((hh2 - 1, side)))])

                    if h == 0:
                        emit_wload(0)

                    for side in range(2):
                        t_tr = None
                        for kcb in range(DK):
                            wl = []
                            if kcb == 0:
                                prev_wt = v_wT.get((h, side - 1)) if side == 1 else \
                                    v_wT.get((h - 1, 1))
                                wl = [("VE", v_wcast[(h, side)]), ("VE", prev_wt)]
                                if h == 0 and side == 0:
                                    wl.append(("VE", v_invg1))
                            t_tr = op("tensor", "TE",
                                lambda e, side=side, kcb=kcb: e.transpose(
                                    ptr_b[:, 128 * kcb:128 * (kcb + 1)],
                                    whb[side][:, 128 * kcb:128 * (kcb + 1)],
                                    ident[:]),
                                waits=wl)
                        v_wT[(h, side)] = op("vector", "VE",
                            lambda e: e.tensor_copy(wT_sb[:], ptr_b[:]),
                            waits=[("TE", t_tr)])
                        t_wq = None
                        for kcb in range(DK):
                            wl = []
                            if kcb == 0:
                                prev_wq = v_wqp.get((h, side - 1)) if side == 1 else \
                                    v_wqp.get((h - 1, 1))
                                wl = [("VE", v_wT[(h, side)]), ("VE", prev_wq),
                                      ("VE", v_sbf.get((h - 1, 14)))]
                            t_wq = op("tensor", "TE",
                                lambda e, kcb=kcb: e.matmul(
                                    pwq[:, FD * kcb:FD * (kcb + 1)],
                                    wT_sb[:, 128 * kcb:128 * (kcb + 1)], proj_b[:],
                                    start=True, stop=True, skip_group_check=True),
                                waits=wl)
                        t_trlast[(h, side)] = t_tr
                        v_wqp[(h, side)] = op("vector", "VE",
                            lambda e, side=side: e.tensor_copy(
                                wqp_sb[side][:], pwq[:]),
                            waits=[("TE", t_wq)])
                    if h + 1 < NH:
                        emit_wload(h + 1)

                    for side, phi in enumerate([q_phi, k_phi]):
                        for si, (sb0, sw) in enumerate(SLABS5):
                            buf = si % 2
                            t_last = None
                            for kc in range(DK):
                                wl = []
                                if kc == 0:
                                    if si >= 2:
                                        pk = (h, side, si - 2)
                                    elif side == 1:
                                        pk = (h, 0, 4 - si)
                                    else:
                                        pk = (h - 1, 1, 4 - si)
                                    wl = [("VE", v_wqp[(h, side)]),
                                          ("SE", s_relu.get(pk)),
                                          ("VE", v_phislab.get(pk))]
                                    if side == 0 and si < 2:
                                        wl.append(("VE", v_sbf.get((h - 1, 15 + si))))
                                t_last = op("tensor", "TE",
                                    lambda e, buf=buf, kc=kc, sb0=sb0, sw=sw,
                                           side=side: e.matmul(
                                        pphi[buf][0:64, 0:sw],
                                        wqp_sb[side][:, FD * kc:FD * (kc + 1)],
                                        xn[:, kc, sb0:sb0 + sw],
                                        start=(kc == 0), stop=(kc == DK - 1),
                                        skip_group_check=True),
                                    waits=wl)
                            s_exp[(h, side, si)] = op("scalar", "SE",
                                lambda e, buf=buf, sw=sw: e.activation(
                                    e_scr[buf][:, 0:sw], pphi[buf][0:64, 0:sw], AF.Exp),
                                waits=[("TE", t_last)])
                            s_relu[(h, side, si)] = op("scalar", "SE",
                                lambda e, buf=buf, sw=sw: e.activation(
                                    r_scr[buf][:, 0:sw], pphi[buf][0:64, 0:sw], AF.Relu),
                                waits=[("SE", s_exp[(h, side, si)])])
                            v_phislab[(h, side, si)] = op("vector", "VE",
                                lambda e, buf=buf, sb0=sb0, sw=sw, phi=phi:
                                e.scalar_tensor_tensor(
                                    phi[:, sb0:sb0 + sw], e_scr[buf][:, 0:sw], 1.0,
                                    r_scr[buf][:, 0:sw], ALU.min, ALU.add),
                                waits=[("SE", s_relu[(h, side, si)]),
                                       ("VE", v_phislab.get((h, side, si - 2)))])
                            emit_piece()

                    op("vector", "VE", lambda e: e.tensor_tensor(
                        k_phi[:, 0:3], k_phi[:, 0:3], emask[:, 0:3], ALU.mult),
                        waits=[("VE", v_phislab[(h, 1, 4)])], inc=0)
                    v_kedge = op("vector", "VE", lambda e: e.tensor_tensor(
                        k_phi[:, 2051:2054], k_phi[:, 2051:2054], emask[:, 3:6],
                        ALU.mult))

                    pSp = [pphi[0], pphi[1], pwq]
                    first_a = [True] * 4
                    first_z = [True] * 4
                    watermark = [0] * 4
                    t_band = {}
                    for kc in range(17):
                        j0, w = KCS[kc]
                        tb, te = _bwin(kc)
                        tw = te - tb
                        buf = kc % 3
                        wl = [("VE", v_sbf.get((h, kc - 3)))]
                        if kc == 0:
                            wl = [("VE", v_kedge), ("cM", d_masks),
                                  ("SE", s_relu[(h, 1, 4)])]
                        t_s = op("tensor", "TE",
                            lambda e, buf=buf, j0=j0, w=w, tb=tb, tw=tw: e.matmul(
                                pSp[buf][0:w, 0:tw], k_phi[:, j0:j0 + w],
                                q_phi[:, tb:tb + tw],
                                start=True, stop=True, skip_group_check=True),
                            waits=wl)
                        sbf_prev = t_sbf_free.get((h - 1, kc))
                        v_sbf[(h, kc)] = op("vector", "VE",
                            lambda e, buf=buf, w=w, tw=tw, kc=kc: e.tensor_tensor(
                                s_bf[kc][0:w, 0:tw], pSp[buf][0:w, 0:tw],
                                masks[0:w, MASKW * kc:MASKW * kc + tw], ALU.mult),
                            waits=[("TE", t_s), ("TE", sbf_prev)])
                    for kc in range(17):
                        j0, w = KCS[kc]
                        tb, te = _bwin(kc)
                        tw = te - tb
                        buf = kc
                        for u in range(4):
                            for (k2, a, b_, pa) in SLAB_PARTS[u]:
                                if k2 != kc:
                                    continue
                                wl_a = [("VE", v_sbf[(h, kc)])]
                                if h == 0 and first_a[u]:
                                    wl_a.append(("VE", v_vevac[N_XT - 1]))
                                if first_a[u]:
                                    pkey = (h, u - 2) if u >= 2 else (h - 1, u + 2)
                                    wl_a.append(("VE", v_aevac.get(pkey)))
                                    wl_a.append(("VE", v_zevac.get(pkey)))
                                    if h == 0 and u == 0:
                                        wl_a.append(("VE", v_invg1))
                                wm = watermark[u]
                                pieces = []
                                if pa < wm:
                                    pieces.append((a, a + min(wm - pa, b_ - a), pa, False))
                                if pa + (b_ - a) > wm:
                                    a2 = a + max(0, wm - pa)
                                    pieces.append((a2, b_, pa + (a2 - a), first_a[u]))
                                for (aa, bb, pp, st_flag) in pieces:
                                    op("tensor", "TE",
                                        lambda e, u=u, aa=aa, bb=bb, pp=pp, buf=buf,
                                               w=w, kc=kc, h=h, fa=st_flag: e.matmul(
                                            pA[u % 2][:, pp:pp + (bb - aa)],
                                            v_tm[0:w, kc, 128 * h:128 * h + 128],
                                            s_bf[kc][0:w, aa:bb],
                                            start=fa, stop=True,
                                            skip_group_check=True),
                                        waits=wl_a)
                                    wl_a = []
                                    t_z = op("tensor", "TE",
                                        lambda e, u=u, aa=aa, bb=bb, pp=pp, buf=buf,
                                               w=w, kc=kc, fz=st_flag: e.matmul(
                                            pZ[u % 2][:, pp:pp + (bb - aa)],
                                            ones_mb[0:w, :], s_bf[kc][0:w, aa:bb],
                                            start=fz, stop=True,
                                            skip_group_check=True))
                                watermark[u] = max(wm, pa + (b_ - a))
                                first_a[u] = False
                                first_z[u] = False
                                t_band[u] = t_z
                                t_sbf_free[(h, kc)] = t_z
                        for u in range(4):
                            if SLAB_LAST_KC[u] != kc:
                                continue
                            idx = h * 4 + u
                            v_aevac[(h, u)] = op("vector", "VE",
                                lambda e, u=u, h=h: e.tensor_copy(
                                    attn_r[:, h, 512 * u:512 * u + 512],
                                    pA[u % 2][:, 0:512]),
                                waits=[("TE", t_band[u])])
                            v_zevac[(h, u)] = op("vector", "VE",
                                lambda e, u=u: e.tensor_copy(
                                    z_stage[u % 2][0:1, :], pZ[u % 2][0:1, 0:512]),
                                waits=[(f"cZS{u % 2}", d_zs.get(idx - 2))])
                            d_zs[idx] = dma(f"cZS{u % 2}",
                                            zdram_f[h:h + 1, 512 * u:512 * u + 512],
                                            z_stage[u % 2][0:1, :],
                                            waits=[("VE", v_zevac[(h, u)])])

                d_zall = dma("cZA", z_all[:], zdram_f,
                             waits=[("cZS0", d_zs[NH * 4 - 2]),
                                    ("cZS1", d_zs[NH * 4 - 1])])
                vz1 = op("vector", "VE", lambda e: e.tensor_scalar(
                    z_all[:], z_all[:], EPS_Z, None, ALU.add),
                    waits=[("cZA", d_zall)])
                vz2 = op("vector", "VE", lambda e: e.reciprocal(z2[:], z_all[:]),
                         waits=[("VE", vz1)])
                v_zi = op("vector", "VE", lambda e: e.tensor_copy(zinv_b[:], z2[:]),
                          waits=[("VE", vz2)])
                d_z1 = dma("cZI", zdram, zinv_b[:], waits=[("VE", v_zi)])
                v_gabs2_last = v_gabs[31]
                sq_scr0 = C.get("cQW0", 0)
                sq_scr1 = C.get("cQW1", 0)
                d_z1_val = C["cZI"]
            h_end = dict(C)
            PH["heads"] = len(steps)

        with contextlib.ExitStack() as es_f:
            h_sb = es_f.enter_context(nc.sbuf_tensor("h_sb", [128, MF, 512], BF16))
            g_fm = es_f.enter_context(nc.sbuf_tensor("g_fm", [128, DK, 512], BF16))
            g_tm = es_f.enter_context(nc.sbuf_tensor("g_tm", [128, N_TC, D], BF16))
            wq1b = [es_f.enter_context(nc.sbuf_tensor(f"wq1b{i}", [128, 2048], BF16))
                    for i in range(2)]
            w2tb = [es_f.enter_context(nc.sbuf_tensor(f"w2tb{i}", [128, 2048], BF16))
                    for i in range(2)]
            wout_b = es_f.enter_context(nc.sbuf_tensor("wout_b", [128, DK, D], BF16))
            qb2 = [es_f.enter_context(nc.sbuf_tensor(f"qb2_{i}", [128, 2048], F32))
                   for i in range(2)]
            qo2 = [es_f.enter_context(nc.sbuf_tensor(f"qo2_{i}", [128, 2048], BF16))
                   for i in range(2)]
            zr2 = es_f.enter_context(nc.sbuf_tensor("zr2", [1, T], BF16))
            xtf = qb2

            with contextlib.ExitStack() as es_p1:
                ph = [es_p1.enter_context(nc.psum_tensor(f"ph{i}", [128, 512], F32))
                      for i in range(2)]
                pg = [es_p1.enter_context(nc.psum_tensor(f"pg{i}", [128, 512], F32))
                      for i in range(2)]
                pgt = [es_p1.enter_context(nc.psum_tensor(f"pgt{i}", [128, D], BF16))
                       for i in range(2)]
                psf = [es_p1.enter_context(nc.psum_tensor(f"psf{i}", [128, 512], F32))
                       for i in range(2)]

                hw = [("TE", h_end["TE"]), ("VE", h_end["VE"]), ("SE", h_end["SE"])]

                v_woc = {}
                for dc in range(DK):
                    ch = f"cWO{dc % 2}"
                    xw = [("VE", v_woc.get(dc - 2))]
                    if dc < 2:
                        xw += hw
                    d_wo = dma(ch, qb2[dc % 2][:, 0:1024],
                               wout_d[128 * dc:128 * dc + 128, :], waits=xw)
                    v_woc[dc] = op("vector", "VE", lambda e, dc=dc: e.tensor_copy(
                        wout_b[:, dc, :], qb2[dc % 2][:, 0:1024]),
                        waits=[(ch, d_wo)])
                v_wo_last = v_woc[DK - 1]

                v_invg2 = emit_invg(1, gam[:, 16:32], ph[0], v_gabs2_last)

                q2_dma = {}
                q2_cons = {}
                q2_wr = {}
                n2 = [0]
                m2 = [0]

                def emit_quant2_dma():
                    i2 = n2[0]
                    if i2 >= 16:
                        return
                    n2[0] += 1
                    i = 16 + i2
                    b = i2 % 2
                    xw = [q2_cons.get(i2 - 2)]
                    if i2 < 2:
                        xw += [("VE", v_woc[6 + b])]
                        xw += hw
                    q2_dma[i2] = dma(f"cQ2{b}", qb2[b][:], qpiece_in(i),
                                     waits=[x for x in xw if x], q="gpsimd")

                def emit_quant2_comp():
                    i2 = m2[0]
                    if i2 >= 16 or i2 >= n2[0]:
                        return
                    m2[0] += 1
                    i = 16 + i2
                    b = i2 % 2
                    p1 = op("vector", "VE", lambda e, b=b: e.tensor_scalar(
                        qb2[b][:], qb2[b][:], invg[:, 1:2], MAGIC,
                        ALU.mult, ALU.add),
                        waits=[(f"cQ2{b}", q2_dma[i2]), ("VE", v_invg2)])
                    p2 = op("vector", "VE", lambda e, b=b: e.tensor_scalar(
                        qb2[b][:], qb2[b][:], MAGIC, 1.0, ALU.subtract, ALU.min),
                        waits=[("VE", p1)])
                    p3 = op("vector", "VE", lambda e, b=b: e.tensor_scalar(
                        qo2[b][:], qb2[b][:], -1.0, None, ALU.max),
                        waits=[(f"cQ2W{b}", q2_wr.get(i2 - 2)), ("VE", p2)])
                    for (o, s) in qpiece_out(i, qo2[b]):
                        q2_wr[i2] = dma(f"cQ2W{b}", o, s, waits=[("VE", p3)],
                                        q="gpsimd")
                    q2_cons[i2] = ("VE", p3)

                v_div = {}
                d_z2d = {}
                for h2 in range(NH):
                    xw = [("cZI", d_z1_val), ("VE", v_div.get(4 * h2 - 1))]
                    if h2 == 0:
                        xw += hw
                    d_z2d[h2] = dma("cZR", zr2[0:1, :],
                                    bass.AP(zdram.tensor, h2 * T, [[T, 1], [1, T]]),
                                    waits=xw)
                    for u in range(4):
                        idx = h2 * 4 + u
                        buf = idx % 2
                        wl = [("cZR", d_z2d[h2])]
                        if idx >= 2:
                            wl.append(("VE", v_div[idx - 2]))
                        t_r = op("tensor", "TE", lambda e, buf=buf, u=u: e.matmul(
                            psf[buf][:, 0:512], ones_rb[:],
                            zr2[0:1, 512 * u:512 * u + 512],
                            start=True, stop=True, skip_group_check=True),
                            waits=wl)
                        v_div[idx] = op("vector", "VE",
                            lambda e, buf=buf, h2=h2, u=u: e.tensor_tensor(
                                attn_r[:, h2, 512 * u:512 * u + 512],
                                attn_r[:, h2, 512 * u:512 * u + 512],
                                psf[buf][:, 0:512], ALU.mult),
                            waits=[("TE", t_r)])
                v_div_last = v_div[NH * 4 - 1]

                h_ev = {}
                h_eng = {}
                t_wq1 = {}
                t_w2_half = {}
                t_ffn2_last = {}
                s_gev = {}
                v_gt = {}
                v_stt = {}
                v_ob = {}
                d_out = {}

                def emit_final(tc):
                    buf = tc % 2
                    xw = [("VE", v_ob.get(tc - 2))]
                    if tc < 2:
                        c2 = q2_cons.get(14 + buf)
                        if c2:
                            xw.append(c2)
                        xw.append((f"cQ2{buf}", q2_dma.get(14 + buf)))
                    d_x = dma(f"cXF{buf}", xtf[buf][:, 0:1024],
                              x_tm_d[HALF + 128 * tc:HALF + 128 * tc + 128, :],
                              waits=xw)
                    for hh in range(2):
                        t_last = None
                        for dc in range(DK):
                            wl = []
                            if dc == 0:
                                wl = [("VE", v_gt[tc]), ("VE", v_consts)]
                                if tc == 0 and hh == 0:
                                    wl += [("VE", v_div_last), ("VE", v_wo_last)]
                                pv_ = v_stt.get((tc - 1, hh))
                                if tc == 0:
                                    pv_ = v_div_last
                                wl.append(("VE", pv_))
                            t_last = op("tensor", "TE",
                                lambda e, hh=hh, dc=dc, tc=tc: e.matmul(
                                    psf[hh][:, 0:512],
                                    attn_r[:, dc, 128 * tc:128 * (tc + 1)],
                                    wout_b[:, dc, 512 * hh:512 * hh + 512],
                                    start=(dc == 0), stop=False,
                                    skip_group_check=True),
                                waits=wl)
                        t_last = op("tensor", "TE",
                            lambda e, hh=hh: e.matmul(
                                psf[hh][:, 0:512], ones_rb[:],
                                bout_b[0:1, 512 * hh:512 * hh + 512],
                                start=False, stop=True, skip_group_check=True))
                        v_stt[(tc, hh)] = op("vector", "VE",
                            lambda e, buf=buf, tc=tc, hh=hh: e.scalar_tensor_tensor(
                                qb2_out[buf][:, 512 * hh:512 * hh + 512],
                                psf[hh][:, 0:512], 0.0,
                                g_tm[:, tc, 512 * hh:512 * hh + 512],
                                ALU.add, ALU.mult),
                            waits=[("TE", t_last), (f"cXF{buf}", d_x),
                                   (f"cOB{buf}", d_out.get(tc - 2))])
                    v_ob[tc] = op("vector", "VE", lambda e, buf=buf: e.tensor_tensor(
                        qb2_out[buf][:], qb2_out[buf][:], xtf[buf][:, 0:1024],
                        ALU.add),
                        waits=[("VE", v_stt[(tc, 0)]), ("VE", v_stt[(tc, 1)])])
                    d_out[tc] = dma(f"cOB{buf}", out_d[128 * tc:128 * (tc + 1), :],
                                    qb2_out[buf][:], waits=[("VE", v_ob[tc])])

                qb2_out = [bass.AP(qb2[0], 1024, [[2048, 128], [1, 1024]]),
                           bass.AP(qb2[1], 1024, [[2048, 128], [1, 1024]])]

                for u in range(4):
                    xs = HALF + 512 * u
                    for mp in range(16):
                        wv = []
                        if (u, mp - 2) in t_wq1:
                            wv.append(("TE", t_wq1[(u, mp - 2)]))
                        elif u > 0 and mp < 2:
                            wv.append(("TE", t_wq1[(u - 1, 14 + mp)]))
                        if u == 0 and mp < 2:
                            wv.append(("cQW0", sq_scr0))
                            wv.append(("cQW1", sq_scr1))
                            wv += hw
                        d_wq1 = dma(
                            f"cW1{mp % 2}", wq1b[mp % 2][:],
                            bass.AP(wq1_sc.tensor, mp * (128 * 8 * 256),
                                    [[8 * 256, 128], [1, 8 * 256]]),
                            waits=wv, q="sync")
                        for mo in range(2):
                            m = 2 * mp + mo
                            t_last = None
                            for kc in range(DK):
                                wl = []
                                if kc == 0:
                                    wl = [(f"cW1{mp % 2}", d_wq1)]
                                    if m >= 2:
                                        wl.append((h_eng[(u, m - 2)], h_ev[(u, m - 2)]))
                                    elif u > 0:
                                        wl.append((h_eng[(u - 1, 30 + m)],
                                                   h_ev[(u - 1, 30 + m)]))
                                t_last = op("tensor", "TE",
                                    lambda e, m=m, kc=kc, xs=xs, mp=mp: e.matmul(
                                        ph[m % 2][:, 0:512],
                                        wq1b[mp % 2][:, 256 * kc + 128 * (m - 2 * mp):
                                                      256 * kc + 128 * (m - 2 * mp) + 128],
                                        xn[:, kc, xs:xs + 512],
                                        start=(kc == 0), stop=(kc == DK - 1),
                                        skip_group_check=True),
                                    waits=wl)
                            t_wq1[(u, mp)] = t_last
                            if m % 2 == 0:
                                h_eng[(u, m)] = "VE"
                                h_ev[(u, m)] = op("vector", "VE",
                                    lambda e, m=m: e.tensor_scalar(
                                        h_sb[:, m, :], ph[m % 2][:, 0:512],
                                        b1t[:, m:m + 1], 0.0, ALU.add, ALU.max),
                                    waits=[("TE", t_last)])
                            else:
                                h_eng[(u, m)] = "SE"
                                h_ev[(u, m)] = op("scalar", "SE",
                                    lambda e, m=m: e.activation(
                                        h_sb[:, m, :], ph[m % 2][:, 0:512], AF.Relu,
                                        bias=b1t[:, m:m + 1]),
                                    waits=[("TE", t_last)])
                        if u == 0:
                            emit_quant2_comp()
                            emit_quant2_dma()
                    if u == 0:
                        emit_quant2_comp()
                        emit_quant2_comp()
                    for dc in range(DK):
                        t_last = None
                        for half in range(2):
                            wv = [("TE", t_w2_half.get(half))]
                            if u == 0 and dc == 0:
                                wv.append((f"cQ2W0", q2_wr.get(14)))
                                wv.append((f"cQ2W1", q2_wr.get(15)))
                                wv += hw
                            d_w2t = dma(
                                f"cW2{half}", w2tb[half][:],
                                bass.AP(wq2_sc.tensor,
                                        dc * (128 * 32 * 128) + half * 2048,
                                        [[32 * 128, 128], [1, 2048]]),
                                waits=wv, q="sync")
                            for kk in range(16):
                                kc2 = 16 * half + kk
                                wl = []
                                if kk == 0:
                                    wl = [(f"cW2{half}", d_w2t)]
                                    if half == 0:
                                        wl += [("VE", h_ev[(u, 30)]),
                                               ("SE", h_ev[(u, 31)]),
                                               ("SE", s_gev.get((u, dc - 2)))]
                                t_last = op("tensor", "TE",
                                    lambda e, dc=dc, kc2=kc2, half=half, kk=kk: e.matmul(
                                        pg[dc % 2][:, 0:512],
                                        w2tb[half][:, 128 * kk:128 * kk + 128],
                                        h_sb[:, kc2, :],
                                        start=(kc2 == 0), stop=(kc2 == MF - 1),
                                        skip_group_check=True),
                                    waits=wl)
                            t_w2_half[half] = t_last
                        t_ffn2_last[(u, dc)] = t_last
                        s_gev[(u, dc)] = op("scalar", "SE",
                            lambda e, dc=dc: e.activation(
                                g_fm[:, dc, :], pg[dc % 2][:, 0:512], AF.Silu,
                                bias=b2t[:, dc:dc + 1]),
                            waits=[("TE", t_last)])
                    for tt in range(4):
                        tc = 4 * u + tt
                        buf = tt % 2
                        t_tr = None
                        for dc2 in range(DK):
                            wl = []
                            if dc2 == 0:
                                wl = [("SE", s_gev[(u, DK - 1)])]
                                if tc >= 2:
                                    wl.append(("VE", v_gt.get(tc - 2)))
                            t_tr = op("tensor", "TE",
                                lambda e, buf=buf, dc2=dc2, tt=tt: e.transpose(
                                    pgt[buf][:, 128 * dc2:128 * (dc2 + 1)],
                                    g_fm[0:128, dc2, 128 * tt:128 * (tt + 1)],
                                    ident[:]),
                                waits=wl)
                        v_gt[tc] = op("vector", "VE",
                            lambda e, buf=buf, tc=tc: e.tensor_copy(
                                g_tm[:, tc, :], pgt[buf][:]),
                            waits=[("TE", t_tr)])
                        emit_final(tc)

                fin0 = C.get("cOB0", 0)
                fin1 = C.get("cOB1", 0)

                def f_fin(e, fin0=fin0, fin1=fin1):
                    e.wait_ge(SEM["cOB0"], fin0)
                    e.wait_ge(SEM["cOB1"], fin1)
                steps.append(("sync", f_fin, 0))

        PH["all"] = len(steps)
        nc._phase_marks = PH
        nc._steps_meta = [(s[0], s[2]) for s in steps]
        block = es.enter_context(nc.Block())
        use_steps = steps if cut is None else steps[:cut]

        def mk(section_name):
            def _(eng):
                for st in use_steps:
                    if st[0] == section_name:
                        st[1](eng)
            return _

        block.sync(mk("sync"))
        block.tensor(mk("tensor"))
        block.vector(mk("vector"))
        block.scalar(mk("scalar"))
        block.gpsimd(mk("gpsimd"))

    return nc


_NC_CACHE = None


def _get_nc():
    global _NC_CACHE
    if _NC_CACHE is None:
        _NC_CACHE = build_nc()
    return _NC_CACHE


def _host_masks():
    m = np.zeros((128, 17 * MASKW), dtype=np.float32)
    for kc in range(17):
        j0, w = KCS[kc]
        tb, te = _bwin(kc)
        jj = np.arange(w)[:, None]
        tt = np.arange(te - tb)[None, :]
        m[0:w, MASKW * kc:MASKW * kc + (te - tb)] = (
            np.abs((tb + tt) - (j0 + jj)) <= HALF).astype(np.float32)
    return m.astype(ml_dtypes.bfloat16)


def kernel(L0_attn_x, L0_attn_w_qkv, L0_attn_proj, L0_attn_w_out, L0_attn_b_out,
           L0_ffn_W1, L0_ffn_b1, L0_ffn_W2, L0_ffn_b2):
    x = np.asarray(L0_attn_x, dtype=np.float32)
    wqkv = np.ascontiguousarray(np.asarray(L0_attn_w_qkv, dtype=np.float32))
    proj = np.ascontiguousarray(np.asarray(L0_attn_proj, dtype=np.float32))
    wout = np.ascontiguousarray(np.asarray(L0_attn_w_out, dtype=np.float32))
    bout = np.ascontiguousarray(np.asarray(L0_attn_b_out, dtype=np.float32).reshape(1, D))
    w1 = np.asarray(L0_ffn_W1, dtype=np.float32)
    b1 = np.asarray(L0_ffn_b1, dtype=np.float32)
    w2 = np.asarray(L0_ffn_W2, dtype=np.float32)
    b2 = np.asarray(L0_ffn_b2, dtype=np.float32)

    w1t = np.ascontiguousarray(w1.T)
    w2t = np.ascontiguousarray(w2.T)
    b1t = np.ascontiguousarray(b1.reshape(MF, 128).T)
    b2t = np.ascontiguousarray(b2.reshape(DK, 128).T)
    ident = np.eye(128, dtype=ml_dtypes.bfloat16)
    masks = _host_masks()

    in_maps = []
    for c in range(8):
        seq, t0 = c // 4, (c % 4) * T
        sl = np.zeros((P, D), dtype=np.float32)
        lo, hi = max(0, t0 - HALF), min(8192, t0 + T + HALF)
        sl[lo - (t0 - HALF):lo - (t0 - HALF) + (hi - lo)] = x[seq, lo:hi]
        em = np.ones((64, 6), dtype=np.float32)
        if t0 == 0:
            em[:, 0:3] = 0.0
        if t0 + T == 8192:
            em[:, 3:6] = 0.0
        in_maps.append({
            "x_tm": sl, "wqkv": wqkv, "proj": proj, "wout": wout, "bout": bout,
            "w1t": w1t, "b1t": b1t, "w2t": w2t, "b2t": b2t,
            "ident": ident, "masks": masks, "emask": em,
        })

    nc = _get_nc()
    res = run_bass_kernel_spmd(nc, in_maps, core_ids=list(range(8)))
    out = np.empty((2, 8192, D), dtype=np.float32)
    for c in range(8):
        seq, t0 = c // 4, (c % 4) * T
        out[seq, t0:t0 + T] = res.results[c]["out"]
    return out


# version 09
# speedup vs baseline: 1.3769x; 1.1037x over previous
import contextlib
import numpy as np
import ml_dtypes
import concourse.bass as bass
import concourse.mybir as mybir
from concourse.bass_utils import run_bass_kernel_spmd

F32 = mybir.dt.float32
BF16 = mybir.dt.bfloat16
AF = mybir.ActivationFunctionType
ALU = mybir.AluOpType

P = 2054
T = 2048
HALF = 3
D = 1024
DK = 8
NH = 8
FD = 64
DFF = 4096
MF = 32
MAGIC = 12582912.0
EPS_LN = 1e-5
EPS_Q = 1e-5
EPS_Z = 1e-8
N_ELEM_W = float(DFF * D)
MASKW = 134
N_XT = 17
N_TC = 16

SLABS5 = [(0, 512), (512, 512), (1024, 512), (1536, 512), (2048, 6)]
KCS = [(k * 128, min(128, P - k * 128)) for k in range(17)]


def _bwin(kc):
    j0, w = KCS[kc]
    return max(HALF, j0 - HALF), min(P - HALF, j0 + w + HALF)


SLAB_PARTS = [[] for _ in range(4)]
SLAB_LAST_KC = [0] * 4
for _kc in range(17):
    _tb, _te = _bwin(_kc)
    for _u in range(4):
        _su, _eu = HALF + 512 * _u, HALF + 512 * (_u + 1)
        _lo, _hi = max(_tb, _su), min(_te, _eu)
        if _lo < _hi:
            SLAB_PARTS[_u].append((_kc, _lo - _tb, _hi - _tb, _lo - _su))
            SLAB_LAST_KC[_u] = max(SLAB_LAST_KC[_u], _kc)


def build_nc(cut=None):
    nc = bass.Bass("TRN2", target_bir_lowering=False)

    x_tm_d = nc.dram_tensor("x_tm", [P, D], F32, kind="ExternalInput").ap()
    wqkv_d = nc.dram_tensor("wqkv", [D, 3 * D], F32, kind="ExternalInput").ap()
    proj_d = nc.dram_tensor("proj", [128, FD], F32, kind="ExternalInput").ap()
    wout_d = nc.dram_tensor("wout", [D, D], F32, kind="ExternalInput").ap()
    bout_d = nc.dram_tensor("bout", [1, D], F32, kind="ExternalInput").ap()
    w1t_d = nc.dram_tensor("w1t", [D, DFF], F32, kind="ExternalInput").ap()
    b1t_d = nc.dram_tensor("b1t", [128, MF], F32, kind="ExternalInput").ap()
    w2t_d = nc.dram_tensor("w2t", [DFF, D], F32, kind="ExternalInput").ap()
    b2t_d = nc.dram_tensor("b2t", [128, DK], F32, kind="ExternalInput").ap()
    ident_d = nc.dram_tensor("ident", [128, 128], BF16, kind="ExternalInput").ap()
    masks_d = nc.dram_tensor("masks", [128, 17 * MASKW], BF16, kind="ExternalInput").ap()
    emask_d = nc.dram_tensor("emask", [64, 6], F32, kind="ExternalInput").ap()
    out_d = nc.dram_tensor("out", [T, D], F32, kind="ExternalOutput").ap()

    wq1_sc = nc.dram_tensor("wq1_sc", [16, 128, 8, 256], BF16, kind="Internal").ap()
    wq2_sc = nc.dram_tensor("wq2_sc", [8, 128, 32, 128], BF16, kind="Internal").ap()
    zdram = nc.dram_tensor("zdram", [NH, T], BF16, kind="Internal").ap()
    zdram_f = nc.dram_tensor("zdram_f", [NH, T], F32, kind="Internal").ap()

    steps = []
    PH = {}
    C = {"TE": 0, "VE": 0, "SE": 0}

    def bump(name, n=1):
        C[name] = C.get(name, 0) + n
        return C[name]

    es = contextlib.ExitStack()
    with es:
        te_s = es.enter_context(nc.semaphore("te"))
        ve_s = es.enter_context(nc.semaphore("ve"))
        se_s = es.enter_context(nc.semaphore("se"))
        SEM = {"TE": te_s, "VE": ve_s, "SE": se_s}
        sem_es = es

        def dch(name):
            if name not in SEM:
                SEM[name] = sem_es.enter_context(nc.semaphore(name))
                C[name] = 0
            return SEM[name]

        def dma(ch, out, in_, waits=(), q="sync"):
            hnd = dch(ch)
            w = [(s, v) for (s, v) in waits if v]
            val = bump(ch, 16)

            def f(e, out=out, in_=in_, w=w, hnd=hnd):
                for (s, v) in w:
                    e.wait_ge(SEM[s], v)
                e.dma_start(out=out, in_=in_).then_inc(hnd, 16)
            import traceback
            steps.append((q, f, traceback.extract_stack()[-2].lineno))
            return val

        def op(eng, sem_name, emitf, waits=(), inc=1):
            w = [(s, v) for (s, v) in waits if v]
            val = bump(sem_name, inc) if inc else C[sem_name]

            def f(e, emitf=emitf, w=w, inc=inc, sem_name=sem_name):
                for (s, v) in w:
                    e.wait_ge(SEM[s], v)
                ins = emitf(e)
                if inc:
                    ins.then_inc(SEM[sem_name], inc)
            import traceback
            steps.append((eng, f, traceback.extract_stack()[-2].lineno))
            return val

        ident = es.enter_context(nc.sbuf_tensor("sb_ident", [128, 128], BF16))
        emask = es.enter_context(nc.sbuf_tensor("sb_emask", [64, 6], F32))
        proj_f = es.enter_context(nc.sbuf_tensor("proj_f", [128, FD], F32))
        proj_b = es.enter_context(nc.sbuf_tensor("proj_b", [128, FD], BF16))
        b1t = es.enter_context(nc.sbuf_tensor("sb_b1t", [128, MF], F32))
        b2t = es.enter_context(nc.sbuf_tensor("sb_b2t", [128, DK], F32))
        bout_f = es.enter_context(nc.sbuf_tensor("bout_f", [1, D], F32))
        bout_b = es.enter_context(nc.sbuf_tensor("bout_b", [1, D], BF16))
        ones_rb = es.enter_context(nc.sbuf_tensor("ones_rb", [1, 128], BF16))
        ones_mb = es.enter_context(nc.sbuf_tensor("ones_mb", [128, 128], BF16))
        ones_cf = es.enter_context(nc.sbuf_tensor("ones_cf", [128, 1], F32))
        magic_c = es.enter_context(nc.sbuf_tensor("magic_c", [128, 1], F32))
        ones_rf = es.enter_context(nc.sbuf_tensor("ones_rf", [1, 128], F32))
        gam = es.enter_context(nc.sbuf_tensor("gam", [128, 32], F32))
        gc2 = es.enter_context(nc.sbuf_tensor("gc2", [128, 2], F32))
        grow = es.enter_context(nc.sbuf_tensor("grow", [1, 2], F32))
        gtmp = es.enter_context(nc.sbuf_tensor("gtmp", [128, 2], F32))
        invg = es.enter_context(nc.sbuf_tensor("invg", [128, 2], F32))

        xn = es.enter_context(nc.sbuf_tensor("xn", [128, DK, P], BF16))
        attn_r = es.enter_context(nc.sbuf_tensor("attn_r", [128, NH, T], BF16))

        v_c0 = 0
        for (dst, src) in [(ident[:], ident_d), (emask[:], emask_d),
                           (proj_f[:], proj_d), (b1t[:], b1t_d), (b2t[:], b2t_d),
                           (bout_f[:], bout_d)]:
            v_c0 = dma("cA", dst, src)
        op("vector", "VE", lambda e: e.memset(ones_rb[:], 1.0), inc=0)
        op("vector", "VE", lambda e: e.memset(ones_mb[:], 1.0), inc=0)
        op("vector", "VE", lambda e: e.memset(ones_cf[:], 1.0), inc=0)
        op("vector", "VE", lambda e: e.memset(magic_c[:], MAGIC), inc=0)
        op("vector", "VE", lambda e: e.memset(ones_rf[:], 1.0), inc=0)
        op("vector", "VE", lambda e: e.tensor_copy(proj_b[:], proj_f[:]),
           waits=[("cA", v_c0)], inc=0)
        v_consts = op("vector", "VE", lambda e: e.tensor_copy(bout_b[:], bout_f[:]),
                      waits=[("cA", v_c0)])

        def qpiece_in(i):
            if i < 16:
                kc, hh = i // 2, i % 2
                return w1t_d[128 * kc:128 * kc + 128, 2048 * hh:2048 * hh + 2048]
            j = i - 16
            return bass.AP(w2t_d.tensor, 256 * j * D, [[D, 128], [128 * D, 2], [1, D]])

        def qpiece_out(i, qo_t):
            src = qo_t[:]
            if i < 16:
                kc, hh = i // 2, i % 2
                o = bass.AP(wq1_sc.tensor, (8 * hh) * (128 * 8 * 256) + kc * 256,
                            [[8 * 256, 128], [128 * 8 * 256, 8], [1, 256]])
                return [(o, src.rearrange("p (mp c) -> p mp c", mp=8))]
            j = i - 16
            pairs = []
            for rr in range(2):
                o = bass.AP(wq2_sc.tensor, (2 * j + rr) * 128,
                            [[32 * 128, 128], [128 * 32 * 128, 8], [1, 128]])
                s = bass.AP(qo_t, rr * 1024, [[2048, 128], [128, 8], [1, 128]])
                pairs.append((o, s))
            return pairs

        def emit_invg(col, gcols, psum_t, wait_v):
            vr = op("vector", "VE", lambda e: e.tensor_reduce(
                gc2[:, col:col + 1], gcols, mybir.AxisListType.X, ALU.add),
                waits=[("VE", wait_v)])
            t1 = op("tensor", "TE", lambda e: e.matmul(
                psum_t[0:1, 0:1], ones_cf[:], gc2[:, col:col + 1],
                start=True, stop=True, skip_group_check=True),
                waits=[("VE", vr)])
            v_gr = op("vector", "VE", lambda e: e.tensor_copy(
                grow[0:1, col:col + 1], psum_t[0:1, 0:1]), waits=[("TE", t1)])
            t2 = op("tensor", "TE", lambda e: e.matmul(
                psum_t[0:128, 2:3], ones_rf[:], grow[0:1, col:col + 1],
                start=True, stop=True, skip_group_check=True),
                waits=[("VE", v_gr)])
            vg1 = op("vector", "VE", lambda e: e.tensor_scalar(
                gtmp[:, col:col + 1], psum_t[0:128, 2:3], 1.0 / N_ELEM_W, EPS_Q,
                ALU.mult, ALU.add), waits=[("TE", t2)])
            return op("vector", "VE", lambda e: e.reciprocal(
                invg[:, col:col + 1], gtmp[:, col:col + 1]), waits=[("VE", vg1)])

        with contextlib.ExitStack() as es_vh:
            v_tm = es_vh.enter_context(nc.sbuf_tensor("v_tm", [128, N_XT, D], BF16))
            qbuf = [es_vh.enter_context(nc.sbuf_tensor(f"qb{i}", [128, 2048], F32))
                    for i in range(3)]
            qo = [es_vh.enter_context(nc.sbuf_tensor(f"qo{i}", [128, 2048], BF16))
                  for i in range(2)]

            q_dma = {}
            q_cons = {}
            q_wr = {}
            v_gabs = {}
            nqnt = [0]

            def emit_gamma(seq, i, extra=()):
                b = seq % 3
                xw = [q_cons.get(seq - 3)] + list(extra)
                q_dma[seq] = dma(f"cQB{b}", qbuf[b][:], qpiece_in(i),
                                 waits=[x for x in xw if x], q="gpsimd")
                v_gabs[i] = op("vector", "VE", lambda e, b=b, i=i: e.tensor_reduce(
                    gam[:, i:i + 1], qbuf[b][:], mybir.AxisListType.X, ALU.add,
                    apply_absolute_value=True),
                    waits=[(f"cQB{b}", q_dma[seq])])
                q_cons[seq] = ("VE", v_gabs[i])

            def emit_piece_dma(seq, i, extra=()):
                b = seq % 3
                xw = [q_cons.get(seq - 3)] + list(extra)
                q_dma[seq] = dma(f"cQB{b}", qbuf[b][:], qpiece_in(i),
                                 waits=[x for x in xw if x], q="gpsimd")

            def emit_quant_comp(seq, i, v_invg):
                b = seq % 3
                col = 0 if i < 16 else 1
                bo = nqnt[0] % 2
                qn = nqnt[0]
                nqnt[0] += 1
                p1 = op("scalar", "SE", lambda e, b=b, col=col: e.activation(
                    qbuf[b][:], qbuf[b][:], AF.Identity,
                    scale=invg[:, col:col + 1], bias=magic_c[:]),
                    waits=[(f"cQB{b}", q_dma[seq]), ("VE", v_invg)])
                p2 = op("vector", "VE", lambda e, b=b: e.tensor_scalar(
                    qbuf[b][:], qbuf[b][:], MAGIC, 1.0, ALU.subtract, ALU.min),
                    waits=[("SE", p1)])
                p3 = op("vector", "VE", lambda e, b=b, bo=bo: e.tensor_scalar(
                    qo[bo][:], qbuf[b][:], -1.0, None, ALU.max),
                    waits=[(f"cQW{bo}", q_wr.get(qn - 2)), ("VE", p2)])
                for (o, s) in qpiece_out(i, qo[bo]):
                    q_wr[qn] = dma(f"cQW{bo}", o, s, waits=[("VE", p3)], q="gpsimd")
                q_cons[seq] = ("VE", p3)

            def emit_gamma_comp(seq, i):
                b = seq % 3
                v_gabs[i] = op("vector", "VE", lambda e, b=b, i=i: e.tensor_reduce(
                    gam[:, i:i + 1], qbuf[b][:], mybir.AxisListType.X, ALU.add,
                    apply_absolute_value=True),
                    waits=[(f"cQB{b}", q_dma[seq])])
                q_cons[seq] = ("VE", v_gabs[i])

            tr_done = {}
            v_xn_evac = {}
            v_vevac = {}
            with contextlib.ExitStack() as es_lnv:
                ptr = [es_lnv.enter_context(nc.psum_tensor(f"ptr{i}", [128, D], BF16))
                       for i in range(2)]
                pv = [es_lnv.enter_context(nc.psum_tensor(f"pv{i}", [128, D], F32))
                      for i in range(2)]

                with contextlib.ExitStack() as es_ln:
                    xt = [es_ln.enter_context(nc.sbuf_tensor(f"xt{i}", [128, D], F32))
                          for i in range(4)]
                    junk = es_ln.enter_context(nc.sbuf_tensor("junk", [128, D], F32))
                    ssum = es_ln.enter_context(nc.sbuf_tensor("ssum", [128, N_XT], F32))
                    ssq = es_ln.enter_context(nc.sbuf_tensor("ssq", [128, N_XT], F32))
                    m2c = es_ln.enter_context(nc.sbuf_tensor("m2c", [128, N_XT], F32))
                    rstd = es_ln.enter_context(nc.sbuf_tensor("rstd", [128, N_XT], F32))
                    nmr = es_ln.enter_context(nc.sbuf_tensor("nmr", [128, N_XT], F32))
                    xnt = es_ln.enter_context(nc.sbuf_tensor("xnt", [128, 2, D], BF16))

                    op("vector", "VE", lambda e: e.memset(ssum[:], 0.0), inc=0)
                    v_ms = op("vector", "VE", lambda e: e.memset(ssq[:], 0.0))

                    d_x1 = {}
                    s_sq = {}
                    v_sm = {}
                    for tc in range(N_XT):
                        rows = 128 if tc < 16 else 6
                        b = tc % 4
                        d_x1[tc] = dma(f"cX{b}", xt[b][0:rows, :],
                                       x_tm_d[128 * tc:128 * tc + rows, :],
                                       waits=[("SE", s_sq.get(tc - 4)),
                                              ("VE", v_sm.get(tc - 4))])
                        s_sq[tc] = op("scalar", "SE",
                            lambda e, tc=tc, rows=rows, b=b: e.activation(
                                junk[0:rows, :], xt[b][0:rows, :], AF.Square,
                                accum_out=ssq[0:rows, tc:tc + 1]),
                            waits=[(f"cX{b}", d_x1[tc]),
                                   ("SE", s_sq.get(tc - 1)),
                                   ("VE", v_ms if tc == 0 else None)])
                        v_sm[tc] = op("vector", "VE",
                            lambda e, tc=tc, rows=rows, b=b: e.tensor_reduce(
                                ssum[0:rows, tc:tc + 1], xt[b][0:rows, :],
                                mybir.AxisListType.X, ALU.add),
                            waits=[(f"cX{b}", d_x1[tc]),
                                   ("VE", v_ms if tc == 0 else None)])
                    w1_ = op("vector", "VE", lambda e: e.tensor_scalar(
                        ssum[:], ssum[:], 1.0 / D, None, ALU.mult),
                        waits=[("SE", s_sq[N_XT - 1]), ("VE", v_sm[N_XT - 1])])
                    w2_ = op("vector", "VE", lambda e: e.tensor_tensor(
                        m2c[:], ssum[:], ssum[:], ALU.mult), waits=[("VE", w1_)])
                    w3_ = op("vector", "VE", lambda e: e.scalar_tensor_tensor(
                        ssq[:], ssq[:], 1.0 / D, m2c[:], ALU.mult, ALU.subtract),
                        waits=[("VE", w2_)])
                    w4_ = op("vector", "VE", lambda e: e.tensor_scalar(
                        ssq[:], ssq[:], EPS_LN, None, ALU.add), waits=[("VE", w3_)])
                    v_stats = op("vector", "VE", lambda e: e.reciprocal(
                        m2c[:], ssq[:]), waits=[("VE", w4_)])
                    s_rstd = op("scalar", "SE", lambda e: e.activation(
                        rstd[:], m2c[:], AF.Sqrt), waits=[("VE", v_stats)])
                    v_nmr = op("vector", "VE", lambda e: e.scalar_tensor_tensor(
                        nmr[:], ssum[:], -1.0, rstd[:], ALU.mult, ALU.mult),
                        waits=[("SE", s_rstd), ("VE", v_stats)])

                    d_x2 = {}
                    s_xn = {}
                    for tc in range(N_XT):
                        rows = 128 if tc < 16 else 6
                        b = tc % 4
                        buf = tc % 2
                        d_x2[tc] = dma(f"cX{b}", xt[b][0:rows, :],
                                       x_tm_d[128 * tc:128 * tc + rows, :],
                                       waits=[("SE", s_xn.get(tc - 4)),
                                              ("SE", s_sq[N_XT - 1]),
                                              ("VE", v_sm[N_XT - 1])])
                        s_w = [(f"cX{b}", d_x2[tc]), ("VE", v_nmr),
                               ("SE", s_xn.get(tc - 1))]
                        if tc >= 2:
                            s_w.append(("TE", tr_done[tc - 2]))
                        s_xn[tc] = op("scalar", "SE",
                            lambda e, tc=tc, rows=rows, b=b, buf=buf: e.activation(
                                xnt[0:rows, buf, :], xt[b][0:rows, :], AF.Identity,
                                scale=rstd[0:rows, tc:tc + 1],
                                bias=nmr[0:rows, tc:tc + 1]),
                            waits=s_w)
                        for j in range(DK):
                            w_list = []
                            if j == 0:
                                w_list = [("SE", s_xn[tc])]
                                if tc == 0:
                                    w_list.append(("cA", v_c0))
                                if tc >= 2:
                                    w_list.append(("VE", v_xn_evac[tc - 2]))
                            tr_done[tc] = op("tensor", "TE",
                                lambda e, rows=rows, buf=buf, j=j: e.transpose(
                                    ptr[buf][:, 128 * j:128 * j + rows],
                                    xnt[0:rows, buf, 128 * j:128 * (j + 1)],
                                    ident[0:rows, 0:rows]),
                                waits=w_list)
                        v_xn_evac[tc] = op("vector", "VE",
                            lambda e, tc=tc, rows=rows, buf=buf: e.tensor_copy(
                                xn[:, :, 128 * tc:128 * tc + rows],
                                ptr[buf][:].rearrange("p (j r) -> p j r", j=DK)[:, :, 0:rows]),
                            waits=[("TE", tr_done[tc])])
                        if tc < 16:
                            emit_gamma(tc, tc)
                PH["ln"] = len(steps)

                ln_end = dict(C)
                with contextlib.ExitStack() as es_v:
                    wvf = [es_v.enter_context(nc.sbuf_tensor(f"wvf{i}", [128, D], F32))
                           for i in range(2)]
                    wv_b = es_v.enter_context(nc.sbuf_tensor("wv_b", [128, DK, D], BF16))

                    v_wvc = {}
                    for kc in range(DK):
                        ch = f"cWV{kc % 2}"
                        xw = [("VE", v_wvc.get(kc - 2))]
                        if kc < 2:
                            xw += [("SE", ln_end["SE"]), ("TE", ln_end["TE"]),
                                   ("VE", ln_end["VE"])]
                        d_wv = dma(ch, wvf[kc % 2][:],
                                   wqkv_d[128 * kc:128 * kc + 128, 2048:3072],
                                   waits=xw)
                        v_wvc[kc] = op("vector", "VE", lambda e, kc=kc: e.tensor_copy(
                            wv_b[:, kc, :], wvf[kc % 2][:]), waits=[(ch, d_wv)])

                    for tc in range(N_XT):
                        rows = 128 if tc < 16 else 6
                        buf = tc % 2
                        t_last = None
                        for kc in range(DK):
                            for hh in range(2):
                                wl = []
                                if kc == 0 and hh == 0:
                                    wl = [("VE", v_xn_evac[tc]), ("VE", v_wvc[DK - 1])]
                                    if tc >= 2:
                                        wl.append(("VE", v_vevac[tc - 2]))
                                t_last = op("tensor", "TE",
                                    lambda e, tc=tc, rows=rows, buf=buf, kc=kc, hh=hh:
                                    e.matmul(
                                        pv[buf][0:rows, 512 * hh:512 * hh + 512],
                                        xn[:, kc, 128 * tc:128 * tc + rows],
                                        wv_b[:, kc, 512 * hh:512 * hh + 512],
                                        start=(kc == 0), stop=(kc == DK - 1),
                                        skip_group_check=True),
                                    waits=wl)
                        v_vevac[tc] = op("vector", "VE",
                            lambda e, tc=tc, rows=rows, buf=buf: e.tensor_copy(
                                v_tm[0:rows, tc, :], pv[buf][0:rows, :]),
                            waits=[("TE", t_last)])
            PH["v"] = len(steps)

            v_end = dict(C)
            v_aevac = {}
            v_zevac = {}
            with contextlib.ExitStack() as es_h:
                masks = es_h.enter_context(
                    nc.sbuf_tensor("sb_masks", [128, 17 * MASKW], BF16))
                whf = [es_h.enter_context(nc.sbuf_tensor(f"whf{i}", [128, D], F32))
                       for i in range(2)]
                whb = [es_h.enter_context(nc.sbuf_tensor(f"whb{i}", [128, D], BF16))
                       for i in range(2)]
                wT_sb = es_h.enter_context(nc.sbuf_tensor("wT_sb", [128, D], BF16))
                wqp_sb = [es_h.enter_context(
                    nc.sbuf_tensor(f"wqp{i}", [128, DK * FD], BF16)) for i in range(2)]
                e_scr = [es_h.enter_context(nc.sbuf_tensor(f"e_scr{i}", [64, 512], F32))
                         for i in range(2)]
                r_scr = [es_h.enter_context(nc.sbuf_tensor(f"r_scr{i}", [64, 512], F32))
                         for i in range(2)]
                q_phi = es_h.enter_context(nc.sbuf_tensor("q_phi", [64, P], BF16))
                k_phi = es_h.enter_context(nc.sbuf_tensor("k_phi", [64, P], BF16))
                s_bf = [es_h.enter_context(nc.sbuf_tensor(f"s_bf{i}", [128, MASKW], BF16))
                        for i in range(17)]
                z_all = es_h.enter_context(nc.sbuf_tensor("z_all", [NH, T], F32))
                z2 = es_h.enter_context(nc.sbuf_tensor("z2", [NH, T], F32))
                zinv_b = es_h.enter_context(nc.sbuf_tensor("zinv_b", [NH, T], BF16))
                z_stage = [es_h.enter_context(nc.sbuf_tensor(f"z_st{i}", [1, 512], F32))
                           for i in range(2)]
                pphi = [es_h.enter_context(nc.psum_tensor(f"pphi{i}", [128, 512], F32))
                        for i in range(2)]
                ptr_b = es_h.enter_context(nc.psum_tensor("ptr_b", [128, D], BF16))
                pwq = es_h.enter_context(nc.psum_tensor("pwq", [128, 512], F32))
                pA = [es_h.enter_context(nc.psum_tensor(f"pA{i}", [128, 512], F32))
                      for i in range(2)]
                pZ = [es_h.enter_context(nc.psum_tensor(f"pZ{i}", [128, 512], F32))
                      for i in range(2)]

                d_masks = dma("cM", masks[:], masks_d,
                              waits=[("TE", v_end["TE"]), ("VE", v_end["VE"])])

                v_invg1 = emit_invg(0, gam[:, 0:16], pA[0], v_gabs[15])

                d_zs = {}
                v_wcast = {}
                v_wT = {}
                v_wqp = {}
                v_phislab = {}
                s_exp = {}
                s_relu = {}
                v_sbf = {}
                t_sbf_free = {}
                t_trlast = {}
                hseq = [16]

                for h in range(NH):
                    plist = [(1, 2 * h), (1, 2 * h + 1), (0, 16 + 2 * h), (0, 17 + 2 * h)]
                    gidx = [0]

                    def emit_piece(h=h, plist=plist, gidx=gidx):
                        g = gidx[0]
                        gidx[0] += 1
                        if g < len(plist):
                            kind, i = plist[g]
                            s = hseq[0] + g
                            extra = []
                            if s < 19:
                                extra = [("SE", ln_end["SE"]), ("TE", v_end["TE"]),
                                         ("VE", v_end["VE"])]
                            emit_piece_dma(s, i, extra)
                        gc = g - 2
                        if 0 <= gc < len(plist):
                            kind, i = plist[gc]
                            s = hseq[0] + gc
                            if kind == 1:
                                emit_quant_comp(s, i, v_invg1)
                            else:
                                emit_gamma_comp(s, i)
                        if gc == len(plist) - 1:
                            hseq[0] += len(plist)

                    def emit_wload(hh2):
                        for side in range(2):
                            coff = side * D + 128 * hh2
                            ch = f"cWH{side}"
                            xw = [("VE", v_wcast.get((hh2 - 1, side)))]
                            if hh2 == 0:
                                xw += [("TE", v_end["TE"]), ("VE", v_end["VE"])]
                            d_w = dma(ch,
                                      whf[side][:].rearrange("p (k c) -> p k c", k=8),
                                      bass.AP(wqkv_d.tensor, coff,
                                              [[3 * D, 128], [128 * 3 * D, 8],
                                               [1, 128]]),
                                      waits=xw)
                            v_wcast[(hh2, side)] = op("vector", "VE",
                                lambda e, side=side: e.tensor_copy(
                                    whb[side][:], whf[side][:]),
                                waits=[(ch, d_w),
                                       ("TE", t_trlast.get((hh2 - 1, side)))])

                    if h == 0:
                        emit_wload(0)

                    for side in range(2):
                        t_tr = None
                        for kcb in range(DK):
                            wl = []
                            if kcb == 0:
                                prev_wt = v_wT.get((h, side - 1)) if side == 1 else \
                                    v_wT.get((h - 1, 1))
                                wl = [("VE", v_wcast[(h, side)]), ("VE", prev_wt)]
                                if h == 0 and side == 0:
                                    wl.append(("VE", v_invg1))
                            t_tr = op("tensor", "TE",
                                lambda e, side=side, kcb=kcb: e.transpose(
                                    ptr_b[:, 128 * kcb:128 * (kcb + 1)],
                                    whb[side][:, 128 * kcb:128 * (kcb + 1)],
                                    ident[:]),
                                waits=wl)
                        v_wT[(h, side)] = op("vector", "VE",
                            lambda e: e.tensor_copy(wT_sb[:], ptr_b[:]),
                            waits=[("TE", t_tr)])
                        t_wq = None
                        for kcb in range(DK):
                            wl = []
                            if kcb == 0:
                                prev_wq = v_wqp.get((h, side - 1)) if side == 1 else \
                                    v_wqp.get((h - 1, 1))
                                wl = [("VE", v_wT[(h, side)]), ("VE", prev_wq),
                                      ("VE", v_sbf.get((h - 1, 14)))]
                            t_wq = op("tensor", "TE",
                                lambda e, kcb=kcb: e.matmul(
                                    pwq[:, FD * kcb:FD * (kcb + 1)],
                                    wT_sb[:, 128 * kcb:128 * (kcb + 1)], proj_b[:],
                                    start=True, stop=True, skip_group_check=True),
                                waits=wl)
                        t_trlast[(h, side)] = t_tr
                        v_wqp[(h, side)] = op("vector", "VE",
                            lambda e, side=side: e.tensor_copy(
                                wqp_sb[side][:], pwq[:]),
                            waits=[("TE", t_wq)])
                    if h + 1 < NH:
                        emit_wload(h + 1)

                    for side, phi in enumerate([q_phi, k_phi]):
                        for si, (sb0, sw) in enumerate(SLABS5):
                            buf = si % 2
                            t_last = None
                            for kc in range(DK):
                                wl = []
                                if kc == 0:
                                    if si >= 2:
                                        pk = (h, side, si - 2)
                                    elif side == 1:
                                        pk = (h, 0, 4 - si)
                                    else:
                                        pk = (h - 1, 1, 4 - si)
                                    wl = [("VE", v_wqp[(h, side)]),
                                          ("SE", s_relu.get(pk)),
                                          ("VE", v_phislab.get(pk))]
                                    if side == 0 and si < 2:
                                        wl.append(("VE", v_sbf.get((h - 1, 15 + si))))
                                t_last = op("tensor", "TE",
                                    lambda e, buf=buf, kc=kc, sb0=sb0, sw=sw,
                                           side=side: e.matmul(
                                        pphi[buf][0:64, 0:sw],
                                        wqp_sb[side][:, FD * kc:FD * (kc + 1)],
                                        xn[:, kc, sb0:sb0 + sw],
                                        start=(kc == 0), stop=(kc == DK - 1),
                                        skip_group_check=True),
                                    waits=wl)
                            s_exp[(h, side, si)] = op("scalar", "SE",
                                lambda e, buf=buf, sw=sw: e.activation(
                                    e_scr[buf][:, 0:sw], pphi[buf][0:64, 0:sw], AF.Exp),
                                waits=[("TE", t_last)])
                            s_relu[(h, side, si)] = op("scalar", "SE",
                                lambda e, buf=buf, sw=sw: e.activation(
                                    r_scr[buf][:, 0:sw], pphi[buf][0:64, 0:sw], AF.Relu),
                                waits=[("SE", s_exp[(h, side, si)])])
                            v_phislab[(h, side, si)] = op("vector", "VE",
                                lambda e, buf=buf, sb0=sb0, sw=sw, phi=phi:
                                e.scalar_tensor_tensor(
                                    phi[:, sb0:sb0 + sw], e_scr[buf][:, 0:sw], 1.0,
                                    r_scr[buf][:, 0:sw], ALU.min, ALU.add),
                                waits=[("SE", s_relu[(h, side, si)]),
                                       ("VE", v_phislab.get((h, side, si - 2)))])
                            emit_piece()

                    op("vector", "VE", lambda e: e.tensor_tensor(
                        k_phi[:, 0:3], k_phi[:, 0:3], emask[:, 0:3], ALU.mult),
                        waits=[("VE", v_phislab[(h, 1, 4)])], inc=0)
                    v_kedge = op("vector", "VE", lambda e: e.tensor_tensor(
                        k_phi[:, 2051:2054], k_phi[:, 2051:2054], emask[:, 3:6],
                        ALU.mult))

                    pSp = [pphi[0], pphi[1], pwq]
                    first_a = [True] * 4
                    first_z = [True] * 4
                    watermark = [0] * 4
                    t_band = {}
                    for kc in range(17):
                        j0, w = KCS[kc]
                        tb, te = _bwin(kc)
                        tw = te - tb
                        buf = kc % 3
                        wl = [("VE", v_sbf.get((h, kc - 3)))]
                        if kc == 0:
                            wl = [("VE", v_kedge), ("cM", d_masks),
                                  ("SE", s_relu[(h, 1, 4)])]
                        t_s = op("tensor", "TE",
                            lambda e, buf=buf, j0=j0, w=w, tb=tb, tw=tw: e.matmul(
                                pSp[buf][0:w, 0:tw], k_phi[:, j0:j0 + w],
                                q_phi[:, tb:tb + tw],
                                start=True, stop=True, skip_group_check=True),
                            waits=wl)
                        sbf_prev = t_sbf_free.get((h - 1, kc))
                        v_sbf[(h, kc)] = op("vector", "VE",
                            lambda e, buf=buf, w=w, tw=tw, kc=kc: e.tensor_tensor(
                                s_bf[kc][0:w, 0:tw], pSp[buf][0:w, 0:tw],
                                masks[0:w, MASKW * kc:MASKW * kc + tw], ALU.mult),
                            waits=[("TE", t_s), ("TE", sbf_prev)])
                    for kc in range(17):
                        j0, w = KCS[kc]
                        tb, te = _bwin(kc)
                        tw = te - tb
                        buf = kc
                        for u in range(4):
                            for (k2, a, b_, pa) in SLAB_PARTS[u]:
                                if k2 != kc:
                                    continue
                                wl_a = [("VE", v_sbf[(h, kc)])]
                                if h == 0 and first_a[u]:
                                    wl_a.append(("VE", v_vevac[N_XT - 1]))
                                if first_a[u]:
                                    pkey = (h, u - 2) if u >= 2 else (h - 1, u + 2)
                                    wl_a.append(("VE", v_aevac.get(pkey)))
                                    wl_a.append(("VE", v_zevac.get(pkey)))
                                    if h == 0 and u == 0:
                                        wl_a.append(("VE", v_invg1))
                                wm = watermark[u]
                                pieces = []
                                if pa < wm:
                                    pieces.append((a, a + min(wm - pa, b_ - a), pa, False))
                                if pa + (b_ - a) > wm:
                                    a2 = a + max(0, wm - pa)
                                    pieces.append((a2, b_, pa + (a2 - a), first_a[u]))
                                for (aa, bb, pp, st_flag) in pieces:
                                    op("tensor", "TE",
                                        lambda e, u=u, aa=aa, bb=bb, pp=pp, buf=buf,
                                               w=w, kc=kc, h=h, fa=st_flag: e.matmul(
                                            pA[u % 2][:, pp:pp + (bb - aa)],
                                            v_tm[0:w, kc, 128 * h:128 * h + 128],
                                            s_bf[kc][0:w, aa:bb],
                                            start=fa, stop=True,
                                            skip_group_check=True),
                                        waits=wl_a)
                                    wl_a = []
                                    t_z = op("tensor", "TE",
                                        lambda e, u=u, aa=aa, bb=bb, pp=pp, buf=buf,
                                               w=w, kc=kc, fz=st_flag: e.matmul(
                                            pZ[u % 2][:, pp:pp + (bb - aa)],
                                            ones_mb[0:w, :], s_bf[kc][0:w, aa:bb],
                                            start=fz, stop=True,
                                            skip_group_check=True))
                                watermark[u] = max(wm, pa + (b_ - a))
                                first_a[u] = False
                                first_z[u] = False
                                t_band[u] = t_z
                                t_sbf_free[(h, kc)] = t_z
                        for u in range(4):
                            if SLAB_LAST_KC[u] != kc:
                                continue
                            idx = h * 4 + u
                            v_aevac[(h, u)] = op("vector", "VE",
                                lambda e, u=u, h=h: e.tensor_copy(
                                    attn_r[:, h, 512 * u:512 * u + 512],
                                    pA[u % 2][:, 0:512]),
                                waits=[("TE", t_band[u])])
                            v_zevac[(h, u)] = op("vector", "VE",
                                lambda e, u=u: e.tensor_copy(
                                    z_stage[u % 2][0:1, :], pZ[u % 2][0:1, 0:512]),
                                waits=[(f"cZS{u % 2}", d_zs.get(idx - 2))])
                            d_zs[idx] = dma(f"cZS{u % 2}",
                                            zdram_f[h:h + 1, 512 * u:512 * u + 512],
                                            z_stage[u % 2][0:1, :],
                                            waits=[("VE", v_zevac[(h, u)])])

                d_zall = dma("cZA", z_all[:], zdram_f,
                             waits=[("cZS0", d_zs[NH * 4 - 2]),
                                    ("cZS1", d_zs[NH * 4 - 1])])
                vz1 = op("vector", "VE", lambda e: e.tensor_scalar(
                    z_all[:], z_all[:], EPS_Z, None, ALU.add),
                    waits=[("cZA", d_zall)])
                vz2 = op("vector", "VE", lambda e: e.reciprocal(z2[:], z_all[:]),
                         waits=[("VE", vz1)])
                v_zi = op("vector", "VE", lambda e: e.tensor_copy(zinv_b[:], z2[:]),
                          waits=[("VE", vz2)])
                d_z1 = dma("cZI", zdram, zinv_b[:], waits=[("VE", v_zi)])
                v_gabs2_last = v_gabs[31]
                sq_scr0 = C.get("cQW0", 0)
                sq_scr1 = C.get("cQW1", 0)
                d_z1_val = C["cZI"]
            h_end = dict(C)
            PH["heads"] = len(steps)

        with contextlib.ExitStack() as es_f:
            h_sb = es_f.enter_context(nc.sbuf_tensor("h_sb", [128, MF, 512], BF16))
            g_fm = es_f.enter_context(nc.sbuf_tensor("g_fm", [128, DK, 512], BF16))
            g_tm = es_f.enter_context(nc.sbuf_tensor("g_tm", [128, N_TC, D], BF16))
            wq1b = [es_f.enter_context(nc.sbuf_tensor(f"wq1b{i}", [128, 2048], BF16))
                    for i in range(2)]
            w2tb = [es_f.enter_context(nc.sbuf_tensor(f"w2tb{i}", [128, 2048], BF16))
                    for i in range(2)]
            wout_b = es_f.enter_context(nc.sbuf_tensor("wout_b", [128, DK, D], BF16))
            qb2 = [es_f.enter_context(nc.sbuf_tensor(f"qb2_{i}", [128, 2048], F32))
                   for i in range(2)]
            qo2 = [es_f.enter_context(nc.sbuf_tensor(f"qo2_{i}", [128, 2048], BF16))
                   for i in range(2)]
            zr2 = es_f.enter_context(nc.sbuf_tensor("zr2", [1, T], BF16))
            xtf = qb2

            with contextlib.ExitStack() as es_p1:
                ph = [es_p1.enter_context(nc.psum_tensor(f"ph{i}", [128, 512], F32))
                      for i in range(2)]
                pg = [es_p1.enter_context(nc.psum_tensor(f"pg{i}", [128, 512], F32))
                      for i in range(2)]
                pgt = [es_p1.enter_context(nc.psum_tensor(f"pgt{i}", [128, D], BF16))
                       for i in range(2)]
                psf = [es_p1.enter_context(nc.psum_tensor(f"psf{i}", [128, 512], F32))
                       for i in range(2)]

                hw = [("TE", h_end["TE"]), ("VE", h_end["VE"]), ("SE", h_end["SE"])]

                v_woc = {}

                def emit_wout(dc):
                    ch = f"cWO{dc % 2}"
                    xw = [("VE", v_woc.get(dc - 2))]
                    if dc < 2:
                        xw += hw
                    d_wo = dma(ch, qb2[dc % 2][:, 0:1024],
                               wout_d[128 * dc:128 * dc + 128, :], waits=xw)
                    v_woc[dc] = op("vector", "VE", lambda e, dc=dc: e.tensor_copy(
                        wout_b[:, dc, :], qb2[dc % 2][:, 0:1024]),
                        waits=[(ch, d_wo)])

                v_invg2 = emit_invg(1, gam[:, 16:32], ph[0], v_gabs2_last)

                q2_dma = {}
                q2_cons = {}
                q2_wr = {}
                n2 = [0]
                m2 = [0]

                def emit_quant2_dma():
                    i2 = n2[0]
                    if i2 >= 16:
                        return
                    n2[0] += 1
                    i = 16 + i2
                    b = i2 % 2
                    xw = [q2_cons.get(i2 - 2)]
                    if i2 < 2:
                        xw += [("VE", v_woc[6 + b])]
                        xw += hw
                    q2_dma[i2] = dma(f"cQ2{b}", qb2[b][:], qpiece_in(i),
                                     waits=[x for x in xw if x], q="gpsimd")

                def emit_quant2_comp():
                    i2 = m2[0]
                    if i2 >= 16 or i2 >= n2[0]:
                        return
                    m2[0] += 1
                    i = 16 + i2
                    b = i2 % 2
                    p1 = op("vector", "VE", lambda e, b=b: e.tensor_scalar(
                        qb2[b][:], qb2[b][:], invg[:, 1:2], MAGIC,
                        ALU.mult, ALU.add),
                        waits=[(f"cQ2{b}", q2_dma[i2]), ("VE", v_invg2)])
                    p2 = op("vector", "VE", lambda e, b=b: e.tensor_scalar(
                        qb2[b][:], qb2[b][:], MAGIC, 1.0, ALU.subtract, ALU.min),
                        waits=[("VE", p1)])
                    p3 = op("vector", "VE", lambda e, b=b: e.tensor_scalar(
                        qo2[b][:], qb2[b][:], -1.0, None, ALU.max),
                        waits=[(f"cQ2W{b}", q2_wr.get(i2 - 2)), ("VE", p2)])
                    for (o, s) in qpiece_out(i, qo2[b]):
                        q2_wr[i2] = dma(f"cQ2W{b}", o, s, waits=[("VE", p3)],
                                        q="sync")
                    q2_cons[i2] = ("VE", p3)

                v_div = {}
                d_z2d = {}
                for h2 in range(NH):
                    xw = [("cZI", d_z1_val), ("VE", v_div.get(4 * h2 - 1))]
                    if h2 == 0:
                        xw += hw
                    d_z2d[h2] = dma("cZR", zr2[0:1, :],
                                    bass.AP(zdram.tensor, h2 * T, [[T, 1], [1, T]]),
                                    waits=xw)
                    for u in range(4):
                        idx = h2 * 4 + u
                        buf = idx % 2
                        wl = [("cZR", d_z2d[h2])]
                        if idx >= 2:
                            wl.append(("VE", v_div[idx - 2]))
                        t_r = op("tensor", "TE", lambda e, buf=buf, u=u: e.matmul(
                            psf[buf][:, 0:512], ones_rb[:],
                            zr2[0:1, 512 * u:512 * u + 512],
                            start=True, stop=True, skip_group_check=True),
                            waits=wl)
                        v_div[idx] = op("vector", "VE",
                            lambda e, buf=buf, h2=h2, u=u: e.tensor_tensor(
                                attn_r[:, h2, 512 * u:512 * u + 512],
                                attn_r[:, h2, 512 * u:512 * u + 512],
                                psf[buf][:, 0:512], ALU.mult),
                            waits=[("TE", t_r)])
                v_div_last = v_div[NH * 4 - 1]

                h_ev = {}
                h_eng = {}
                t_wq1 = {}
                t_w2_half = {}
                t_ffn2_last = {}
                s_gev = {}
                v_gt = {}
                v_stt = {}
                v_ob = {}
                d_out = {}

                def emit_final(tc):
                    buf = tc % 2
                    xw = [("VE", v_ob.get(tc - 2))]
                    if tc < 2:
                        c2 = q2_cons.get(14 + buf)
                        if c2:
                            xw.append(c2)
                        xw.append((f"cQ2{buf}", q2_dma.get(14 + buf)))
                    d_x = dma(f"cXF{buf}", xtf[buf][:, 0:1024],
                              x_tm_d[HALF + 128 * tc:HALF + 128 * tc + 128, :],
                              waits=xw)
                    for hh in range(2):
                        t_last = None
                        for dc in range(DK):
                            wl = []
                            if dc == 0:
                                wl = [("VE", v_gt[tc]), ("VE", v_consts)]
                                if tc == 0 and hh == 0:
                                    wl += [("VE", v_div_last), ("VE", v_wo_last)]
                                pv_ = v_stt.get((tc - 1, hh))
                                if tc == 0:
                                    pv_ = v_div_last
                                wl.append(("VE", pv_))
                            t_last = op("tensor", "TE",
                                lambda e, hh=hh, dc=dc, tc=tc: e.matmul(
                                    psf[hh][:, 0:512],
                                    attn_r[:, dc, 128 * tc:128 * (tc + 1)],
                                    wout_b[:, dc, 512 * hh:512 * hh + 512],
                                    start=(dc == 0), stop=False,
                                    skip_group_check=True),
                                waits=wl)
                        t_last = op("tensor", "TE",
                            lambda e, hh=hh: e.matmul(
                                psf[hh][:, 0:512], ones_rb[:],
                                bout_b[0:1, 512 * hh:512 * hh + 512],
                                start=False, stop=True, skip_group_check=True))
                        v_stt[(tc, hh)] = op("vector", "VE",
                            lambda e, buf=buf, tc=tc, hh=hh: e.scalar_tensor_tensor(
                                qb2_out[buf][:, 512 * hh:512 * hh + 512],
                                psf[hh][:, 0:512], 0.0,
                                g_tm[:, tc, 512 * hh:512 * hh + 512],
                                ALU.add, ALU.mult),
                            waits=[("TE", t_last), (f"cXF{buf}", d_x),
                                   (f"cOB{buf}", d_out.get(tc - 2))])
                    v_ob[tc] = op("vector", "VE", lambda e, buf=buf: e.tensor_tensor(
                        qb2_out[buf][:], qb2_out[buf][:], xtf[buf][:, 0:1024],
                        ALU.add),
                        waits=[("VE", v_stt[(tc, 0)]), ("VE", v_stt[(tc, 1)])])
                    d_out[tc] = dma(f"cOB{buf}", out_d[128 * tc:128 * (tc + 1), :],
                                    qb2_out[buf][:], waits=[("VE", v_ob[tc])])

                qb2_out = [bass.AP(qb2[0], 1024, [[2048, 128], [1, 1024]]),
                           bass.AP(qb2[1], 1024, [[2048, 128], [1, 1024]])]

                for u in range(4):
                    xs = HALF + 512 * u
                    for mp in range(16):
                        wv = []
                        if (u, mp - 2) in t_wq1:
                            wv.append(("TE", t_wq1[(u, mp - 2)]))
                        elif u > 0 and mp < 2:
                            wv.append(("TE", t_wq1[(u - 1, 14 + mp)]))
                        if u == 0 and mp < 2:
                            wv.append(("cQW0", sq_scr0))
                            wv.append(("cQW1", sq_scr1))
                            wv += hw
                        d_wq1 = dma(
                            f"cW1{mp % 2}", wq1b[mp % 2][:],
                            bass.AP(wq1_sc.tensor, mp * (128 * 8 * 256),
                                    [[8 * 256, 128], [1, 8 * 256]]),
                            waits=wv, q="sync")
                        for mo in range(2):
                            m = 2 * mp + mo
                            t_last = None
                            for kc in range(DK):
                                wl = []
                                if kc == 0:
                                    wl = [(f"cW1{mp % 2}", d_wq1)]
                                    if m >= 2:
                                        wl.append((h_eng[(u, m - 2)], h_ev[(u, m - 2)]))
                                    elif u > 0:
                                        wl.append((h_eng[(u - 1, 30 + m)],
                                                   h_ev[(u - 1, 30 + m)]))
                                t_last = op("tensor", "TE",
                                    lambda e, m=m, kc=kc, xs=xs, mp=mp: e.matmul(
                                        ph[m % 2][:, 0:512],
                                        wq1b[mp % 2][:, 256 * kc + 128 * (m - 2 * mp):
                                                      256 * kc + 128 * (m - 2 * mp) + 128],
                                        xn[:, kc, xs:xs + 512],
                                        start=(kc == 0), stop=(kc == DK - 1),
                                        skip_group_check=True),
                                    waits=wl)
                            t_wq1[(u, mp)] = t_last
                            if m % 2 == 0 and u > 0:
                                h_eng[(u, m)] = "VE"
                                h_ev[(u, m)] = op("vector", "VE",
                                    lambda e, m=m: e.tensor_scalar(
                                        h_sb[:, m, :], ph[m % 2][:, 0:512],
                                        b1t[:, m:m + 1], 0.0, ALU.add, ALU.max),
                                    waits=[("TE", t_last)])
                            else:
                                h_eng[(u, m)] = "SE"
                                h_ev[(u, m)] = op("scalar", "SE",
                                    lambda e, m=m: e.activation(
                                        h_sb[:, m, :], ph[m % 2][:, 0:512], AF.Relu,
                                        bias=b1t[:, m:m + 1]),
                                    waits=[("TE", t_last),
                                           ("SE", h_ev.get((m if False else u,
                                                            m - 1)) if False else None)])
                        if u == 0:
                            if mp < DK:
                                emit_wout(mp)
                                if mp == DK - 1:
                                    v_wo_last = v_woc[DK - 1]
                            else:
                                emit_quant2_comp()
                                emit_quant2_dma()
                    if u == 0:
                        for _fl in range(12):
                            emit_quant2_comp()
                            emit_quant2_dma()
                    for dc in range(DK):
                        t_last = None
                        for half in range(2):
                            wv = [("TE", t_w2_half.get(half))]
                            if u == 0 and dc == 0:
                                wv.append((f"cQ2W0", q2_wr.get(14)))
                                wv.append((f"cQ2W1", q2_wr.get(15)))
                                wv += hw
                            d_w2t = dma(
                                f"cW2{half}", w2tb[half][:],
                                bass.AP(wq2_sc.tensor,
                                        dc * (128 * 32 * 128) + half * 2048,
                                        [[32 * 128, 128], [1, 2048]]),
                                waits=wv, q="sync")
                            for kk in range(16):
                                kc2 = 16 * half + kk
                                wl = []
                                if kk == 0:
                                    wl = [(f"cW2{half}", d_w2t)]
                                    if half == 0:
                                        wl += [(h_eng[(u, 30)], h_ev[(u, 30)]),
                                               (h_eng[(u, 31)], h_ev[(u, 31)]),
                                               ("SE", s_gev.get((u, dc - 2)))]
                                t_last = op("tensor", "TE",
                                    lambda e, dc=dc, kc2=kc2, half=half, kk=kk: e.matmul(
                                        pg[dc % 2][:, 0:512],
                                        w2tb[half][:, 128 * kk:128 * kk + 128],
                                        h_sb[:, kc2, :],
                                        start=(kc2 == 0), stop=(kc2 == MF - 1),
                                        skip_group_check=True),
                                    waits=wl)
                            t_w2_half[half] = t_last
                        t_ffn2_last[(u, dc)] = t_last
                        s_gev[(u, dc)] = op("scalar", "SE",
                            lambda e, dc=dc: e.activation(
                                g_fm[:, dc, :], pg[dc % 2][:, 0:512], AF.Silu,
                                bias=b2t[:, dc:dc + 1]),
                            waits=[("TE", t_last)])
                    for tt in range(4):
                        tc = 4 * u + tt
                        buf = tt % 2
                        t_tr = None
                        for dc2 in range(DK):
                            wl = []
                            if dc2 == 0:
                                wl = [("SE", s_gev[(u, DK - 1)])]
                                if tc >= 2:
                                    wl.append(("VE", v_gt.get(tc - 2)))
                            t_tr = op("tensor", "TE",
                                lambda e, buf=buf, dc2=dc2, tt=tt: e.transpose(
                                    pgt[buf][:, 128 * dc2:128 * (dc2 + 1)],
                                    g_fm[0:128, dc2, 128 * tt:128 * (tt + 1)],
                                    ident[:]),
                                waits=wl)
                        v_gt[tc] = op("vector", "VE",
                            lambda e, buf=buf, tc=tc: e.tensor_copy(
                                g_tm[:, tc, :], pgt[buf][:]),
                            waits=[("TE", t_tr)])
                        emit_final(tc)

                fin0 = C.get("cOB0", 0)
                fin1 = C.get("cOB1", 0)

                def f_fin(e, fin0=fin0, fin1=fin1):
                    e.wait_ge(SEM["cOB0"], fin0)
                    e.wait_ge(SEM["cOB1"], fin1)
                steps.append(("sync", f_fin, 0))

        PH["all"] = len(steps)
        nc._phase_marks = PH
        nc._steps_meta = [(s[0], s[2]) for s in steps]
        block = es.enter_context(nc.Block())
        use_steps = steps if cut is None else steps[:cut]

        def mk(section_name):
            def _(eng):
                for st in use_steps:
                    if st[0] == section_name:
                        st[1](eng)
            return _

        block.sync(mk("sync"))
        block.tensor(mk("tensor"))
        block.vector(mk("vector"))
        block.scalar(mk("scalar"))
        block.gpsimd(mk("gpsimd"))

    return nc


_NC_CACHE = None


def _get_nc():
    global _NC_CACHE
    if _NC_CACHE is None:
        _NC_CACHE = build_nc()
    return _NC_CACHE


def _host_masks():
    m = np.zeros((128, 17 * MASKW), dtype=np.float32)
    for kc in range(17):
        j0, w = KCS[kc]
        tb, te = _bwin(kc)
        jj = np.arange(w)[:, None]
        tt = np.arange(te - tb)[None, :]
        m[0:w, MASKW * kc:MASKW * kc + (te - tb)] = (
            np.abs((tb + tt) - (j0 + jj)) <= HALF).astype(np.float32)
    return m.astype(ml_dtypes.bfloat16)


def kernel(L0_attn_x, L0_attn_w_qkv, L0_attn_proj, L0_attn_w_out, L0_attn_b_out,
           L0_ffn_W1, L0_ffn_b1, L0_ffn_W2, L0_ffn_b2):
    x = np.asarray(L0_attn_x, dtype=np.float32)
    wqkv = np.ascontiguousarray(np.asarray(L0_attn_w_qkv, dtype=np.float32))
    proj = np.ascontiguousarray(np.asarray(L0_attn_proj, dtype=np.float32))
    wout = np.ascontiguousarray(np.asarray(L0_attn_w_out, dtype=np.float32))
    bout = np.ascontiguousarray(np.asarray(L0_attn_b_out, dtype=np.float32).reshape(1, D))
    w1 = np.asarray(L0_ffn_W1, dtype=np.float32)
    b1 = np.asarray(L0_ffn_b1, dtype=np.float32)
    w2 = np.asarray(L0_ffn_W2, dtype=np.float32)
    b2 = np.asarray(L0_ffn_b2, dtype=np.float32)

    w1t = np.ascontiguousarray(w1.T)
    w2t = np.ascontiguousarray(w2.T)
    b1t = np.ascontiguousarray(b1.reshape(MF, 128).T)
    b2t = np.ascontiguousarray(b2.reshape(DK, 128).T)
    ident = np.eye(128, dtype=ml_dtypes.bfloat16)
    masks = _host_masks()

    in_maps = []
    for c in range(8):
        seq, t0 = c // 4, (c % 4) * T
        sl = np.zeros((P, D), dtype=np.float32)
        lo, hi = max(0, t0 - HALF), min(8192, t0 + T + HALF)
        sl[lo - (t0 - HALF):lo - (t0 - HALF) + (hi - lo)] = x[seq, lo:hi]
        em = np.ones((64, 6), dtype=np.float32)
        if t0 == 0:
            em[:, 0:3] = 0.0
        if t0 + T == 8192:
            em[:, 3:6] = 0.0
        in_maps.append({
            "x_tm": sl, "wqkv": wqkv, "proj": proj, "wout": wout, "bout": bout,
            "w1t": w1t, "b1t": b1t, "w2t": w2t, "b2t": b2t,
            "ident": ident, "masks": masks, "emask": em,
        })

    nc = _get_nc()
    res = run_bass_kernel_spmd(nc, in_maps, core_ids=list(range(8)))
    out = np.empty((2, 8192, D), dtype=np.float32)
    for c in range(8):
        seq, t0 = c // 4, (c % 4) * T
        out[seq, t0:t0 + T] = res.results[c]["out"]
    return out
